# version 18
# baseline (speedup 1.0000x reference)
import math
import numpy as np
from contextlib import ExitStack

import concourse.bass as bass
import concourse.mybir as mybir
from concourse.bass_utils import run_bass_kernel_spmd

F32 = mybir.dt.float32
BF16 = mybir.dt.bfloat16
AF = mybir.ActivationFunctionType
ALU = mybir.AluOpType
AXX = mybir.AxisListType.X

D = 1024
DC = 8
HFF = 2048
HC = 16
NH = 16
HD = 64
ASCALE = HD ** (-0.5)
EPS = 1e-5
TWO_PI = 2.0 * math.pi


class P:
    ENG = ("sync", "gpsimd", "tensor", "scalar", "vector")

    def __init__(self, nc, ctx):
        self.nc = nc
        self.ctx = ctx
        self.streams = {e: [] for e in self.ENG}
        self.sems = {}
        self.counts = {}
        self.waited = {}
        self.res = {}
        self.last_prod = {e: None for e in self.ENG}
        self.dma_sems = []
        for e in self.ENG:
            self.addsem("s_" + e)

    def addsem(self, name, dma=False):
        if name in self.sems:
            return name
        self.sems[name] = self.ctx.enter_context(self.nc.semaphore(name))
        self.counts[name] = 0
        if dma:
            self.dma_sems.append(name)
        return name

    def emit(self, eng, fn=None, needs=(), produces=(), sem=None, dma=False):
        waits = []
        for r in needs:
            if r is None:
                continue
            s, v = self.res[r]
            if v <= 0:
                continue
            key = (eng, s)
            if self.waited.get(key, 0) < v:
                waits.append((s, v))
                self.waited[key] = v
        incs = []
        if produces:
            assert fn is not None
            s = sem if sem is not None else ("s_" + eng)
            amt = 16 if dma else 1
            self.counts[s] += amt
            v = self.counts[s]
            incs = [(s, amt)]
            for r in produces:
                self.res[r] = (s, v)
            if not dma:
                self.last_prod[eng] = produces[0]

        def closure(engine, waits=waits, incs=incs, fn=fn):
            for s, v in waits:
                engine.wait_ge(self.sems[s], v)
            if fn is not None:
                ins = fn(engine)
                for s, amt in incs:
                    ins.then_inc(self.sems[s], amt)
        self.streams[eng].append(closure)

    def mark(self, name, sem="s_sync", val=0):
        self.res[name] = (sem, val)

    def wait_raw(self, eng, sem, val):
        key = (eng, sem)
        if val > 0 and self.waited.get(key, 0) < val:
            sh = self.sems[sem]
            self.streams[eng].append(
                lambda engine, sh=sh, v=val: engine.wait_ge(sh, v))
            self.waited[key] = val

    def barrier(self):
        needs = [r for r in self.last_prod.values() if r is not None]
        for e in self.ENG:
            self.emit(e, fn=None, needs=needs)
            for s in self.dma_sems:
                self.wait_raw(e, s, self.counts[s])


class Bank:
    def __init__(self, slots):
        self.slots = list(slots)
        self.free = [None] * len(self.slots)
        self.i = 0

    def acquire(self):
        i = self.i % len(self.slots)
        self.i += 1
        g = self.free[i]
        return i, self.slots[i], ([g] if g else [])

    def consumed(self, i, res):
        self.free[i] = res


def emit_ln(p, C, tag, W, x_tile, x_cols, xln_tile, xln_cols, scr, banks,
            xq_ready, add_half, xln_guard=None):
    xbf, xsq, row0, row1, hil, tvec, uvec = scr
    pm_bank, pr_bank = banks
    im, psm, gm = pm_bank.acquire()
    ir, psr, gr = pr_bank.acquire()
    kb = f"lnscr:{id(xbf[0])}"
    xbf_free = [p.res.get(kb + ":x0") and (kb + ":x0"),
                p.res.get(kb + ":x1") and (kb + ":x1")]
    row_guard = p.res.get(kb + ":row") and (kb + ":row")
    hil_guard = p.res.get(kb + ":hil") and (kb + ":hil")
    for dc in range(DC):
        sl = dc % 2
        p.emit("vector", lambda e, dc=dc, sl=sl, xbf=xbf, W=W, x_tile=x_tile,
               xc=x_cols(dc): e.tensor_copy(xbf[sl][:, 0:W], x_tile[:, xc]),
               needs=xq_ready + ([xbf_free[sl]] if xbf_free[sl] else []),
               produces=[f"{tag}:c:{dc}"])
        p.emit("vector", lambda e, sl=sl, xbf=xbf, xsq=xsq, W=W: e.tensor_tensor(
            out=xsq[sl][:, 0:W], in0=xbf[sl][:, 0:W], in1=xbf[sl][:, 0:W],
            op=ALU.mult),
            needs=[f"{tag}:c:{dc}"], produces=[f"{tag}:sq:{dc}"])
        p.emit("tensor", lambda e, dc=dc, sl=sl, psm=psm, xbf=xbf, W=W, C=C: e.matmul(
            psm[0:1, 0:W], C["ones_b"][:, 0:1], xbf[sl][:, 0:W],
            start=(dc == 0), stop=(dc == DC - 1)),
            needs=[f"{tag}:c:{dc}", "onesb"] + (gm if dc == 0 else []),
            produces=[f"{tag}:mx:{dc}"])
        p.emit("tensor", lambda e, dc=dc, sl=sl, psm=psm, xsq=xsq, W=W, C=C: e.matmul(
            psm[32:33, 0:W], C["ones_b"][:, 0:1], xsq[sl][:, 0:W],
            start=(dc == 0), stop=(dc == DC - 1), tile_position=(0, 32)),
            needs=[f"{tag}:sq:{dc}"],
            produces=[f"{tag}:ms:{dc}"])
        xbf_free[sl] = f"{tag}:ms:{dc}"
    sums = [f"{tag}:mx:{DC-1}", f"{tag}:ms:{DC-1}"]
    p.emit("scalar", lambda e, psm=psm, W=W, row0=row0: e.activation(
        out=row0[:, 0:W], in_=psm[0:1, 0:W], func=AF.Copy, bias=0.0,
        scale=1.0 / D), needs=sums + [row_guard], produces=[f"{tag}:m"])
    p.emit("scalar", lambda e, W=W, row0=row0, row1=row1: e.activation(
        out=row1[:, 0:W], in_=row0[:, 0:W], func=AF.Square, bias=0.0, scale=1.0),
        needs=[f"{tag}:m"], produces=[f"{tag}:msq"])
    p.emit("vector", lambda e, psm=psm, W=W, row1=row1: e.scalar_tensor_tensor(
        out=row1[:, 0:W], in0=psm[32:33, 0:W], scalar=1.0 / D, in1=row1[:, 0:W],
        op0=ALU.mult, op1=ALU.subtract),
        needs=sums + [f"{tag}:msq"], produces=[f"{tag}:var"])
    p.emit("vector", lambda e, W=W, row1=row1: e.tensor_scalar(
        out=row1[:, 0:W], in0=row1[:, 0:W], scalar1=EPS, scalar2=None,
        op0=ALU.add), needs=[f"{tag}:var"], produces=[f"{tag}:vare"])
    p.emit("scalar", lambda e, W=W, row1=row1: e.activation(
        out=row1[:, 0:W], in_=row1[:, 0:W], func=AF.Ln, bias=0.0, scale=1.0),
        needs=[f"{tag}:vare"], produces=[f"{tag}:lnv"])
    p.emit("scalar", lambda e, W=W, row1=row1: e.activation(
        out=row1[:, 0:W], in_=row1[:, 0:W], func=AF.Exp, bias=0.0, scale=-0.5),
        needs=[f"{tag}:lnv"], produces=[f"{tag}:r"])
    p.emit("vector", lambda e, W=W, hil=hil, row0=row0: e.tensor_copy(
        hil[0:1, 0:W], row0[:, 0:W]), needs=[f"{tag}:m", hil_guard],
        produces=[f"{tag}:mh"])
    p.emit("vector", lambda e, W=W, hil=hil, row0=row0: e.tensor_tensor(
        out=hil[0:1, W:2 * W], in0=row0[:, 0:W], in1=hil[0:1, 0:W],
        op=ALU.subtract), needs=[f"{tag}:mh"], produces=[f"{tag}:ml"])
    p.emit("vector", lambda e, W=W, hil=hil, row1=row1: e.tensor_copy(
        hil[0:1, 2 * W:3 * W], row1[:, 0:W]), needs=[f"{tag}:r"],
        produces=[f"{tag}:rh"])
    p.emit("vector", lambda e, W=W, hil=hil, row1=row1: e.tensor_tensor(
        out=hil[0:1, 3 * W:4 * W], in0=row1[:, 0:W], in1=hil[0:1, 2 * W:3 * W],
        op=ALU.subtract), needs=[f"{tag}:rh"], produces=[f"{tag}:rl"])
    p.emit("tensor", lambda e, psm=psm, W=W, hil=hil, C=C: e.matmul(
        psm[:, 0:W], C["ones_b"][0:1, :], hil[0:1, 0:W], start=True, stop=False),
        needs=[f"{tag}:ml", f"{tag}:var", "onesb"], produces=())
    p.emit("tensor", lambda e, psm=psm, W=W, hil=hil, C=C: e.matmul(
        psm[:, 0:W], C["ones_b"][0:1, :], hil[0:1, W:2 * W], start=False,
        stop=True), needs=[], produces=[f"{tag}:mbc"])
    p.emit("tensor", lambda e, psr=psr, W=W, hil=hil, C=C: e.matmul(
        psr[:, 0:W], C["ones_b"][0:1, :], hil[0:1, 2 * W:3 * W], start=True,
        stop=False), needs=[f"{tag}:rl", "onesb"] + gr, produces=())
    p.emit("tensor", lambda e, psr=psr, W=W, hil=hil, C=C: e.matmul(
        psr[:, 0:W], C["ones_b"][0:1, :], hil[0:1, 3 * W:4 * W], start=False,
        stop=True), needs=[], produces=[f"{tag}:rbc"])
    done = []
    for dc in range(DC):
        p.emit("vector", lambda e, psm=psm, W=W, x_tile=x_tile, tvec=tvec,
               xc=x_cols(dc): e.tensor_tensor(
                   out=tvec[:, 0:W], in0=x_tile[:, xc], in1=psm[:, 0:W],
                   op=ALU.subtract),
               needs=[f"{tag}:mbc"] + xq_ready, produces=[f"{tag}:tv:{dc}"])
        p.emit("vector", lambda e, psr=psr, W=W, tvec=tvec, uvec=uvec: e.tensor_tensor(
            out=uvec[:, 0:W], in0=tvec[:, 0:W], in1=psr[:, 0:W], op=ALU.mult),
            needs=[f"{tag}:tv:{dc}", f"{tag}:rbc"], produces=[f"{tag}:uv:{dc}"])
        p.emit("vector", lambda e, W=W, uvec=uvec, xln_tile=xln_tile,
               lc=xln_cols(dc): e.tensor_copy(xln_tile[:, lc], uvec[:, 0:W]),
               needs=[f"{tag}:uv:{dc}"] + ([xln_guard] if xln_guard else []),
               produces=[f"{tag}:xln:{dc}"])
        done.append(f"{tag}:xln:{dc}")
        if add_half:
            p.emit("vector", lambda e, W=W, uvec=uvec, x_tile=x_tile,
                   xc=x_cols(dc): e.scalar_tensor_tensor(
                       out=x_tile[:, xc], in0=uvec[:, 0:W], scalar=0.5,
                       in1=x_tile[:, xc], op0=ALU.mult, op1=ALU.add),
                   needs=[f"{tag}:uv:{dc}"], produces=[f"{tag}:xup:{dc}"])
            done.append(f"{tag}:xup:{dc}")
    pm_bank.consumed(im, f"{tag}:tv:{DC-1}")
    pr_bank.consumed(ir, f"{tag}:uv:{DC-1}")
    p.res[kb + ":x0"] = p.res[f"{tag}:ms:{DC-2}"]
    p.res[kb + ":x1"] = p.res[f"{tag}:ms:{DC-1}"]
    p.res[kb + ":row"] = p.res[f"{tag}:rl"]
    p.res[kb + ":hil"] = p.res[f"{tag}:rbc"]
    return done


def build_program(T=4096, E=16, emit_phases=("ffn1", "attn", "ffn2")):
    NQ = 4
    QT = T // NQ
    TC = QT // 2
    assert TC <= 512

    nc = bass.Bass("TRN2", target_bir_lowering=False)
    x_in = nc.dram_tensor("x", [T, D], F32, kind="ExternalInput").ap()
    w_qkv = nc.dram_tensor("w_qkv", [3 * D, D], F32, kind="ExternalInput").ap()
    w_out = nc.dram_tensor("w_out", [D, D], F32, kind="ExternalInput").ap()
    b_out = nc.dram_tensor("b_out", [D], F32, kind="ExternalInput").ap()
    wr = nc.dram_tensor("Wr", [D, E], F32, kind="ExternalInput").ap()
    br = nc.dram_tensor("br", [E], F32, kind="ExternalInput").ap()
    w1 = nc.dram_tensor("W1", [E, D, HFF], F32, kind="ExternalInput").ap()
    b1 = nc.dram_tensor("b1", [E, HFF], F32, kind="ExternalInput").ap()
    w2 = nc.dram_tensor("W2", [E, HFF, D], F32, kind="ExternalInput").ap()
    b2 = nc.dram_tensor("b2", [E, D], F32, kind="ExternalInput").ap()
    out = nc.dram_tensor("out", [T, D], F32, kind="ExternalOutput").ap()

    x_scr = nc.dram_tensor("x_scr", [DC, 128, T], F32, kind="Internal").ap()
    q_scr = nc.dram_tensor("q_scr", [DC, 128, T], BF16, kind="Internal").ap()
    v_scr = nc.dram_tensor("v_scr", [T // 128, 128, NH * 65], BF16,
                           kind="Internal").ap()
    cs_scr = nc.dram_tensor("cs_scr", [2, 128, T], F32, kind="Internal").ap()
    w_scr = nc.dram_tensor("w_scr", [E, 2, 128, DC * HFF], BF16,
                           kind="Internal").ap()

    with ExitStack() as ctx:
        p = P(nc, ctx)
        sb = lambda n, s, d=F32: ctx.enter_context(nc.sbuf_tensor(n, s, d))

        ident_f = sb("ident_f", [128, 128], F32)
        ident_b = sb("ident_b", [128, 128], BF16)
        ones_b = sb("ones_b", [128, 128], BF16)
        halfsel = sb("halfsel", [128, 2], BF16)
        m2swap = sb("m2swap", [128, 128], BF16)
        wr_sb = sb("wr_sb", [128, DC * E], BF16)
        br_sb = sb("br_sb", [1, 16], BF16)
        b2_sb = sb("b2_sb", [16, D], BF16)
        b1t = sb("b1t", [128, HC * E], F32)
        boutt = sb("boutt", [128, DC], F32)
        ltri_b = sb("ltri_b", [128, 128], BF16)
        iof0 = sb("iof0", [128, 128], BF16)
        iof1 = sb("iof1", [128, 128], BF16)
        C = dict(ident_f=ident_f, ident_b=ident_b, ones_b=ones_b,
                 halfsel=halfsel, m2swap=m2swap, wr_sb=wr_sb,
                 br_sb=br_sb, b2_sb=b2_sb, b1t=b1t, boutt=boutt,
                 ltri_b=ltri_b, iof0=iof0, iof1=iof1)

        setup(nc, p, ctx, C, T, E, wr, br, b1, b2, b_out, cs_scr)
        p.barrier()

        last_ffn1 = ("attn" not in emit_phases and "ffn2" not in emit_phases)
        if "ffn1" in emit_phases:
            ffn_phase(nc, p, ctx, C, dict(
                T=T, E=E, NQ=NQ, QT=QT, TC=TC, layer=0,
                x_nat_in=x_in, x_fm_in=None,
                x_fm_out=None if last_ffn1 else x_scr,
                x_nat_out=out if last_ffn1 else None, w1=w1, w2=w2,
                w_scr=w_scr, w_cast=True))
        else:
            ingest_only(nc, p, ctx, C, T, x_in, x_scr)
        p.barrier()

        if "attn" in emit_phases:
            attn_phase(nc, p, ctx, C, dict(
                T=T, x_scr=x_scr, q_scr=q_scr, v_scr=v_scr, cs_scr=cs_scr,
                w_qkv=w_qkv, w_out=w_out,
                x_nat_out=None if "ffn2" in emit_phases else out))
            p.barrier()

        if "ffn2" in emit_phases:
            ffn_phase(nc, p, ctx, C, dict(
                T=T, E=E, NQ=NQ, QT=QT, TC=TC, layer=2,
                x_nat_in=None, x_fm_in=x_scr,
                x_fm_out=None, x_nat_out=out, w1=w1, w2=w2,
                w_scr=w_scr, w_cast=("ffn1" not in emit_phases)))
            p.barrier()
        elif "attn" not in emit_phases and "ffn1" not in emit_phases:
            raise ValueError("no phases")

        if "attn" in emit_phases and "ffn2" not in emit_phases:
            spill_out(nc, p, ctx, C, T, x_scr, out)
            p.barrier()

        for s in p.dma_sems:
            p.wait_raw("sync", s, p.counts[s])

        block = ctx.enter_context(nc.Block())
        for eng in P.ENG:
            def runner(e, eng=eng):
                for f in p.streams[eng]:
                    f(e)
            getattr(block, eng)(runner)
    return nc


def setup(nc, p, ctx, C, T, E, wr, br, b1, b2, b_out, cs_scr):
    g = "gpsimd"
    p.addsem("d_wr", dma=True)
    p.addsem("d_br", dma=True)
    p.addsem("d_b2", dma=True)
    p.addsem("d_b1", dma=True)
    p.addsem("d_bo", dma=True)
    p.addsem("d_csp", dma=True)
    with ExitStack() as st:
        ssb = lambda n, s, d=F32: st.enter_context(nc.sbuf_tensor(n, s, d))
        iota_c = ssb("iota_c", [128, 128])
        iota_r = ssb("iota_r", [128, 128])
        t1 = ssb("t1c", [128, 1])
        t2 = ssb("t2c", [128, 1])
        t3 = ssb("t3c", [128, 1])
        b1_stage = ssb("b1_stage", [16, HFF])
        bo_stage = ssb("bo_stage", [DC, 128])
        tab_s = ssb("cs_tab_s", [32, T])
        tab_c = ssb("cs_tab_c", [32, T])
        tb2 = ssb("cs_tb2", [32, T])
        invf = ssb("cs_invf", [32, 1])
        frq = ssb("cs_frq", [32, T])
        kint = st.enter_context(nc.sbuf_tensor("cs_kint", [32, T], mybir.dt.int32))
        ps_su = st.enter_context(nc.psum_tensor("ps_su", [128, 512], F32))

        p.emit(g, lambda e, iota_c=iota_c: e.iota(
            iota_c[:], [[1, 128]], channel_multiplier=0,
            allow_small_or_imprecise_dtypes=True), produces=["su:ic"])
        p.emit(g, lambda e, iota_r=iota_r: e.iota(
            iota_r[:], [[0, 128]], channel_multiplier=1,
            allow_small_or_imprecise_dtypes=True), produces=["su:ir"])
        p.emit("vector", lambda e, C=C, iota_c=iota_c, iota_r=iota_r: e.tensor_tensor(
            out=C["ident_f"][:], in0=iota_c[:], in1=iota_r[:], op=ALU.is_equal),
            needs=["su:ic", "su:ir"], produces=["identf"])
        p.emit("vector", lambda e, C=C: e.tensor_copy(C["ident_b"][:],
                                                      C["ident_f"][:]),
               needs=["identf"], produces=["identb"])
        p.emit("vector", lambda e, C=C: e.memset(C["ones_b"][:], 1.0),
               produces=["onesb"])
        p.emit("vector", lambda e, C=C, iota_r=iota_r: e.tensor_scalar(
            out=C["halfsel"][:, 0:1], in0=iota_r[:, 0:1], scalar1=64.0,
            scalar2=None, op0=ALU.is_lt), needs=["su:ir"], produces=["hs0"])
        p.emit("vector", lambda e, C=C, iota_r=iota_r: e.tensor_scalar(
            out=C["halfsel"][:, 1:2], in0=iota_r[:, 0:1], scalar1=64.0,
            scalar2=None, op0=ALU.is_ge), needs=["su:ir", "hs0"],
            produces=["halfsel"])
        p.emit("vector", lambda e, C=C, iota_r=iota_r, iota_c=iota_c:
               e.tensor_tensor(out=C["ltri_b"][:], in0=iota_r[:], in1=iota_c[:],
                               op=ALU.is_lt),
               needs=["su:ic", "su:ir"], produces=["ltri"])
        p.emit("vector", lambda e, C=C, iota_c=iota_c: e.tensor_scalar(
            out=C["iof0"][:], in0=iota_c[:], scalar1=1.0, scalar2=None,
            op0=ALU.add), needs=["su:ic"], produces=["iof0"])
        p.emit("vector", lambda e, C=C, iota_c=iota_c: e.tensor_scalar(
            out=C["iof1"][:], in0=iota_c[:], scalar1=129.0, scalar2=None,
            op0=ALU.add), needs=["su:ic"], produces=["iof1"])
        p.emit("vector", lambda e, C=C, iota_r=iota_r: e.tensor_scalar(
            out=t1[:], in0=iota_r[:, 0:1], scalar1=64.0, scalar2=64.0,
            op0=ALU.is_ge, op1=ALU.mult) if False else e.tensor_scalar(
            out=t1[:], in0=iota_r[:, 0:1], scalar1=64.0, scalar2=64.0,
            op0=ALU.is_ge, op1=ALU.mult), needs=["su:ir"], produces=["su:r64"])
        p.emit("vector", lambda e, iota_r=iota_r, t1=t1, t2=t2: e.tensor_tensor(
            out=t2[:], in0=iota_r[:, 0:1], in1=t1[:], op=ALU.subtract),
            needs=["su:r64"], produces=["su:loc"])
        p.emit("vector", lambda e, t2=t2, t3=t3: e.tensor_scalar(
            out=t3[:], in0=t2[:], scalar1=32.0, scalar2=-64.0,
            op0=ALU.is_ge, op1=ALU.mult), needs=["su:loc"], produces=["su:u"])
        p.emit("vector", lambda e, t2=t2: e.tensor_scalar(
            out=t2[:], in0=t2[:], scalar1=32.0, scalar2=None, op0=ALU.add),
            needs=["su:u"], produces=["su:l32"])
        p.emit("vector", lambda e, t2=t2, t3=t3: e.tensor_tensor(
            out=t2[:], in0=t2[:], in1=t3[:], op=ALU.add),
            needs=["su:l32"], produces=["su:tl"])
        p.emit("vector", lambda e, t1=t1, t2=t2: e.tensor_tensor(
            out=t2[:], in0=t2[:], in1=t1[:], op=ALU.add),
            needs=["su:tl"], produces=["su:tgt"])
        p.emit("vector", lambda e, C=C, iota_c=iota_c, t2=t2: e.tensor_scalar(
            out=C["m2swap"][:], in0=iota_c[:], scalar1=t2[:], scalar2=None,
            op0=ALU.is_equal), needs=["su:tgt", "su:ic"], produces=["m2swap"])
        p.emit("vector", lambda e, C=C: e.memset(C["b2_sb"][:], 0.0),
               produces=["b2z"])
        p.emit("vector", lambda e, C=C: e.memset(C["br_sb"][:], 0.0),
               produces=["brz"])
        p.emit("vector", lambda e, b1_stage=b1_stage: e.memset(b1_stage[:], 0.0),
               produces=["b1z"])

        p.emit(g, lambda e, C=C, wr=wr: e.dma_start(
            out=C["wr_sb"][:, 0:DC * E].rearrange("p (dc e) -> p dc e", e=E),
            in_=wr.rearrange("(dc p) e -> p dc e", p=128)),
            sem="d_wr", dma=True, produces=["wr_sb"])
        p.emit(g, lambda e, C=C, br=br: e.dma_start(
            out=C["br_sb"][0:1, 0:E], in_=br.unsqueeze(0)),
            needs=["brz"], sem="d_br", dma=True, produces=["br_sb"])
        p.emit(g, lambda e, C=C, b2=b2: e.dma_start(
            out=C["b2_sb"][0:E, :], in_=b2),
            needs=["b2z"], sem="d_b2", dma=True, produces=["b2_sb"])
        p.emit("sync", lambda e, b1_stage=b1_stage, b1=b1: e.dma_start(
            out=b1_stage[0:E, :], in_=b1), needs=["b1z"],
            sem="d_b1", dma=True, produces=["b1_stage"])
        p.emit("sync", lambda e, bo_stage=bo_stage, b_out=b_out: e.dma_start(
            out=bo_stage[:], in_=b_out.rearrange("(dc p) -> dc p", p=128)),
            sem="d_bo", dma=True, produces=["bo_stage"])

        for hc in range(HC):
            p.emit("tensor", lambda e, hc=hc, ps_su=ps_su, b1_stage=b1_stage,
                   C=C: e.transpose(
                       ps_su[:, 0:16], b1_stage[0:16, hc * 128:(hc + 1) * 128],
                       C["ident_f"][0:16, 0:16]),
                   needs=["b1_stage", "identf"] + ([f"b1t:{hc-1}"] if hc else []),
                   produces=[f"ps_b1t:{hc}"])
            p.emit("vector", lambda e, hc=hc, ps_su=ps_su, C=C: e.tensor_copy(
                C["b1t"][:, hc * E:(hc + 1) * E], ps_su[:, 0:E]),
                needs=[f"ps_b1t:{hc}"], produces=[f"b1t:{hc}"])
        p.emit("tensor", lambda e, ps_su=ps_su, bo_stage=bo_stage, C=C: e.transpose(
            ps_su[:, 0:DC], bo_stage[:, :], C["ident_f"][0:DC, 0:DC]),
            needs=["bo_stage", "identf", f"b1t:{HC-1}"], produces=["ps_bo"])
        p.emit("vector", lambda e, ps_su=ps_su, C=C: e.tensor_copy(
            C["boutt"][:, 0:DC], ps_su[:, 0:DC]),
            needs=["ps_bo"], produces=["boutt"])

        p.emit(g, lambda e, invf=invf: e.iota(
            invf[:], [[0, 1]], channel_multiplier=1,
            allow_small_or_imprecise_dtypes=True), produces=["cs:i"])
        p.emit("scalar", lambda e, invf=invf: e.activation(
            out=invf[:], in_=invf[:], func=AF.Exp, bias=0.0,
            scale=-math.log(10000.0) / 32.0), needs=["cs:i"], produces=["cs:if"])
        p.emit(g, lambda e, frq=frq, T=T: e.iota(
            frq[:], [[1, T]], channel_multiplier=0,
            allow_small_or_imprecise_dtypes=True), produces=["cs:t"])
        p.emit("vector", lambda e, frq=frq, invf=invf: e.tensor_scalar(
            out=frq[:], in0=frq[:], scalar1=invf[0:32, :], scalar2=None,
            op0=ALU.mult), needs=["cs:t", "cs:if"], produces=["cs:f"])

        def rr(dst, bias, tg):
            p.emit("vector", lambda e, dst=dst, frq=frq,
                   bias=bias: e.tensor_scalar(
                       out=dst[:], in0=frq[:], scalar1=bias,
                       scalar2=1.0 / TWO_PI, op0=ALU.add, op1=ALU.mult),
                   needs=["cs:f"], produces=[f"cs:u{tg}"])
            p.emit("vector", lambda e, dst=dst, kint=kint: e.tensor_copy(
                kint[:], dst[:]),
                needs=[f"cs:u{tg}"], produces=[f"cs:k{tg}"])
            p.emit("vector", lambda e, dst=dst, kint=kint: e.tensor_copy(
                dst[:], kint[:]),
                needs=[f"cs:k{tg}"], produces=[f"cs:kf{tg}"])
            p.emit("vector", lambda e, dst=dst, frq=frq: e.scalar_tensor_tensor(
                out=dst[:], in0=dst[:], scalar=-TWO_PI,
                in1=frq[:], op0=ALU.mult, op1=ALU.add),
                needs=[f"cs:kf{tg}", "cs:f"], produces=[f"cs:d{tg}"])
            if bias != 0.0:
                p.emit("vector", lambda e, dst=dst, bias=bias: e.tensor_scalar(
                    out=dst[:], in0=dst[:], scalar1=bias, scalar2=None,
                    op0=ALU.add),
                    needs=[f"cs:d{tg}"], produces=[f"cs:db{tg}"])
            else:
                p.res[f"cs:db{tg}"] = p.res[f"cs:d{tg}"]
            p.emit("vector", lambda e, dst=dst, tb2=tb2: e.tensor_scalar(
                out=tb2[:], in0=dst[:], scalar1=math.pi,
                scalar2=-TWO_PI, op0=ALU.is_gt, op1=ALU.mult),
                needs=[f"cs:db{tg}"], produces=[f"cs:w{tg}"])
            p.emit("vector", lambda e, dst=dst, tb2=tb2: e.tensor_tensor(
                out=dst[:], in0=dst[:], in1=tb2[:], op=ALU.add),
                needs=[f"cs:w{tg}"], produces=[f"cs:r0{tg}"])
            p.emit("vector", lambda e, dst=dst, tb2=tb2: e.tensor_scalar(
                out=tb2[:], in0=dst[:], scalar1=-math.pi,
                scalar2=TWO_PI, op0=ALU.is_lt, op1=ALU.mult),
                needs=[f"cs:r0{tg}"], produces=[f"cs:w2{tg}"])
            p.emit("vector", lambda e, dst=dst, tb2=tb2: e.tensor_tensor(
                out=dst[:], in0=dst[:], in1=tb2[:], op=ALU.add),
                needs=[f"cs:w2{tg}"], produces=[f"cs:r{tg}"])
            p.emit("scalar", lambda e, dst=dst: e.activation(
                out=dst[:], in_=dst[:], func=AF.Sin, bias=0.0,
                scale=1.0), needs=[f"cs:r{tg}"], produces=[f"cs:s{tg}"])

        rr(tab_s, 0.0, "s")
        rr(tab_c, math.pi / 2.0, "c")
        for kk in range(4):
            p.emit("sync", lambda e, kk=kk, tab_c=tab_c, cs_scr=cs_scr: e.dma_start(
                out=cs_scr[0, kk * 32:(kk + 1) * 32, :], in_=tab_c[:]),
                needs=["cs:sc"], sem="d_csp", dma=True, produces=[f"cs:C{kk}"])
        p.emit("vector", lambda e, tab_s=tab_s, tb2=tb2: e.tensor_scalar(
            out=tb2[:], in0=tab_s[:], scalar1=-1.0, scalar2=None,
            op0=ALU.mult), needs=["cs:ss", "cs:rc"],
            produces=["cs:neg"])
        for kk in range(4):
            buf, srcr = ((tb2, "cs:neg") if kk % 2 == 0 else (tab_s, "cs:ss"))
            p.emit("sync", lambda e, kk=kk, buf=buf, cs_scr=cs_scr: e.dma_start(
                out=cs_scr[1, kk * 32:(kk + 1) * 32, :], in_=buf[:]),
                needs=[srcr], sem="d_csp", dma=True, produces=[f"cs:S{kk}"])


def ffn_phase(nc, p, ctx, C, a):
    T, E, NQ, QT, TC = a["T"], a["E"], a["NQ"], a["QT"], a["TC"]
    L = a["layer"]
    w1, w2 = a["w1"], a["w2"]
    NTC = 2
    ntb = QT // 128
    CAP = 256
    NCC = CAP // 128
    PADE = max(E, 8)
    assert E == 16 and QT == 1024 and TC == 512

    with ExitStack() as ph:
        psb = lambda n, s, d=F32: ph.enter_context(
            nc.sbuf_tensor(f"f{L}_{n}", s, d))
        pps = lambda n: ph.enter_context(
            nc.psum_tensor(f"f{L}_{n}", [128, 512], F32))
        x_q = psb("x_q", [128, DC * QT])
        xln = psb("xln", [128, DC * QT], BF16)
        xtm = psb("xtm", [128, ntb * D], BF16)
        stage = psb("stage", [128, D])
        xbf = [psb("xbf0", [128, TC], BF16), psb("xbf1", [128, TC], BF16)]
        xsq = [psb("xsq0", [128, TC], BF16), psb("xsq1", [128, TC], BF16)]
        row0 = psb("row0", [1, TC])
        row1 = psb("row1", [1, TC])
        hil = psb("hil", [1, 4 * TC], BF16)
        tvec = psb("tvec", [128, TC])
        uvec = psb("uvec", [128, TC])
        wbufs = [psb("wa", [128, DC * HFF], BF16),
                 psb("wb", [128, DC * HFF], BF16),
                 psb("wc", [128, DC * HFF], BF16)]
        hact = psb("hact", [128, HC * CAP], BF16)
        xg = psb("xg", [128, DC * CAP], BF16)
        otm = xg
        sS = psb("sS", [128, NCC * QT], BF16)
        stmp = [psb(f"stmp{i}", [128, 128], BF16) for i in range(3)]
        pc_all = psb("pc_all", [128, ntb * E])
        comb_all = psb("comb_all", [128, ntb * E])
        offs_f = psb("offs_f", [1, E])
        offs_b = psb("offs_b", [1, 2 * E], BF16)
        mask_b = psb("mask_b", [128, E], BF16)
        comb_tm = psb("comb_tm", [128, 16])
        chi = psb("chi", [128, 16], BF16)
        clo = psb("clo", [128, 16], BF16)
        comb_hi = psb("comb_hi", [16, QT], BF16)
        comb_lo = psb("comb_lo", [16, QT], BF16)
        expd = psb("expd", [128, PADE])
        m8 = psb("m8", [128, 8])
        rsum = psb("rsum", [128, 1])
        nmax = psb("nmax", [128, 1])
        ln_scr = (xbf, xsq, row0, row1, hil, tvec, uvec)

        def P_ap(tb):
            return stage[:, tb * 128:(tb + 1) * 128].bitcast(BF16)

        ps_pool = Bank([pps(f"ps_p{i}") for i in range(6)])
        ps_h = ps_o = ps_c = ps_pool
        ps_m = Bank([pps("ps_m")])
        ps_r = Bank([pps("ps_r")])

        wsem = [p.addsem(f"d_f{L}w{i}", dma=True) for i in range(3)]
        wsem_s = [p.addsem(f"d_f{L}s{i}", dma=True) for i in range(3)]
        wbsem = ([p.addsem(f"d_wb{i}", dma=True) for i in range(3)]
                 if a.get("w_cast") else None)
        w_scr = a.get("w_scr")
        xsem = p.addsem(f"d_f{L}x", dma=True)
        osem = p.addsem(f"d_f{L}o", dma=True)
        wfree = [[], [], []]
        WB_ALL = ["wb_all0", "wb_all1", "wb_all2"]
        xq_free = [None, None]
        stage_free = None
        xln_free = None
        xtm_guard = None
        pP_guard = None
        xg_guard = None
        hact_guard = None
        otm_guard = None
        sS_guard = None
        offs_guard = None
        mask_guard = None

        p.emit("vector", lambda e, comb_tm=comb_tm: e.memset(comb_tm[:], 0.0),
               produces=[f"f{L}:ctz"])
        p.emit("vector", lambda e, expd=expd: e.memset(expd[:], 0.0),
               produces=[f"f{L}:exz"])
        p.emit("vector", lambda e, comb_hi=comb_hi: e.memset(comb_hi[:], 0.0),
               produces=[f"f{L}:chz0"])
        p.emit("vector", lambda e, comb_lo=comb_lo: e.memset(comb_lo[:], 0.0),
               needs=[f"f{L}:chz0"], produces=[f"f{L}:chz"])
        comb_hl_free = f"f{L}:chz"
        expd_guard = f"f{L}:exz"
        nmax_guard = None
        chi_guard = None
        clo_guard = None

        for q in range(NQ):
            qt0 = q * QT
            xq_ready = {0: [], 1: []}
            if a["x_fm_in"] is not None:
                for tc in range(NTC):
                    t0l = tc * TC
                    p.emit("sync", lambda e, qt0=qt0, t0l=t0l, tc=tc, x_q=x_q,
                           a=a, QT=QT, TC=TC: e.dma_start(
                        out=x_q[:].rearrange("p (c t) -> p c t",
                                             t=QT)[:, :, t0l:t0l + TC],
                        in_=a["x_fm_in"][:, :, qt0 + t0l:qt0 + t0l + TC]
                        .transpose([1, 0, 2])),
                        needs=[xq_free[tc]], sem=xsem, dma=True,
                        produces=[f"f{L}:xq:{q}:{tc}"])
                    xq_ready[tc] = [f"f{L}:xq:{q}:{tc}"]
            else:
                for tb in range(ntb):
                    p.emit("sync", lambda e, qt0=qt0, tb=tb, stage=stage,
                           a=a: e.dma_start(
                               out=stage[:],
                               in_=a["x_nat_in"][qt0 + tb * 128:
                                                 qt0 + (tb + 1) * 128, :]),
                           needs=[xq_free[tb // (ntb // NTC)], stage_free],
                           sem=xsem, dma=True,
                           produces=[f"f{L}:xs:{q}:{tb}"])
                    for dc in range(DC):
                        bank = ps_m if dc % 2 == 0 else ps_r
                        bi, pst, gg = bank.acquire()
                        p.emit("tensor", lambda e, dc=dc, pst=pst, stage=stage,
                               C=C: e.transpose(
                                   pst[:, 0:128],
                                   stage[:, dc * 128:(dc + 1) * 128],
                                   C["ident_f"][:]),
                               needs=[f"f{L}:xs:{q}:{tb}", "identf"] + gg,
                               produces=[f"f{L}:it:{q}:{tb}:{dc}"])
                        p.emit("vector", lambda e, dc=dc, tb=tb, pst=pst,
                               x_q=x_q, QT=QT: e.tensor_copy(
                                   x_q[:, dc * QT + tb * 128:
                                       dc * QT + (tb + 1) * 128],
                                   pst[:, 0:128]),
                               needs=[f"f{L}:it:{q}:{tb}:{dc}"],
                               produces=[f"f{L}:ic:{q}:{tb}:{dc}"])
                        bank.consumed(bi, f"f{L}:ic:{q}:{tb}:{dc}")
                        xq_ready[tb // (ntb // NTC)].append(
                            f"f{L}:ic:{q}:{tb}:{dc}")
                    stage_free = f"f{L}:it:{q}:{tb}:{DC-1}"

            ln_done = list(xq_ready[0]) + list(xq_ready[1])
            ln_tc = {}
            for tc in range(NTC):
                t0 = tc * TC
                done = emit_ln(
                    p, C, f"f{L}:ln:{q}:{tc}", TC, x_q,
                    lambda dc, t0=t0, QT=QT: slice(dc * QT + t0,
                                                   dc * QT + t0 + TC),
                    xln,
                    lambda dc, t0=t0, QT=QT: slice(dc * QT + t0,
                                                   dc * QT + t0 + TC),
                    ln_scr, (ps_m, ps_r), xq_ready[tc], True,
                    xln_guard=xln_free)
                ln_done += done
                ln_tc[tc] = [r for r in done if ":xln:" in r]

            rt_done = []
            rt_tc = {}
            p.emit("vector", lambda e, offs_f=offs_f: e.memset(offs_f[:], 0.0),
                   needs=[offs_guard], produces=[f"f{L}:oz:{q}"])
            p.emit("vector", lambda e, offs_b=offs_b: e.memset(offs_b[:], 0.0),
                   needs=[offs_guard, f"f{L}:oz:{q}"],
                   produces=[f"f{L}:ozb:{q}"])
            for tb in range(ntb):
                bi, psc, gg = ps_c.acquire()
                for dc in range(DC):
                    p.emit("tensor", lambda e, dc=dc, tb=tb, psc=psc, xln=xln,
                           C=C, QT=QT, E=E: e.matmul(
                               psc[:, 0:E],
                               xln[:, dc * QT + tb * 128:
                                   dc * QT + (tb + 1) * 128],
                               C["wr_sb"][:, dc * E:(dc + 1) * E],
                               start=(dc == 0), stop=False),
                           needs=(ln_tc[tb // (ntb // NTC)] + ["wr_sb"] +
                                  gg) if dc == 0 else [],
                           produces=())
                p.emit("tensor", lambda e, psc=psc, C=C, E=E: e.matmul(
                    psc[:, 0:E], C["ones_b"][0:1, :], C["br_sb"][0:1, 0:E],
                    start=False, stop=True),
                    needs=["br_sb", "onesb"], produces=[f"f{L}:rt:{q}:{tb}"])
                p.emit("vector", lambda e, psc=psc, nmax=nmax, E=E: e.tensor_reduce(
                    out=nmax[:], in_=psc[:, 0:E], axis=AXX, op=ALU.max,
                    negate=True),
                    needs=[f"f{L}:rt:{q}:{tb}", nmax_guard],
                    produces=[f"f{L}:nm:{q}:{tb}"])
                p.emit("scalar", lambda e, psc=psc, expd=expd, nmax=nmax,
                       E=E: e.activation(
                           out=expd[:, 0:E], in_=psc[:, 0:E], func=AF.Exp,
                           bias=nmax[:], scale=1.0),
                       needs=[f"f{L}:nm:{q}:{tb}", expd_guard],
                       produces=[f"f{L}:ex:{q}:{tb}"])
                p.emit("vector", lambda e, expd=expd, m8=m8, PADE=PADE: e.max(
                    m8[:], expd[:, 0:PADE]),
                    needs=[f"f{L}:ex:{q}:{tb}"], produces=[f"f{L}:m8:{q}:{tb}"])
                p.emit("vector", lambda e, m8=m8, rsum=rsum: e.tensor_tensor(
                    out=rsum[:], in0=m8[:, 0:1], in1=m8[:, 1:2], op=ALU.add),
                    needs=[f"f{L}:m8:{q}:{tb}"], produces=[f"f{L}:s:{q}:{tb}"])
                p.emit("vector", lambda e, rsum=rsum: e.reciprocal(
                    rsum[:], rsum[:]),
                    needs=[f"f{L}:s:{q}:{tb}"], produces=[f"f{L}:rc:{q}:{tb}"])
                p.emit("vector", lambda e, rsum=rsum: e.tensor_scalar(
                    out=rsum[:], in0=rsum[:], scalar1=0.5, scalar2=None,
                    op0=ALU.mult),
                    needs=[f"f{L}:rc:{q}:{tb}"], produces=[f"f{L}:rh:{q}:{tb}"])
                p.emit("vector", lambda e, expd=expd, m8=m8, comb_tm=comb_tm,
                       E=E: e.tensor_scalar(
                           out=comb_tm[:, 0:E], in0=expd[:, 0:E],
                           scalar1=m8[:, 1:2], scalar2=None, op0=ALU.is_ge),
                       needs=[f"f{L}:m8:{q}:{tb}", f"f{L}:ctz"],
                       produces=[f"f{L}:mk:{q}:{tb}"])
                p.emit("vector", lambda e, expd=expd, rsum=rsum,
                       comb_tm=comb_tm, E=E: e.scalar_tensor_tensor(
                           out=comb_tm[:, 0:E], in0=expd[:, 0:E],
                           scalar=rsum[:], in1=comb_tm[:, 0:E],
                           op0=ALU.mult, op1=ALU.mult),
                       needs=[f"f{L}:mk:{q}:{tb}", f"f{L}:rh:{q}:{tb}"],
                       produces=[f"f{L}:cb:{q}:{tb}"])
                expd_guard = f"f{L}:cb:{q}:{tb}"
                nmax_guard = f"f{L}:ex:{q}:{tb}"
                p.emit("vector", lambda e, chi=chi, comb_tm=comb_tm: e.tensor_copy(
                    chi[:], comb_tm[:]),
                    needs=[f"f{L}:cb:{q}:{tb}", chi_guard],
                    produces=[f"f{L}:ch:{q}:{tb}"])
                p.emit("vector", lambda e, clo=clo, chi=chi,
                       comb_tm=comb_tm: e.tensor_tensor(
                           out=clo[:], in0=comb_tm[:], in1=chi[:],
                           op=ALU.subtract),
                       needs=[f"f{L}:ch:{q}:{tb}", clo_guard],
                       produces=[f"f{L}:cl:{q}:{tb}"])
                bi2, psc2, gg2 = ps_c.acquire()
                p.emit("tensor", lambda e, psc2=psc2, chi=chi, C=C: e.transpose(
                    psc2[0:16, 0:64].bitcast(BF16), chi[:], C["ident_b"][:]),
                    needs=[f"f{L}:ch:{q}:{tb}", "identb"] + gg2,
                    produces=[f"f{L}:th:{q}:{tb}"])
                bi3, psc3, gg3 = ps_c.acquire()
                p.emit("tensor", lambda e, psc3=psc3, clo=clo, C=C: e.transpose(
                    psc3[0:16, 0:64].bitcast(BF16), clo[:], C["ident_b"][:]),
                    needs=[f"f{L}:cl:{q}:{tb}"] + gg3,
                    produces=[f"f{L}:tl:{q}:{tb}"])
                p.emit("vector", lambda e, tb=tb, psc2=psc2,
                       comb_hi=comb_hi: e.tensor_copy(
                           comb_hi[0:16, tb * 128:(tb + 1) * 128],
                           psc2[0:16, 0:64].bitcast(BF16)),
                       needs=[f"f{L}:th:{q}:{tb}", comb_hl_free],
                       produces=[f"f{L}:chl:{q}:{tb}"])
                p.emit("vector", lambda e, tb=tb, psc3=psc3,
                       comb_lo=comb_lo: e.tensor_copy(
                           comb_lo[0:16, tb * 128:(tb + 1) * 128],
                           psc3[0:16, 0:64].bitcast(BF16)),
                       needs=[f"f{L}:tl:{q}:{tb}", comb_hl_free],
                       produces=[f"f{L}:cll:{q}:{tb}"])
                ps_c.consumed(bi, f"f{L}:ex:{q}:{tb}")
                ps_c.consumed(bi2, f"f{L}:chl:{q}:{tb}")
                ps_c.consumed(bi3, f"f{L}:cll:{q}:{tb}")
                rt_done += [f"f{L}:chl:{q}:{tb}", f"f{L}:cll:{q}:{tb}"]
                rt_tc[tb] = [f"f{L}:chl:{q}:{tb}", f"f{L}:cll:{q}:{tb}"]
                chi_guard = f"f{L}:th:{q}:{tb}"
                clo_guard = f"f{L}:tl:{q}:{tb}"

                p.emit("vector", lambda e, comb_tm=comb_tm, mask_b=mask_b,
                       E=E: e.tensor_scalar(
                           out=mask_b[:, 0:E], in0=comb_tm[:, 0:E],
                           scalar1=0.0, scalar2=None, op0=ALU.is_gt),
                       needs=[f"f{L}:cb:{q}:{tb}", mask_guard],
                       produces=[f"f{L}:mkb:{q}:{tb}"])
                p.emit("vector", lambda e, tb=tb, comb_all=comb_all,
                       comb_tm=comb_tm, E=E: e.tensor_copy(
                           comb_all[:, tb * E:(tb + 1) * E],
                           comb_tm[:, 0:E]),
                       needs=[f"f{L}:cb:{q}:{tb}"],
                       produces=[f"f{L}:ca:{q}:{tb}"])
                mi, pspin, mgg = ps_m.acquire()
                offs_state = (f"f{L}:ozb:{q}" if tb == 0
                              else f"f{L}:obl:{q}:{tb-1}")
                p.emit("tensor", lambda e, pspin=pspin, C=C, mask_b=mask_b,
                       E=E: e.matmul(
                           pspin[:, 0:E], C["ltri_b"][:], mask_b[:, 0:E],
                           start=True, stop=False),
                       needs=[f"f{L}:mkb:{q}:{tb}", "ltri"] + mgg,
                       produces=())
                p.emit("tensor", lambda e, pspin=pspin, C=C, offs_b=offs_b,
                       E=E: e.matmul(
                           pspin[:, 0:E], C["ones_b"][0:1, :],
                           offs_b[0:1, 0:E], start=False, stop=False),
                       needs=[offs_state, "onesb"], produces=())
                p.emit("tensor", lambda e, pspin=pspin, C=C, offs_b=offs_b,
                       E=E: e.matmul(
                           pspin[:, 0:E], C["ones_b"][0:1, :],
                           offs_b[0:1, E:2 * E], start=False, stop=True),
                       needs=[], produces=[f"f{L}:pin:{q}:{tb}"])
                p.emit("vector", lambda e, tb=tb, pspin=pspin, pc_all=pc_all,
                       mask_b=mask_b, E=E: e.scalar_tensor_tensor(
                           out=pc_all[:, tb * E:(tb + 1) * E],
                           in0=pspin[:, 0:E], scalar=1.0,
                           in1=mask_b[:, 0:E], op0=ALU.add, op1=ALU.mult),
                       needs=[f"f{L}:pin:{q}:{tb}", f"f{L}:mkb:{q}:{tb}"],
                       produces=[f"f{L}:pc:{q}:{tb}"])
                ri2, pscnt, rgg2 = ps_r.acquire()
                p.emit("tensor", lambda e, pscnt=pscnt, C=C, mask_b=mask_b,
                       E=E: e.matmul(
                           pscnt[0:1, 0:E], C["ones_b"][:, 0:1],
                           mask_b[:, 0:E], start=True, stop=True),
                       needs=[f"f{L}:mkb:{q}:{tb}", "onesb"] + rgg2,
                       produces=[f"f{L}:cnt:{q}:{tb}"])
                p.emit("vector", lambda e, offs_f=offs_f, pscnt=pscnt,
                       E=E: e.tensor_tensor(
                           out=offs_f[0:1, 0:E], in0=offs_f[0:1, 0:E],
                           in1=pscnt[0:1, 0:E], op=ALU.add),
                       needs=[f"f{L}:cnt:{q}:{tb}"],
                       produces=[f"f{L}:of:{q}:{tb}"])
                p.emit("vector", lambda e, offs_b=offs_b, offs_f=offs_f,
                       E=E: e.tensor_copy(offs_b[0:1, 0:E], offs_f[0:1, 0:E]),
                       needs=[f"f{L}:of:{q}:{tb}", f"f{L}:pin:{q}:{tb}"],
                       produces=[f"f{L}:obh:{q}:{tb}"])
                p.emit("vector", lambda e, offs_b=offs_b, offs_f=offs_f,
                       E=E: e.tensor_tensor(
                           out=offs_b[0:1, E:2 * E], in0=offs_f[0:1, 0:E],
                           in1=offs_b[0:1, 0:E], op=ALU.subtract),
                       needs=[f"f{L}:obh:{q}:{tb}"],
                       produces=[f"f{L}:obl:{q}:{tb}"])
                ps_m.consumed(mi, f"f{L}:pc:{q}:{tb}")
                ps_r.consumed(ri2, f"f{L}:of:{q}:{tb}")
                mask_guard = f"f{L}:cnt:{q}:{tb}"
            offs_guard = f"f{L}:pin:{q}:{ntb-1}"

            tm_all = []
            for tb in range(ntb):
                tc = tb // (ntb // NTC)
                for dc in range(DC):
                    bank = ps_m if dc % 2 == 0 else ps_r
                    bi, pst, gg = bank.acquire()
                    p.emit("tensor", lambda e, tb=tb, dc=dc, pst=pst, xln=xln,
                           C=C, QT=QT: e.transpose(
                               pst[:, 0:64].bitcast(BF16),
                               xln[:, dc * QT + tb * 128:
                                   dc * QT + (tb + 1) * 128],
                               C["ident_b"][:]),
                           needs=[f"f{L}:ln:{q}:{tc}:xln:{dc}", "identb"] + gg,
                           produces=[f"f{L}:tmt:{q}:{tb}:{dc}"])
                    p.emit("vector", lambda e, tb=tb, dc=dc, pst=pst,
                           xtm=xtm: e.tensor_copy(
                               xtm[:, tb * D + dc * 128:
                                   tb * D + (dc + 1) * 128],
                               pst[:, 0:64].bitcast(BF16)),
                           needs=[f"f{L}:tmt:{q}:{tb}:{dc}", xtm_guard],
                           produces=[f"f{L}:tm:{q}:{tb}:{dc}"])
                    bank.consumed(bi, f"f{L}:tm:{q}:{tb}:{dc}")
            xln_free = f"f{L}:tmt:{q}:{ntb-1}:{DC-1}"

            use_cast = a.get("w_cast") and q == 0

            def emit_P(ee, extra):
                for tb in range(ntb):
                    for cc in range(NCC):
                        p.emit("gpsimd", lambda e, tb=tb, cc=cc, ee=ee,
                               pc_all=pc_all, C=C, E=E: e.tensor_scalar(
                                   out=P_ap(tb)[:, cc * 128:(cc + 1) * 128],
                                   in0=C[f"iof{cc}"][:],
                                   scalar1=pc_all[:, tb * E + ee:
                                                  tb * E + ee + 1],
                                   scalar2=None, op0=ALU.is_equal),
                               needs=[f"f{L}:pc:{q}:{tb}", f"iof{cc}"] + extra,
                               produces=[f"f{L}:P:{q}:{ee}:{tb}:{cc}"])

            emit_P(0, [pP_guard, stage_free])
            for ee in range(E):
                wi, wj = (2 * ee) % 3, (2 * ee + 1) % 3
                if use_cast or w_scr is None:
                    p.emit("gpsimd", lambda e, ee=ee, wi=wi, wbufs=wbufs,
                           w1=w1: e.dma_start(
                               out=wbufs[wi][:].rearrange("p (dc h) -> p dc h", h=HFF),
                               in_=w1[ee].rearrange("(dc p) h -> p dc h", p=128)),
                           needs=wfree[wi], sem=wsem[wi], dma=True,
                           produces=[f"f{L}:w1:{q}:{ee}"])
                    p.emit("gpsimd", lambda e, ee=ee, wj=wj, wbufs=wbufs,
                           w2=w2: e.dma_start(
                               out=wbufs[wj][:, 0:HC * D].rearrange("p (hc d) -> p hc d", d=D),
                               in_=w2[ee].rearrange("(hc p) d -> p hc d", p=128)),
                           needs=wfree[wj], sem=wsem[wj], dma=True,
                           produces=[f"f{L}:w2:{q}:{ee}"])
                    if use_cast:
                        p.emit("sync", lambda e, ee=ee, wi=wi, wbufs=wbufs,
                               w_scr=w_scr: e.dma_start(
                                   out=w_scr[ee, 0], in_=wbufs[wi][:]),
                               needs=[f"f{L}:w1:{q}:{ee}"], sem=wbsem[wi],
                               dma=True, produces=[f"wb1:{ee}"])
                        p.emit("sync", lambda e, ee=ee, wj=wj, wbufs=wbufs,
                               w_scr=w_scr: e.dma_start(
                                   out=w_scr[ee, 1][:, 0:HC * D],
                                   in_=wbufs[wj][:, 0:HC * D]),
                               needs=[f"f{L}:w2:{q}:{ee}"], sem=wbsem[wj],
                               dma=True, produces=[f"wb2:{ee}"])
                else:
                    p.emit("sync", lambda e, ee=ee, wi=wi, wbufs=wbufs,
                           w_scr=w_scr: e.dma_start(
                               out=wbufs[wi][:], in_=w_scr[ee, 0]),
                           needs=wfree[wi] + WB_ALL, sem=wsem_s[wi],
                           dma=True, produces=[f"f{L}:w1:{q}:{ee}"])
                    p.emit("gpsimd", lambda e, ee=ee, wj=wj, wbufs=wbufs,
                           w_scr=w_scr: e.dma_start(
                               out=wbufs[wj][:, 0:HC * D],
                               in_=w_scr[ee, 1][:, 0:HC * D]),
                           needs=wfree[wj] + WB_ALL, sem=wsem_s[wj],
                           dma=True, produces=[f"f{L}:w2:{q}:{ee}"])
                for dc in range(DC):
                    ci, psg, cgg = ps_c.acquire()
                    for tb in range(ntb):
                        p.emit("tensor", lambda e, tb=tb, dc=dc, psg=psg,
                               xtm=xtm, CAP=CAP: e.matmul(
                                   psg[:, 0:CAP],
                                   xtm[:, tb * D + dc * 128:
                                       tb * D + (dc + 1) * 128],
                                   P_ap(tb),
                                   start=(tb == 0), stop=(tb == ntb - 1)),
                               needs=[f"f{L}:tm:{q}:{tb}:{dc}",
                                      f"f{L}:P:{q}:{ee}:{tb}:0",
                                      f"f{L}:P:{q}:{ee}:{tb}:1"] +
                               (cgg if tb == 0 else []),
                               produces=[f"f{L}:g:{q}:{ee}:{dc}"]
                               if tb == ntb - 1 else ())
                    p.emit("vector", lambda e, dc=dc, psg=psg, xg=xg,
                           CAP=CAP: e.tensor_copy(
                               xg[:, dc * CAP:(dc + 1) * CAP],
                               psg[:, 0:CAP]),
                           needs=[f"f{L}:g:{q}:{ee}:{dc}", otm_guard],
                           produces=[f"f{L}:xgc:{q}:{ee}:{dc}"])
                    ps_c.consumed(ci, f"f{L}:xgc:{q}:{ee}:{dc}")
                pP_guard = f"f{L}:g:{q}:{ee}:{DC-1}"
                st_hist = [None, None, None]

                def emit_S_chunk(sidx, ee=ee):
                    tb, cc = sidx // NCC, sidx % NCC
                    sl4 = sidx % 3
                    p.emit("gpsimd", lambda e, tb=tb, cc=cc, ee=ee,
                           sl4=sl4, stmp=stmp, comb_all=comb_all,
                           E=E: e.tensor_scalar(
                               out=stmp[sl4][:],
                               in0=P_ap(tb)[:, cc * 128:(cc + 1) * 128],
                               scalar1=comb_all[:, tb * E + ee:
                                                tb * E + ee + 1],
                               scalar2=None, op0=ALU.mult),
                           needs=[f"f{L}:P:{q}:{ee}:{tb}:{cc}",
                                  f"f{L}:ca:{q}:{tb}", st_hist[sl4]],
                           produces=[f"f{L}:sm:{q}:{ee}:{tb}:{cc}"])
                    bank = ps_m if sidx % 2 == 0 else ps_r
                    bi4, pst4, gg4 = bank.acquire()
                    p.emit("tensor", lambda e, sl4=sl4, pst4=pst4,
                           stmp=stmp, C=C: e.transpose(
                               pst4[:, 0:64].bitcast(BF16), stmp[sl4][:],
                               C["ident_b"][:]),
                           needs=[f"f{L}:sm:{q}:{ee}:{tb}:{cc}",
                                  "identb"] + gg4,
                           produces=[f"f{L}:st:{q}:{ee}:{tb}:{cc}"])
                    st_hist[sl4] = f"f{L}:st:{q}:{ee}:{tb}:{cc}"
                    p.emit("vector", lambda e, tb=tb, cc=cc, pst4=pst4,
                           sS=sS, QT=QT: e.tensor_copy(
                               sS[:, cc * QT + tb * 128:
                                  cc * QT + (tb + 1) * 128],
                               pst4[:, 0:64].bitcast(BF16)),
                           needs=[f"f{L}:st:{q}:{ee}:{tb}:{cc}", sS_guard],
                           produces=[f"f{L}:S:{q}:{ee}:{tb}:{cc}"])
                    bank.consumed(bi4, f"f{L}:S:{q}:{ee}:{tb}:{cc}")

                for hcc in range(HC):
                    pi, psh, pgg = ps_h.acquire()
                    for dc in range(DC):
                        p.emit("tensor", lambda e, hcc=hcc, dc=dc, wi=wi,
                               psh=psh, wbufs=wbufs, xg=xg, CAP=CAP: e.matmul(
                                   psh[:, 0:CAP],
                                   wbufs[wi][:, dc * HFF + hcc * 128:
                                             dc * HFF + (hcc + 1) * 128],
                                   xg[:, dc * CAP:(dc + 1) * CAP],
                                   start=(dc == 0), stop=(dc == DC - 1)),
                               needs=[f"f{L}:w1:{q}:{ee}",
                                      f"f{L}:xgc:{q}:{ee}:{dc}"] +
                               (pgg if dc == 0 else []),
                               produces=[f"f{L}:m1:{q}:{ee}:{hcc}"]
                               if dc == DC - 1 else ())
                    p.emit("scalar", lambda e, hcc=hcc, hact=hact, psh=psh,
                           ee=ee, C=C, CAP=CAP, E=E: e.activation(
                               out=hact[:, hcc * CAP:(hcc + 1) * CAP],
                               in_=psh[:, 0:CAP], func=AF.Gelu,
                               bias=C["b1t"][:, hcc * E + ee:
                                             hcc * E + ee + 1],
                               scale=1.0),
                           needs=[f"f{L}:m1:{q}:{ee}:{hcc}", f"b1t:{hcc}",
                                  hact_guard],
                           produces=[f"f{L}:gel:{q}:{ee}:{hcc}"])
                    ps_h.consumed(pi, f"f{L}:gel:{q}:{ee}:{hcc}")
                    emit_S_chunk(hcc)
                xg_guard = f"f{L}:m1:{q}:{ee}:{HC-1}"
                if ee + 1 < E:
                    emit_P(ee + 1, [pP_guard,
                                    f"f{L}:sm:{q}:{ee}:{ntb-1}:{NCC-1}"])
                for sc in range(NCC):
                    for dh in range(2):
                        oi, pso, ogg = ps_o.acquire()
                        for hcc in range(HC):
                            p.emit("tensor", lambda e, hcc=hcc, sc=sc, dh=dh,
                                   wj=wj, pso=pso, wbufs=wbufs, hact=hact,
                                   CAP=CAP: e.matmul(
                                       pso[:, 0:TC],
                                       hact[:, hcc * CAP + sc * 128:
                                            hcc * CAP + (sc + 1) * 128],
                                       wbufs[wj][:, hcc * D + dh * 512:
                                                 hcc * D + (dh + 1) * 512],
                                       start=(hcc == 0), stop=(hcc == HC - 1)),
                                   needs=[f"f{L}:gel:{q}:{ee}:{hcc}"] +
                                   (([f"f{L}:w2:{q}:{ee}"] + ogg)
                                    if hcc == 0 else []),
                                   produces=[f"f{L}:m2:{q}:{ee}:{sc}:{dh}"]
                                   if hcc == HC - 1 else ())
                        p.emit("scalar", lambda e, sc=sc, dh=dh, pso=pso,
                               otm=otm: e.activation(
                                   out=otm[:, sc * D + dh * 512:
                                           sc * D + (dh + 1) * 512],
                                   in_=pso[:, 0:TC], func=AF.Copy, bias=0.0,
                                   scale=1.0),
                               needs=[f"f{L}:m2:{q}:{ee}:{sc}:{dh}",
                                      xg_guard],
                               produces=[f"f{L}:oe:{q}:{ee}:{sc}:{dh}"])
                        ps_o.consumed(oi, f"f{L}:oe:{q}:{ee}:{sc}:{dh}")
                hact_guard = f"f{L}:m2:{q}:{ee}:{NCC-1}:1"
                for dc in range(DC):
                    for th in range(NTC):
                        ci2, psc2s, cgg2s = ps_c.acquire()
                        for cc in range(NCC):
                            p.emit("tensor", lambda e, dc=dc, th=th, cc=cc,
                                   psc2s=psc2s, otm=otm, sS=sS, QT=QT: e.matmul(
                                       psc2s[:, 0:TC],
                                       otm[:, cc * D + dc * 128:
                                           cc * D + (dc + 1) * 128],
                                       sS[:, cc * QT + th * TC:
                                          cc * QT + th * TC + TC],
                                       start=(cc == 0), stop=(cc == NCC - 1)),
                                   needs=[f"f{L}:oe:{q}:{ee}:{cc}:{dc//4}"] +
                                   [f"f{L}:S:{q}:{ee}:{tb2}:{cc}" for tb2 in
                                    range(th * 4, th * 4 + 4)] +
                                   (cgg2s if cc == 0 else []),
                                   produces=[f"f{L}:sc:{q}:{ee}:{dc}:{th}"]
                                   if cc == NCC - 1 else ())
                        p.emit("vector", lambda e, dc=dc, th=th, psc2s=psc2s,
                               x_q=x_q, QT=QT: e.tensor_tensor(
                                   out=x_q[:, dc * QT + th * TC:
                                           dc * QT + th * TC + TC],
                                   in0=x_q[:, dc * QT + th * TC:
                                           dc * QT + th * TC + TC],
                                   in1=psc2s[:, 0:TC], op=ALU.add),
                               needs=[f"f{L}:sc:{q}:{ee}:{dc}:{th}"],
                               produces=[f"f{L}:sca:{q}:{ee}:{dc}:{th}"])
                        ps_c.consumed(ci2, f"f{L}:sca:{q}:{ee}:{dc}:{th}")
                otm_guard = f"f{L}:sc:{q}:{ee}:{DC-1}:{NTC-1}"
                sS_guard = otm_guard
                wfree[wi] = [f"f{L}:m1:{q}:{ee}:{HC-1}"] + \
                    ([f"wb1:{ee}"] if use_cast else [])
                wfree[wj] = [f"f{L}:m2:{q}:{ee}:{NCC-1}:1"] + \
                    ([f"wb2:{ee}"] if use_cast else [])

            xtm_guard = f"f{L}:g:{q}:{E-1}:{DC-1}"
            if use_cast:
                for i in range(3):
                    p.res[f"wb_all{i}"] = (wbsem[i], p.counts[wbsem[i]])
            fin = []
            for tc in range(NTC):
                t0 = tc * TC
                for dc in range(DC):
                    ci, psc, cgg = ps_c.acquire()
                    p.emit("tensor", lambda e, dc=dc, t0=t0, psc=psc, C=C,
                           comb_hi=comb_hi, comb_lo=comb_lo, TC=TC: e.matmul(
                               psc[:, 0:TC],
                               C["b2_sb"][0:16, dc * 128:(dc + 1) * 128],
                               comb_hi[0:16, t0:t0 + TC],
                               start=True, stop=False),
                           needs=["b2_sb"] + rt_done + cgg, produces=())
                    p.emit("tensor", lambda e, dc=dc, t0=t0, psc=psc, C=C,
                           comb_hi=comb_hi, comb_lo=comb_lo, TC=TC: e.matmul(
                               psc[:, 0:TC],
                               C["b2_sb"][0:16, dc * 128:(dc + 1) * 128],
                               comb_lo[0:16, t0:t0 + TC],
                               start=False, stop=True),
                           needs=[], produces=[f"f{L}:b2m:{q}:{tc}:{dc}"])
                    p.emit("vector", lambda e, dc=dc, t0=t0, psc=psc, x_q=x_q,
                           QT=QT, TC=TC: e.tensor_tensor(
                               out=x_q[:, dc * QT + t0: dc * QT + t0 + TC],
                               in0=x_q[:, dc * QT + t0: dc * QT + t0 + TC],
                               in1=psc[:, 0:TC], op=ALU.add),
                           needs=[f"f{L}:b2m:{q}:{tc}:{dc}",
                                  f"f{L}:sca:{q}:{E-1}:{dc}:{tc}"],
                           produces=[f"f{L}:b2a:{q}:{tc}:{dc}"])
                    ps_c.consumed(ci, f"f{L}:b2a:{q}:{tc}:{dc}")
                    fin.append(f"f{L}:b2a:{q}:{tc}:{dc}")

            if a["x_fm_out"] is not None:
                for tc in range(NTC):
                    t0s = tc * TC
                    p.emit("sync", lambda e, qt0=qt0, t0s=t0s, x_q=x_q, a=a,
                           QT=QT, TC=TC: e.dma_start(
                        out=a["x_fm_out"][:, :, qt0 + t0s:qt0 + t0s + TC]
                        .transpose([1, 0, 2]),
                        in_=x_q[:].rearrange("p (c t) -> p c t",
                                             t=QT)[:, :, t0s:t0s + TC]),
                        needs=[f"f{L}:b2a:{q}:{tc}:{dc}" for dc in range(DC)],
                        sem=osem, dma=True, produces=[f"f{L}:xo:{q}:{tc}"])
                    xq_free[tc] = f"f{L}:xo:{q}:{tc}"
            else:
                prev_oc = None
                for tb in range(ntb):
                    ocs = []
                    for dc in range(DC):
                        bank = ps_m if dc % 2 == 0 else ps_r
                        bi, pst, gg = bank.acquire()
                        p.emit("tensor", lambda e, dc=dc, tb=tb, pst=pst,
                               x_q=x_q, C=C, QT=QT: e.transpose(
                                   pst[:, 0:128],
                                   x_q[:, dc * QT + tb * 128:
                                       dc * QT + (tb + 1) * 128],
                                   C["ident_f"][:]),
                               needs=[f"f{L}:b2a:{q}:{tb // (ntb // NTC)}:{d2}"
                                      for d2 in range(DC)] + gg,
                               produces=[f"f{L}:ot:{q}:{tb}:{dc}"])
                        p.emit("vector", lambda e, dc=dc, pst=pst,
                               stage=stage: e.tensor_copy(
                                   stage[:, dc * 128:(dc + 1) * 128],
                                   pst[:, 0:128]),
                               needs=[f"f{L}:ot:{q}:{tb}:{dc}", stage_free],
                               produces=[f"f{L}:oc:{q}:{tb}:{dc}"])
                        bank.consumed(bi, f"f{L}:oc:{q}:{tb}:{dc}")
                        ocs.append(f"f{L}:oc:{q}:{tb}:{dc}")
                        stage_free = None
                    p.emit("sync", lambda e, qt0=qt0, tb=tb, stage=stage,
                           a=a: e.dma_start(
                               out=a["x_nat_out"][qt0 + tb * 128:
                                                  qt0 + (tb + 1) * 128, :],
                               in_=stage[:]),
                           needs=ocs, sem=osem, dma=True,
                           produces=[f"f{L}:os:{q}:{tb}"])
                    stage_free = f"f{L}:os:{q}:{tb}"
                xq_free[0] = f"f{L}:os:{q}:{ntb // NTC - 1}"
                xq_free[1] = f"f{L}:os:{q}:{ntb-1}"
            comb_hl_free = f"f{L}:b2m:{q}:{NTC-1}:{DC-1}"


def ingest_only(nc, p, ctx, C, T, x_in, x_scr):
    with ExitStack() as ph:
        stage = ph.enter_context(nc.sbuf_tensor("ing_stage", [128, D], F32))
        fmt = ph.enter_context(nc.sbuf_tensor("ing_fm", [128, DC * 128], F32))
        bank = Bank([ph.enter_context(nc.psum_tensor("ing_ps0", [128, 512], F32)),
                     ph.enter_context(nc.psum_tensor("ing_ps1", [128, 512], F32))])
        seml = p.addsem("d_ingl", dma=True)
        sems = p.addsem("d_ings", dma=True)
        stage_free = None
        fm_free = None
        for tb in range(T // 128):
            p.emit("sync", lambda e, tb=tb, stage=stage, x_in=x_in: e.dma_start(
                out=stage[:], in_=x_in[tb * 128:(tb + 1) * 128, :]),
                needs=[stage_free], sem=seml, dma=True, produces=[f"ing:l:{tb}"])
            ocs = []
            for dc in range(DC):
                bi, pst, gg = bank.acquire()
                p.emit("tensor", lambda e, dc=dc, pst=pst, stage=stage,
                       C=C: e.transpose(
                           pst[:, 0:128], stage[:, dc * 128:(dc + 1) * 128],
                           C["ident_f"][:]),
                       needs=[f"ing:l:{tb}", "identf"] + gg,
                       produces=[f"ing:t:{tb}:{dc}"])
                p.emit("vector", lambda e, dc=dc, pst=pst, fmt=fmt: e.tensor_copy(
                    fmt[:, dc * 128:(dc + 1) * 128], pst[:, 0:128]),
                    needs=[f"ing:t:{tb}:{dc}", fm_free],
                    produces=[f"ing:c:{tb}:{dc}"])
                bank.consumed(bi, f"ing:c:{tb}:{dc}")
                ocs.append(f"ing:c:{tb}:{dc}")
                fm_free = None
            stage_free = f"ing:t:{tb}:{DC-1}"
            p.emit("sync", lambda e, tb=tb, fmt=fmt, x_scr=x_scr: e.dma_start(
                out=x_scr[:, :, tb * 128:(tb + 1) * 128].transpose([1, 0, 2]),
                in_=fmt[:].rearrange("p (c t) -> p c t", t=128)),
                needs=ocs, sem=sems, dma=True, produces=[f"ing:s:{tb}"])
            fm_free = f"ing:s:{tb}"


def spill_out(nc, p, ctx, C, T, x_scr, out):
    with ExitStack() as ph:
        fmt = ph.enter_context(nc.sbuf_tensor("sp_fm", [128, DC * 128], F32))
        stage = ph.enter_context(nc.sbuf_tensor("sp_st", [128, D], F32))
        bank = Bank([ph.enter_context(nc.psum_tensor("sp_ps0", [128, 512], F32)),
                     ph.enter_context(nc.psum_tensor("sp_ps1", [128, 512], F32))])
        seml = p.addsem("d_spl", dma=True)
        sems = p.addsem("d_sps", dma=True)
        fm_free = None
        stage_free = None
        for tb in range(T // 128):
            p.emit("sync", lambda e, tb=tb, fmt=fmt, x_scr=x_scr: e.dma_start(
                out=fmt[:].rearrange("p (c t) -> p c t", t=128),
                in_=x_scr[:, :, tb * 128:(tb + 1) * 128].transpose([1, 0, 2])),
                needs=[fm_free], sem=seml, dma=True, produces=[f"sp:l:{tb}"])
            ocs = []
            for dc in range(DC):
                bi, pst, gg = bank.acquire()
                p.emit("tensor", lambda e, dc=dc, pst=pst, fmt=fmt,
                       C=C: e.transpose(
                           pst[:, 0:128], fmt[:, dc * 128:(dc + 1) * 128],
                           C["ident_f"][:]),
                       needs=[f"sp:l:{tb}", "identf"] + gg,
                       produces=[f"sp:t:{tb}:{dc}"])
                p.emit("vector", lambda e, dc=dc, pst=pst,
                       stage=stage: e.tensor_copy(
                           stage[:, dc * 128:(dc + 1) * 128], pst[:, 0:128]),
                       needs=[f"sp:t:{tb}:{dc}", stage_free],
                       produces=[f"sp:c:{tb}:{dc}"])
                bank.consumed(bi, f"sp:c:{tb}:{dc}")
                ocs.append(f"sp:c:{tb}:{dc}")
                stage_free = None
            fm_free = f"sp:t:{tb}:{DC-1}"
            p.emit("sync", lambda e, tb=tb, stage=stage, out=out: e.dma_start(
                out=out[tb * 128:(tb + 1) * 128, :], in_=stage[:]),
                needs=ocs, sem=sems, dma=True, produces=[f"sp:s:{tb}"])
            stage_free = f"sp:s:{tb}"


def attn_phase(nc, p, ctx, C, a):
    with ExitStack() as actx:
        attn_phase_inner(nc, p, actx, C, a)


def attn_phase_inner(nc, p, ctx, C, a):
    T = a["T"]
    NKC = T // 128
    SBW = min(512, T // 8)
    NSB = T // SBW
    x_scr, q_scr, v_scr, cs_scr = a["x_scr"], a["q_scr"], a["v_scr"], a["cs_scr"]

    k_sb = ctx.enter_context(nc.sbuf_tensor("at_k", [128, 8 * T], BF16))
    bneg = ctx.enter_context(nc.sbuf_tensor("at_bneg", [128, 16], F32))

    with ExitStack() as ph:
        psb = lambda n, s, d=F32: ph.enter_context(nc.sbuf_tensor("atA_" + n, s, d))
        ppsA = lambda n, s: ph.enter_context(nc.psum_tensor("atA_" + n, s, F32))
        wqkvT = psb("wqkvT", [128, DC * 3 * D], BF16)
        wstage = [psb("wstage0", [128, D], BF16),
                  psb("wstage1", [128, D], BF16)]
        cs_c = [psb("cs_c0", [128, 512]), psb("cs_c1", [128, 512])]
        cs_s = [psb("cs_s0", [128, 512]), psb("cs_s1", [128, 512])]
        x1t = [psb("x1t0", [128, DC * 512]), psb("x1t1", [128, DC * 512])]
        xln2 = [psb("xln20", [128, DC * 512], BF16),
                psb("xln21", [128, DC * 512], BF16)]
        prer = [psb("prer0", [128, 512], BF16), psb("prer1", [128, 512], BF16)]
        rt1 = psb("rt1", [128, 512])
        rt2 = psb("rt2", [128, 512])
        qout = [psb("qout0", [128, 512], BF16), psb("qout1", [128, 512], BF16)]
        vstage = [psb("vst0", [128, NH * 65], BF16),
                  psb("vst1", [128, NH * 65], BF16)]
        qsq = psb("qsq", [128, 512], BF16)
        nrmq = psb("nrmq", [2, 8])
        nrmk = psb("nrmk", [2, 8])
        nscr = psb("nscr", [2, 1])
        brow = psb("brow", [1, 16])
        bmx = psb("bmx", [2, 8])
        bmxb = psb("bmxb", [2, 8], BF16)
        brb = psb("brb", [1, 16], BF16)
        xbf = [psb("xbf0", [128, 512], BF16), psb("xbf1", [128, 512], BF16)]
        xsq = [psb("xsq0", [128, 512], BF16), psb("xsq1", [128, 512], BF16)]
        row0 = psb("row0", [1, 512])
        row1 = psb("row1", [1, 512])
        hil = psb("hil", [1, 2048], BF16)
        tvec = psb("tvec", [128, 512])
        uvec = psb("uvec", [128, 512])
        ln_scr = (xbf, xsq, row0, row1, hil, tvec, uvec)

        pq = Bank([ppsA("pq0", [128, 512]), ppsA("pq1", [128, 512])])
        pw = Bank([ppsA("pw0", [128, 512]), ppsA("pw1", [128, 512])])
        pn = Bank([ppsA("pn", [128, 512])])
        pm = Bank([ppsA("pm", [128, 512])])

        wsem = p.addsem("d_aw", dma=True)
        xsem = p.addsem("d_ax", dma=True)
        qsem = [p.addsem("d_aq0", dma=True), p.addsem("d_aq1", dma=True)]
        vsem = [p.addsem("d_av0", dma=True), p.addsem("d_av1", dma=True)]
        csem = [p.addsem("d_ac0", dma=True), p.addsem("d_ac1", dma=True)]
        csem_s = [p.addsem("d_as0", dma=True), p.addsem("d_as1", dma=True)]

        csfree2 = [None, None]

        wstage_free = [None, None]
        for ec in range(3 * D // 128):
            wsl = ec % 2
            p.emit("gpsimd", lambda e, ec=ec, wsl=wsl, wstage=wstage,
                   a=a: e.dma_start(
                       out=wstage[wsl][:],
                       in_=a["w_qkv"][ec * 128:(ec + 1) * 128, :]),
                needs=[wstage_free[wsl]], sem=wsem, dma=True,
                produces=[f"at:ws:{ec}"])
            for dc in range(DC):
                bi, pst, gg = pq.acquire()
                p.emit("tensor", lambda e, dc=dc, wsl=wsl, pst=pst,
                       wstage=wstage, C=C: e.transpose(
                           pst[:, 0:64].bitcast(BF16),
                           wstage[wsl][:, dc * 128:(dc + 1) * 128],
                           C["ident_b"][:]),
                       needs=[f"at:ws:{ec}", "identb"] + gg,
                       produces=[f"at:wt:{ec}:{dc}"])
                p.emit("vector", lambda e, dc=dc, ec=ec, pst=pst,
                       wqkvT=wqkvT: e.tensor_copy(
                           wqkvT[:, dc * 3 * D + ec * 128:
                                 dc * 3 * D + (ec + 1) * 128],
                           pst[:, 0:64].bitcast(BF16)),
                       needs=[f"at:wt:{ec}:{dc}"],
                       produces=[f"at:wc:{ec}:{dc}"])
                pq.consumed(bi, f"at:wc:{ec}:{dc}")
            wstage_free[wsl] = f"at:wt:{ec}:{DC-1}"
        wq_done = [f"at:wc:{3*D//128-1}:{dc}" for dc in range(DC)] + \
                  [f"at:wc:{ec}:{DC-1}" for ec in range(3 * D // 128)]

        p.emit("vector", lambda e, nrmq=nrmq: e.memset(nrmq[:], -3.0e38),
               produces=["at:nq0"])
        p.emit("vector", lambda e, nrmk=nrmk: e.memset(nrmk[:], -3.0e38),
               produces=["at:nk0"])

        x1_free = [None, None]
        prfree = [None, None]
        prfree_t = [None, None]
        qofree = [None, None]
        vsfree = [None, None]
        qsq_guard = None
        NTCH = T // 512
        ln_next = {}

        def emit_ln_block(tcbn):
            t0n = tcbn * 512
            tsln = tcbn % 2
            p.emit("sync", lambda e, t0n=t0n, tsln=tsln, x1t=x1t,
                   x_scr=x_scr: e.dma_start(
                       out=x1t[tsln][:].rearrange("p (c t) -> p c t", t=512),
                       in_=x_scr[:, :, t0n:t0n + 512].transpose([1, 0, 2])),
                needs=[x1_free[tsln]], sem=xsem, dma=True,
                produces=[f"at:x1:{tcbn}"])
            p.emit("sync", lambda e, t0n=t0n, tsln=tsln, cs_c=cs_c,
                   cs_scr=cs_scr: e.dma_start(
                       out=cs_c[tsln][:], in_=cs_scr[0, :, t0n:t0n + 512]),
                   needs=[csfree2[tsln]], sem=csem[tsln], dma=True,
                   produces=[f"at:csc:{tcbn}"])
            p.emit("sync", lambda e, t0n=t0n, tsln=tsln, cs_s=cs_s,
                   cs_scr=cs_scr: e.dma_start(
                       out=cs_s[tsln][:], in_=cs_scr[1, :, t0n:t0n + 512]),
                   needs=[csfree2[tsln]], sem=csem_s[tsln], dma=True,
                   produces=[f"at:css:{tcbn}"])
            done_n = emit_ln(
                p, C, f"at:ln:{tcbn}", 512, x1t[tsln],
                lambda dc: slice(dc * 512, (dc + 1) * 512),
                xln2[tsln], lambda dc: slice(dc * 512, (dc + 1) * 512),
                ln_scr, (pm, pn), [f"at:x1:{tcbn}"], False,
                xln_guard=(f"at:v:{tcbn-2}:3:1" if tcbn >= 2 else None))
            x1_free[tsln] = done_n[-1]
            ln_next[tcbn] = done_n

        emit_ln_block(0)
        for tcb in range(NTCH):
            t0 = tcb * 512
            tsl = tcb % 2
            csl2 = tcb % 2
            ln_done = ln_next.pop(tcb)

            def emit_tail(ec, tcb=tcb, t0=t0, csl2=csl2):
                nonlocal qsq_guard
                psl = ec % 2
                wi2, psw, wgg = pw.acquire()
                p.emit("tensor", lambda e, psl=psl, psw=psw, prer=prer,
                       C=C: e.matmul(
                           psw[:, 0:512], C["m2swap"][:], prer[psl][:],
                           start=True, stop=True),
                       needs=[f"at:pr:{tcb}:{ec}", "m2swap"] + wgg,
                       produces=[f"at:sw:{tcb}:{ec}"])
                p.emit("vector", lambda e, psl=psl, csl2=csl2, rt1=rt1,
                       prer=prer, cs_c=cs_c: e.tensor_tensor(
                           out=rt1[:], in0=prer[psl][:],
                           in1=cs_c[csl2][:], op=ALU.mult),
                       needs=[f"at:pr:{tcb}:{ec}", f"at:csc:{tcb}"],
                       produces=[f"at:r1:{tcb}:{ec}"])
                p.emit("vector", lambda e, psw=psw, csl2=csl2, rt2=rt2,
                       cs_s=cs_s: e.tensor_tensor(
                           out=rt2[:], in0=psw[:, 0:512],
                           in1=cs_s[csl2][:], op=ALU.mult),
                       needs=[f"at:sw:{tcb}:{ec}", f"at:css:{tcb}"],
                       produces=[f"at:r2:{tcb}:{ec}"])
                pw.consumed(wi2, f"at:r2:{tcb}:{ec}")
                prfree[psl] = f"at:r1:{tcb}:{ec}"
                prfree_t[psl] = f"at:sw:{tcb}:{ec}"
                if ec < 8:
                    qsl = ec % 2
                    p.emit("vector", lambda e, qsl=qsl, rt1=rt1, rt2=rt2,
                           qout=qout: e.tensor_tensor(
                               out=qout[qsl][:], in0=rt1[:], in1=rt2[:],
                               op=ALU.add),
                           needs=[f"at:r1:{tcb}:{ec}", f"at:r2:{tcb}:{ec}",
                                  qofree[qsl]],
                           produces=[f"at:q:{tcb}:{ec}"])
                    p.emit("sync", lambda e, ec=ec, t0=t0, qsl=qsl, qout=qout,
                           q_scr=q_scr: e.dma_start(
                               out=q_scr[ec, :, t0:t0 + 512],
                               in_=qout[qsl][:]),
                           needs=[f"at:q:{tcb}:{ec}"], sem=qsem[qsl], dma=True,
                           produces=[f"at:qst:{tcb}:{ec}"])
                    qofree[qsl] = f"at:qst:{tcb}:{ec}"
                    rres = f"at:q:{tcb}:{ec}"
                    src_ap = (lambda qsl=qsl, qout=qout: qout[qsl][:])
                else:
                    kc = ec - 8
                    p.emit("vector", lambda e, kc=kc, t0=t0, rt1=rt1, rt2=rt2,
                           k_sb=k_sb, T=T: e.tensor_tensor(
                               out=k_sb[:, kc * T + t0: kc * T + t0 + 512],
                               in0=rt1[:], in1=rt2[:], op=ALU.add),
                           needs=[f"at:r1:{tcb}:{ec}", f"at:r2:{tcb}:{ec}"],
                           produces=[f"at:kk:{tcb}:{kc}"])
                    rres = f"at:kk:{tcb}:{kc}"

            def emit_norm(ec, tcb=tcb, t0=t0):
                nonlocal qsq_guard
                if ec < 8:
                    qsl = ec % 2
                    rres = f"at:q:{tcb}:{ec}"
                    src_ap = (lambda qsl=qsl, qout=qout: qout[qsl][:])
                else:
                    kc = ec - 8
                    rres = f"at:kk:{tcb}:{kc}"
                    src_ap = (lambda kc=kc, t0=t0, k_sb=k_sb, T=T:
                              k_sb[:, kc * T + t0: kc * T + t0 + 512])
                p.emit("vector", lambda e, src_ap=src_ap, qsq=qsq: e.tensor_tensor(
                    out=qsq[:], in0=src_ap(), in1=src_ap(), op=ALU.mult),
                    needs=[rres, qsq_guard], produces=[f"at:sq:{tcb}:{ec}"])
                ni, psn, ngg = pn.acquire()
                p.emit("tensor", lambda e, psn=psn, qsq=qsq, C=C: e.matmul(
                    psn[0:2, 0:512], C["halfsel"][:], qsq[:], start=True,
                    stop=True),
                    needs=[f"at:sq:{tcb}:{ec}", "halfsel"] + ngg,
                    produces=[f"at:nm:{tcb}:{ec}"])
                qsq_guard = f"at:nm:{tcb}:{ec}"
                p.emit("vector", lambda e, psn=psn, nscr=nscr: e.tensor_reduce(
                    out=nscr[:], in_=psn[0:2, 0:512], axis=AXX, op=ALU.max),
                    needs=[f"at:nm:{tcb}:{ec}"], produces=[f"at:nr:{tcb}:{ec}"])
                pn.consumed(ni, f"at:nr:{tcb}:{ec}")
                tgt, col = (nrmq, ec) if ec < 8 else (nrmk, ec - 8)
                p.emit("vector", lambda e, tgt=tgt, col=col,
                       nscr=nscr: e.tensor_tensor(
                           out=tgt[:, col:col + 1], in0=tgt[:, col:col + 1],
                           in1=nscr[:], op=ALU.max),
                       needs=[f"at:nr:{tcb}:{ec}", "at:nq0", "at:nk0"],
                       produces=[f"at:nx:{tcb}:{ec}"])

            for ec in range(16):
                bi, psq, gg = pq.acquire()
                for dc in range(DC):
                    p.emit("tensor", lambda e, ec=ec, dc=dc, tsl=tsl, psq=psq,
                           wqkvT=wqkvT, xln2=xln2: e.matmul(
                               psq[:, 0:512],
                               wqkvT[:, dc * 3 * D + ec * 128:
                                     dc * 3 * D + (ec + 1) * 128],
                               xln2[tsl][:, dc * 512:(dc + 1) * 512],
                               start=(dc == 0), stop=(dc == DC - 1)),
                           needs=(wq_done + ln_done + gg) if dc == 0 else [],
                           produces=[f"at:qk:{tcb}:{ec}"]
                           if dc == DC - 1 else ())
                psl = ec % 2
                p.emit("scalar", lambda e, psl=psl, psq=psq,
                       prer=prer: e.activation(
                           out=prer[psl][:], in_=psq[:, 0:512], func=AF.Copy,
                           bias=0.0, scale=1.0),
                       needs=[f"at:qk:{tcb}:{ec}", prfree[psl], prfree_t[psl]],
                       produces=[f"at:pr:{tcb}:{ec}"])
                pq.consumed(bi, f"at:pr:{tcb}:{ec}")
                if ec > 0:
                    emit_tail(ec - 1)
                if ec > 1:
                    emit_norm(ec - 2)
            emit_tail(15)
            emit_norm(14)
            emit_norm(15)
            if tcb + 1 < NTCH:
                emit_ln_block(tcb + 1)

            csfree2[csl2] = f"at:r2:{tcb}:{15}"
            for tb in range(4):
                vsl = tb % 2
                evdone = []
                for ev in range(2):
                    bi, psv, gg = pq.acquire()
                    for dc in range(DC):
                        p.emit("tensor", lambda e, tb=tb, ev=ev, dc=dc,
                               tsl=tsl, psv=psv, xln2=xln2,
                               wqkvT=wqkvT: e.matmul(
                                   psv[:, 0:512],
                                   xln2[tsl][:, dc * 512 + tb * 128:
                                             dc * 512 + (tb + 1) * 128],
                                   wqkvT[:, dc * 3 * D + 2 * D + ev * 512:
                                         dc * 3 * D + 2 * D + (ev + 1) * 512],
                                   start=(dc == 0), stop=(dc == DC - 1)),
                               needs=(wq_done + ln_done + gg)
                               if dc == 0 else [],
                               produces=[f"at:v:{tcb}:{tb}:{ev}"]
                               if dc == DC - 1 else ())
                    p.emit("vector", lambda e, vsl=vsl, ev=ev, psv=psv,
                           vstage=vstage: e.tensor_copy(
                               vstage[vsl][:, :].rearrange(
                                   "p (h c) -> p h c", c=65)
                               [:, ev * 8:(ev + 1) * 8, 0:64],
                               psv[:, 0:512]),
                           needs=[f"at:v:{tcb}:{tb}:{ev}", vsfree[vsl]],
                           produces=[f"at:ve:{tcb}:{tb}:{ev}"])
                    pq.consumed(bi, f"at:ve:{tcb}:{tb}:{ev}")
                    evdone.append(f"at:ve:{tcb}:{tb}:{ev}")
                p.emit("vector", lambda e, vsl=vsl, vstage=vstage: e.memset(
                    vstage[vsl][:, :].rearrange("p (h c) -> p h c", c=65)
                    [:, :, 64:65], 1.0),
                    needs=evdone, produces=[f"at:vo:{tcb}:{tb}"])
                p.emit("sync", lambda e, tcb=tcb, tb=tb, vsl=vsl,
                       vstage=vstage, v_scr=v_scr: e.dma_start(
                           out=v_scr[tcb * 4 + tb], in_=vstage[vsl][:]),
                       needs=[f"at:vo:{tcb}:{tb}"], sem=vsem[vsl], dma=True,
                       produces=[f"at:vst:{tcb}:{tb}"])
                vsfree[vsl] = f"at:vst:{tcb}:{tb}"

        nall = [f"at:nx:{NTCH-1}:{ec}" for ec in range(16)]
        p.emit("vector", lambda e, bmx=bmx, nrmq=nrmq, nrmk=nrmk: e.tensor_tensor(
            out=bmx[:], in0=nrmq[:], in1=nrmk[:], op=ALU.mult),
            needs=nall, produces=["at:bp"])
        p.emit("scalar", lambda e, bmx=bmx: e.activation(
            out=bmx[:], in_=bmx[:], func=AF.Ln, bias=0.0, scale=1.0),
            needs=["at:bp"], produces=["at:bl"])
        p.emit("scalar", lambda e, bmx=bmx: e.activation(
            out=bmx[:], in_=bmx[:], func=AF.Exp, bias=0.0, scale=0.5),
            needs=["at:bl"], produces=["at:be"])
        p.emit("scalar", lambda e, bmx=bmx: e.activation(
            out=bmx[:], in_=bmx[:], func=AF.Copy, bias=0.0, scale=-ASCALE),
            needs=["at:be"], produces=["at:bn"])
        p.emit("vector", lambda e, bmxb=bmxb, bmx=bmx: e.tensor_copy(
            bmxb[:], bmx[:]), needs=["at:bn"], produces=["at:bnb"])
        bi, psB, gg = pm.acquire()
        for h in range(16):
            p.emit("tensor", lambda e, h=h, psB=psB, C=C, bmxb=bmxb: e.matmul(
                psB[0:1, h:h + 1], C["ident_b"][0:2, (h % 2):(h % 2) + 1],
                bmxb[0:2, (h // 2):(h // 2) + 1], start=True, stop=True),
                needs=(["at:bnb", "identb"] + gg) if h == 0 else [],
                produces=[f"at:b1:{h}"] if h == 15 else ())
        p.emit("vector", lambda e, psB=psB, brow=brow: e.tensor_copy(
            brow[0:1, 0:16], psB[0:1, 0:16]),
            needs=["at:b1:15"], produces=["at:brow"])
        p.emit("vector", lambda e, brb=brb, brow=brow: e.tensor_copy(
            brb[:], brow[0:1, 0:16]), needs=["at:brow"], produces=["at:brb"])
        bi2, psB2, gg2 = pm.acquire()
        p.emit("tensor", lambda e, psB2=psB2, C=C, brb=brb: e.matmul(
            psB2[:, 0:16], C["ones_b"][0:1, :], brb[0:1, 0:16],
            start=True, stop=True),
            needs=["at:brb", "onesb"] + gg2, produces=["at:bbc"])
        p.emit("vector", lambda e, psB2=psB2, bneg=bneg: e.tensor_copy(
            bneg[:], psB2[:, 0:16]), needs=["at:bbc"], produces=["at:bneg"])
        pm.consumed(bi, "at:brow")
        pm.consumed(bi2, "at:bneg")

    p.barrier()

    with ExitStack() as ph:
        psb = lambda n, s, d=F32: ph.enter_context(nc.sbuf_tensor("atB_" + n, s, d))
        v_aug = psb("vaug", [128, NKC * NH * 65], BF16)
        woutT = psb("woutT", [64, NH * DC * 128], BF16)
        qblk = [psb("qblk0", [128, 8 * SBW], BF16)]
        wstage = qblk[0] if 8 * SBW >= D else psb("wstage", [128, D], BF16)
        attnT = [psb(f"attnT{i}", [128, 2 * SBW], BF16) for i in range(3)]
        ofm = psb("ofm", [64, NH * SBW], BF16)
        rbc_sb = [psb("rbc0", [128, SBW])]
        dpad = rbc_sb[0]
        dhi = psb("dhi", [65, SBW], BF16)
        dlo = psb("dlo", [65, SBW], BF16)
        x1p = [psb("x1p0", [128, SBW])]
        x2p = [psb("x2p0", [128, SBW])]

        psS = [ph.enter_context(nc.psum_tensor("atB_scA", [128, 1024], F32)),
               ph.enter_context(nc.psum_tensor("atB_scB", [128, 1024], F32)),
               ph.enter_context(nc.psum_tensor("atB_scC", [128, 1024], F32))]
        psO = [ph.enter_context(nc.psum_tensor("atB_oA", [65, 512], F32)),
               ph.enter_context(nc.psum_tensor("atB_oB", [65, 512], F32))]

        vsem2 = p.addsem("d_bv", dma=True)
        qsem2 = [p.addsem("d_bq0", dma=True), p.addsem("d_bq1", dma=True)]
        wsem2 = p.addsem("d_bw", dma=True)
        xsem2 = [p.addsem("d_bx0", dma=True), p.addsem("d_bx1", dma=True)]
        osem2 = [p.addsem("d_bo0", dma=True), p.addsem("d_bo1", dma=True)]

        sc_bank = Bank(psS)
        p.emit("sync", lambda e, v_aug=v_aug, v_scr=v_scr: e.dma_start(
            out=v_aug[:].rearrange("p (b c) -> p b c", c=NH * 65),
            in_=v_scr.transpose([1, 0, 2])),
            sem=vsem2, dma=True, produces=["bt:vaug"])

        wst_free = None
        for dc in range(DC):
            p.emit("gpsimd", lambda e, dc=dc, wstage=wstage, a=a: e.dma_start(
                out=wstage[:, 0:D], in_=a["w_out"][dc * 128:(dc + 1) * 128, :]),
                needs=[wst_free], sem=wsem2, dma=True,
                produces=[f"bt:ws:{dc}"])
            for h in range(NH):
                bi, psw, gg = sc_bank.acquire()
                p.emit("tensor", lambda e, h=h, psw=psw, wstage=wstage,
                       C=C: e.transpose(
                           psw[0:64, 0:64].bitcast(BF16),
                           wstage[:, h * 64:(h + 1) * 64], C["ident_b"][:]),
                       needs=[f"bt:ws:{dc}", "identb"] + gg,
                       produces=[f"bt:wt:{h}:{dc}"])
                p.emit("vector", lambda e, h=h, dc=dc, psw=psw,
                       woutT=woutT: e.tensor_copy(
                           woutT[:, (h * DC + dc) * 128:
                                 (h * DC + dc + 1) * 128],
                           psw[0:64, 0:64].bitcast(BF16)),
                       needs=[f"bt:wt:{h}:{dc}"], produces=[f"bt:wc:{h}:{dc}"])
                sc_bank.consumed(bi, f"bt:wc:{h}:{dc}")
            wst_free = f"bt:wt:{NH-1}:{dc}"
        wout_done = [f"bt:wc:{h}:{DC-1}" for h in range(NH)] + \
                    [f"bt:wc:{NH-1}:{dc}" for dc in range(DC)]

        batches = []
        kk = 0
        while kk < NKC:
            n = min(2, NKC - kk)
            batches.append((kk, n))
            kk += n

        o_bank = Bank(psO)
        at_bank = Bank(attnT)
        rbc_bank = Bank(rbc_sb)
        qb_free = [f"bt:wt:{NH-1}:{DC-1}", None]
        x1_free2 = [None, None]
        x2_free = [None, None]
        ofree = {}
        hold = {"dfree": None}
        pending_norm = None

        for sbi in range(NSB):
            s0 = sbi * SBW
            qsl = 0
            p.emit("sync", lambda e, s0=s0, qsl=qsl, qblk=qblk, q_scr=q_scr,
                   SBW=SBW: e.dma_start(
                       out=qblk[qsl][:].rearrange("p (c t) -> p c t", t=SBW),
                       in_=q_scr[:, :, s0:s0 + SBW].transpose([1, 0, 2])),
                   needs=[qb_free[qsl]], sem=qsem2[qsl], dma=True,
                   produces=[f"bt:qb:{sbi}"])
            head_done = []
            for hp in range(8):
                heads = (2 * hp, 2 * hp + 1)
                osl = {}

                def emit_vms(pend, side, sbi=sbi):
                    k0, nb, atb, ai = pend
                    oi, pO, ogg = osl[side]
                    h = heads[side]
                    for j in range(nb):
                        tch = k0 + j
                        lastm = (k0 + nb == NKC and j == nb - 1)
                        p.emit("tensor", lambda e, tch=tch, h=h, j=j, pO=pO,
                               atb=atb, v_aug=v_aug,
                               fi=(k0 == 0 and j == 0), lastm=lastm,
                               SBW=SBW: e.matmul(
                                   pO[:, 0:SBW],
                                   v_aug[:, (tch * NH + h) * 65:
                                         (tch * NH + h) * 65 + 65],
                                   atb[:, j * SBW:(j + 1) * SBW],
                                   start=fi, stop=lastm),
                               needs=([f"bt:ex:{sbi}:{h}:{k0}", "bt:vaug"] +
                                      (ogg if k0 == 0 else []))
                               if j == 0 else [],
                               produces=[f"bt:vm:{sbi}:{h}" if lastm else
                                         f"bt:vmb:{sbi}:{h}:{k0}"]
                               if j == nb - 1 else ())
                    at_bank.consumed(
                        ai, f"bt:vm:{sbi}:{h}" if (k0 + nb == NKC)
                        else f"bt:vmb:{sbi}:{h}:{k0}")

                pending = {0: None, 1: None}
                for k_idx, (k0, nb) in enumerate(batches):
                    psinfo = {0: sc_bank.acquire(), 1: sc_bank.acquire()}
                    for j in range(nb):
                        for side in (0, 1):
                            h = heads[side]
                            kc = h // 2
                            roff = (h % 2) * 64
                            si, pS, sgg = psinfo[side]
                            p.emit("tensor", lambda e, kc=kc, roff=roff,
                                   tch=k0 + j, j=j, pS=pS, qsl=qsl, qblk=qblk,
                                   k_sb=k_sb, T=T, SBW=SBW: e.matmul(
                                       pS[:, j * SBW:(j + 1) * SBW],
                                       k_sb[roff:roff + 64,
                                            kc * T + tch * 128:
                                            kc * T + (tch + 1) * 128],
                                       qblk[qsl][roff:roff + 64,
                                                 kc * SBW:(kc + 1) * SBW],
                                       start=True, stop=True,
                                       tile_position=(roff, 0)),
                                   needs=([f"bt:qb:{sbi}"] + sgg)
                                   if j == 0 else [],
                                   produces=[f"bt:sc:{sbi}:{h}:{k0}:{j}"])
                    for side in (0, 1):
                        h = heads[side]
                        si, pS, _sg = psinfo[side]
                        ai, atb, agg = at_bank.acquire()
                        p.emit("scalar", lambda e, nb=nb, pS=pS, atb=atb, h=h,
                               bneg=bneg, SBW=SBW: e.activation(
                                   out=atb[:, 0:nb * SBW],
                                   in_=pS[:, 0:nb * SBW],
                                   func=AF.Exp, bias=bneg[:, h:h + 1],
                                   scale=ASCALE),
                               needs=[f"bt:sc:{sbi}:{h}:{k0}:{j}"
                                      for j in range(nb)] + ["at:bneg"] + agg,
                               produces=[f"bt:ex:{sbi}:{h}:{k0}"])
                        sc_bank.consumed(si, f"bt:ex:{sbi}:{h}:{k0}")
                        if pending[side] is not None:
                            emit_vms(pending[side], side)
                        pending[side] = (k0, nb, atb, ai)
                    if k_idx == 0:
                        if pending_norm is not None:
                            pending_norm()
                            pending_norm = None
                        for side in (0, 1):
                            osl[side] = o_bank.acquire()
                for side in (0, 1):
                    emit_vms(pending[side], side)

                def make_norm(heads=heads, osl=osl, sbi=sbi):
                    def norm():
                        for side in (0, 1):
                            h = heads[side]
                            oi, pO, ogg = osl[side]
                            p.emit("vector", lambda e, pO=pO, dpad=dpad,
                                   SBW=SBW: e.reciprocal(
                                       dpad[64:65, 0:SBW], pO[64:65, 0:SBW]),
                                   needs=[f"bt:vm:{sbi}:{h}", hold["dfree"]],
                                   produces=[f"bt:rcp:{sbi}:{h}"])
                            p.emit("vector", lambda e, dpad=dpad, dhi=dhi,
                                   SBW=SBW: e.tensor_copy(
                                       dhi[64:65, 0:SBW], dpad[64:65, 0:SBW]),
                                   needs=[f"bt:rcp:{sbi}:{h}"],
                                   produces=[f"bt:rhi:{sbi}:{h}"])
                            p.emit("vector", lambda e, dpad=dpad, dhi=dhi,
                                   dlo=dlo, SBW=SBW: e.tensor_tensor(
                                       out=dlo[64:65, 0:SBW],
                                       in0=dpad[64:65, 0:SBW],
                                       in1=dhi[64:65, 0:SBW],
                                       op=ALU.subtract),
                                   needs=[f"bt:rhi:{sbi}:{h}"],
                                   produces=[f"bt:rlo:{sbi}:{h}"])
                            ci, pC, cgg = sc_bank.acquire()
                            p.emit("tensor", lambda e, pC=pC, dhi=dhi, C=C,
                                   SBW=SBW: e.matmul(
                                       pC[:, 0:SBW], C["ones_b"][64:65, :],
                                       dhi[64:65, 0:SBW], start=True,
                                       stop=False, tile_position=(64, 0)),
                                   needs=[f"bt:rhi:{sbi}:{h}", "onesb"] + cgg,
                                   produces=())
                            p.emit("tensor", lambda e, pC=pC, dlo=dlo, C=C,
                                   SBW=SBW: e.matmul(
                                       pC[:, 0:SBW], C["ones_b"][64:65, :],
                                       dlo[64:65, 0:SBW], start=False,
                                       stop=True, tile_position=(64, 0)),
                                   needs=[f"bt:rlo:{sbi}:{h}"],
                                   produces=[f"bt:rbm:{sbi}:{h}"])
                            hold["dfree"] = f"bt:rbm:{sbi}:{h}"
                            ri, rb, rgg = rbc_bank.acquire()
                            p.emit("vector", lambda e, rb=rb, pC=pC,
                                   SBW=SBW: e.tensor_copy(
                                       rb[:, 0:SBW], pC[:, 0:SBW]),
                                   needs=[f"bt:rbm:{sbi}:{h}"] + rgg,
                                   produces=[f"bt:rbc:{sbi}:{h}"])
                            sc_bank.consumed(ci, f"bt:rbc:{sbi}:{h}")
                            p.emit("vector", lambda e, h=h, rb=rb, pO=pO,
                                   ofm=ofm, SBW=SBW: e.tensor_tensor(
                                       out=ofm[:, h * SBW:(h + 1) * SBW],
                                       in0=pO[0:64, 0:SBW],
                                       in1=rb[0:64, 0:SBW], op=ALU.mult),
                                   needs=[f"bt:rbc:{sbi}:{h}", ofree.get(h)],
                                   produces=[f"bt:o:{sbi}:{h}"])
                            rbc_bank.consumed(ri, f"bt:o:{sbi}:{h}")
                            o_bank.consumed(oi, f"bt:o:{sbi}:{h}")
                            head_done.append(f"bt:o:{sbi}:{h}")
                    return norm

                pending_norm = make_norm()
            if pending_norm is not None:
                pending_norm()
                pending_norm = None
            qb_free[qsl] = f"bt:sc:{sbi}:{NH-1}:{batches[-1][0]}:{batches[-1][1]-1}"

            for dc in range(DC):
                pi, pP, pgg = sc_bank.acquire()
                for h in range(NH):
                    p.emit("tensor", lambda e, h=h, dc=dc, pP=pP, woutT=woutT,
                           ofm=ofm, SBW=SBW: e.matmul(
                               pP[:, 0:SBW],
                               woutT[:, (h * DC + dc) * 128:
                                     (h * DC + dc + 1) * 128],
                               ofm[:, h * SBW:(h + 1) * SBW],
                               start=(h == 0), stop=(h == NH - 1)),
                           needs=(head_done + wout_done + pgg)
                           if h == 0 else [],
                           produces=[f"bt:pj:{sbi}:{dc}"]
                           if h == NH - 1 else ())
                x1s = 0
                p.emit("sync", lambda e, dc=dc, s0=s0, x1s=x1s, x1p=x1p,
                       x_scr=x_scr, SBW=SBW: e.dma_start(
                           out=x1p[x1s][:, 0:SBW],
                           in_=x_scr[dc, :, s0:s0 + SBW]),
                       needs=[x1_free2[x1s]], sem=xsem2[x1s], dma=True,
                       produces=[f"bt:x1:{sbi}:{dc}"])
                x2s = 0
                p.emit("vector", lambda e, dc=dc, pP=pP, x1s=x1s, x2s=x2s,
                       x1p=x1p, x2p=x2p, C=C, SBW=SBW: e.scalar_tensor_tensor(
                           out=x2p[x2s][:, 0:SBW], in0=pP[:, 0:SBW],
                           scalar=C["boutt"][:, dc:dc + 1],
                           in1=x1p[x1s][:, 0:SBW], op0=ALU.add, op1=ALU.add),
                       needs=[f"bt:pj:{sbi}:{dc}", "boutt",
                              f"bt:x1:{sbi}:{dc}", x2_free[x2s]],
                       produces=[f"bt:x2:{sbi}:{dc}"])
                sc_bank.consumed(pi, f"bt:x2:{sbi}:{dc}")
                x1_free2[x1s] = f"bt:x2:{sbi}:{dc}"
                p.emit("sync", lambda e, dc=dc, s0=s0, x2s=x2s, x2p=x2p,
                       x_scr=x_scr, SBW=SBW: e.dma_start(
                           out=x_scr[dc, :, s0:s0 + SBW],
                           in_=x2p[x2s][:, 0:SBW]),
                       needs=[f"bt:x2:{sbi}:{dc}"], sem=osem2[x2s], dma=True,
                       produces=[f"bt:xst:{sbi}:{dc}"])
                x2_free[x2s] = f"bt:xst:{sbi}:{dc}"
            for h in range(NH):
                ofree[h] = f"bt:pj:{sbi}:{DC-1}"


def kernel(L0_attn_x, L0_attn_w_qkv, L0_attn_w_out, L0_attn_b_out,
           L0_ffn_Wr, L0_ffn_br, L0_ffn_W1, L0_ffn_b1, L0_ffn_W2, L0_ffn_b2):
    x = np.ascontiguousarray(np.asarray(L0_attn_x, dtype=np.float32))
    B, S, _ = x.shape
    E = int(np.asarray(L0_ffn_Wr).shape[1])
    nc = build_program(T=S, E=E)
    base = dict(
        w_qkv=np.ascontiguousarray(np.asarray(L0_attn_w_qkv, dtype=np.float32)),
        w_out=np.ascontiguousarray(np.asarray(L0_attn_w_out, dtype=np.float32)),
        b_out=np.ascontiguousarray(np.asarray(L0_attn_b_out, dtype=np.float32)),
        Wr=np.ascontiguousarray(np.asarray(L0_ffn_Wr, dtype=np.float32)),
        br=np.ascontiguousarray(np.asarray(L0_ffn_br, dtype=np.float32)),
        W1=np.ascontiguousarray(np.asarray(L0_ffn_W1, dtype=np.float32)),
        b1=np.ascontiguousarray(np.asarray(L0_ffn_b1, dtype=np.float32)),
        W2=np.ascontiguousarray(np.asarray(L0_ffn_W2, dtype=np.float32)),
        b2=np.ascontiguousarray(np.asarray(L0_ffn_b2, dtype=np.float32)),
    )
    in_maps = [dict(base, x=np.ascontiguousarray(x[c])) for c in range(B)]
    res = run_bass_kernel_spmd(nc, in_maps, core_ids=list(range(B)))
    return np.stack([res.results[c]["out"] for c in range(B)], axis=0)


# version 19
# speedup vs baseline: 1.2853x; 1.2853x over previous
import math
import numpy as np
from contextlib import ExitStack

import concourse.bass as bass
import concourse.mybir as mybir
from concourse.bass_utils import run_bass_kernel_spmd

F32 = mybir.dt.float32
BF16 = mybir.dt.bfloat16
AF = mybir.ActivationFunctionType
ALU = mybir.AluOpType
AXX = mybir.AxisListType.X

D = 1024
DC = 8
HFF = 2048
HC = 16
NH = 16
HD = 64
ASCALE = HD ** (-0.5)
EPS = 1e-5
TWO_PI = 2.0 * math.pi


class P:
    ENG = ("sync", "gpsimd", "tensor", "scalar", "vector")

    def __init__(self, nc, ctx):
        self.nc = nc
        self.ctx = ctx
        self.streams = {e: [] for e in self.ENG}
        self.sems = {}
        self.counts = {}
        self.waited = {}
        self.res = {}
        self.last_prod = {e: None for e in self.ENG}
        self.dma_sems = []
        for e in self.ENG:
            self.addsem("s_" + e)

    def addsem(self, name, dma=False):
        if name in self.sems:
            return name
        self.sems[name] = self.ctx.enter_context(self.nc.semaphore(name))
        self.counts[name] = 0
        if dma:
            self.dma_sems.append(name)
        return name

    def emit(self, eng, fn=None, needs=(), produces=(), sem=None, dma=False):
        waits = []
        for r in needs:
            if r is None:
                continue
            s, v = self.res[r]
            if v <= 0:
                continue
            key = (eng, s)
            if self.waited.get(key, 0) < v:
                waits.append((s, v))
                self.waited[key] = v
        incs = []
        if produces:
            assert fn is not None
            s = sem if sem is not None else ("s_" + eng)
            amt = 16 if dma else 1
            self.counts[s] += amt
            v = self.counts[s]
            incs = [(s, amt)]
            for r in produces:
                self.res[r] = (s, v)
            if not dma:
                self.last_prod[eng] = produces[0]

        def closure(engine, waits=waits, incs=incs, fn=fn):
            for s, v in waits:
                engine.wait_ge(self.sems[s], v)
            if fn is not None:
                ins = fn(engine)
                for s, amt in incs:
                    ins.then_inc(self.sems[s], amt)
        self.streams[eng].append(closure)

    def mark(self, name, sem="s_sync", val=0):
        self.res[name] = (sem, val)

    def wait_raw(self, eng, sem, val):
        key = (eng, sem)
        if val > 0 and self.waited.get(key, 0) < val:
            sh = self.sems[sem]
            self.streams[eng].append(
                lambda engine, sh=sh, v=val: engine.wait_ge(sh, v))
            self.waited[key] = val

    def barrier(self):
        needs = [r for r in self.last_prod.values() if r is not None]
        for e in self.ENG:
            self.emit(e, fn=None, needs=needs)
            for s in self.dma_sems:
                self.wait_raw(e, s, self.counts[s])


class Bank:
    def __init__(self, slots):
        self.slots = list(slots)
        self.free = [None] * len(self.slots)
        self.i = 0

    def acquire(self):
        i = self.i % len(self.slots)
        self.i += 1
        g = self.free[i]
        return i, self.slots[i], ([g] if g else [])

    def consumed(self, i, res):
        self.free[i] = res


def emit_ln(p, C, tag, W, x_tile, x_cols, xln_tile, xln_cols, scr, banks,
            xq_ready, add_half, xln_guard=None):
    xbf, xsq, row0, row1, hil, tvec, uvec = scr
    pm_bank, pr_bank = banks
    im, psm, gm = pm_bank.acquire()
    ir, psr, gr = pr_bank.acquire()
    kb = f"lnscr:{id(xbf[0])}"
    xbf_free = [p.res.get(kb + ":x0") and (kb + ":x0"),
                p.res.get(kb + ":x1") and (kb + ":x1")]
    row_guard = p.res.get(kb + ":row") and (kb + ":row")
    hil_guard = p.res.get(kb + ":hil") and (kb + ":hil")
    for dc in range(DC):
        sl = dc % 2
        p.emit("vector", lambda e, dc=dc, sl=sl, xbf=xbf, W=W, x_tile=x_tile,
               xc=x_cols(dc): e.tensor_copy(xbf[sl][:, 0:W], x_tile[:, xc]),
               needs=xq_ready + ([xbf_free[sl]] if xbf_free[sl] else []),
               produces=[f"{tag}:c:{dc}"])
        p.emit("vector", lambda e, sl=sl, xbf=xbf, xsq=xsq, W=W: e.tensor_tensor(
            out=xsq[sl][:, 0:W], in0=xbf[sl][:, 0:W], in1=xbf[sl][:, 0:W],
            op=ALU.mult),
            needs=[f"{tag}:c:{dc}"], produces=[f"{tag}:sq:{dc}"])
        p.emit("tensor", lambda e, dc=dc, sl=sl, psm=psm, xbf=xbf, W=W, C=C: e.matmul(
            psm[0:1, 0:W], C["ones_b"][:, 0:1], xbf[sl][:, 0:W],
            start=(dc == 0), stop=(dc == DC - 1)),
            needs=[f"{tag}:c:{dc}", "onesb"] + (gm if dc == 0 else []),
            produces=[f"{tag}:mx:{dc}"])
        p.emit("tensor", lambda e, dc=dc, sl=sl, psm=psm, xsq=xsq, W=W, C=C: e.matmul(
            psm[32:33, 0:W], C["ones_b"][:, 0:1], xsq[sl][:, 0:W],
            start=(dc == 0), stop=(dc == DC - 1), tile_position=(0, 32)),
            needs=[f"{tag}:sq:{dc}"],
            produces=[f"{tag}:ms:{dc}"])
        xbf_free[sl] = f"{tag}:ms:{dc}"
    sums = [f"{tag}:mx:{DC-1}", f"{tag}:ms:{DC-1}"]
    p.emit("scalar", lambda e, psm=psm, W=W, row0=row0: e.activation(
        out=row0[:, 0:W], in_=psm[0:1, 0:W], func=AF.Copy, bias=0.0,
        scale=1.0 / D), needs=sums + [row_guard], produces=[f"{tag}:m"])
    p.emit("scalar", lambda e, W=W, row0=row0, row1=row1: e.activation(
        out=row1[:, 0:W], in_=row0[:, 0:W], func=AF.Square, bias=0.0, scale=1.0),
        needs=[f"{tag}:m"], produces=[f"{tag}:msq"])
    p.emit("vector", lambda e, psm=psm, W=W, row1=row1: e.scalar_tensor_tensor(
        out=row1[:, 0:W], in0=psm[32:33, 0:W], scalar=1.0 / D, in1=row1[:, 0:W],
        op0=ALU.mult, op1=ALU.subtract),
        needs=sums + [f"{tag}:msq"], produces=[f"{tag}:var"])
    p.emit("vector", lambda e, W=W, row1=row1: e.tensor_scalar(
        out=row1[:, 0:W], in0=row1[:, 0:W], scalar1=EPS, scalar2=None,
        op0=ALU.add), needs=[f"{tag}:var"], produces=[f"{tag}:vare"])
    p.emit("scalar", lambda e, W=W, row1=row1: e.activation(
        out=row1[:, 0:W], in_=row1[:, 0:W], func=AF.Ln, bias=0.0, scale=1.0),
        needs=[f"{tag}:vare"], produces=[f"{tag}:lnv"])
    p.emit("scalar", lambda e, W=W, row1=row1: e.activation(
        out=row1[:, 0:W], in_=row1[:, 0:W], func=AF.Exp, bias=0.0, scale=-0.5),
        needs=[f"{tag}:lnv"], produces=[f"{tag}:r"])
    p.emit("vector", lambda e, W=W, hil=hil, row0=row0: e.tensor_copy(
        hil[0:1, 0:W], row0[:, 0:W]), needs=[f"{tag}:m", hil_guard],
        produces=[f"{tag}:mh"])
    p.emit("vector", lambda e, W=W, hil=hil, row0=row0: e.tensor_tensor(
        out=hil[0:1, W:2 * W], in0=row0[:, 0:W], in1=hil[0:1, 0:W],
        op=ALU.subtract), needs=[f"{tag}:mh"], produces=[f"{tag}:ml"])
    p.emit("vector", lambda e, W=W, hil=hil, row1=row1: e.tensor_copy(
        hil[0:1, 2 * W:3 * W], row1[:, 0:W]), needs=[f"{tag}:r"],
        produces=[f"{tag}:rh"])
    p.emit("vector", lambda e, W=W, hil=hil, row1=row1: e.tensor_tensor(
        out=hil[0:1, 3 * W:4 * W], in0=row1[:, 0:W], in1=hil[0:1, 2 * W:3 * W],
        op=ALU.subtract), needs=[f"{tag}:rh"], produces=[f"{tag}:rl"])
    p.emit("tensor", lambda e, psm=psm, W=W, hil=hil, C=C: e.matmul(
        psm[:, 0:W], C["ones_b"][0:1, :], hil[0:1, 0:W], start=True, stop=False),
        needs=[f"{tag}:ml", f"{tag}:var", "onesb"], produces=())
    p.emit("tensor", lambda e, psm=psm, W=W, hil=hil, C=C: e.matmul(
        psm[:, 0:W], C["ones_b"][0:1, :], hil[0:1, W:2 * W], start=False,
        stop=True), needs=[], produces=[f"{tag}:mbc"])
    p.emit("tensor", lambda e, psr=psr, W=W, hil=hil, C=C: e.matmul(
        psr[:, 0:W], C["ones_b"][0:1, :], hil[0:1, 2 * W:3 * W], start=True,
        stop=False), needs=[f"{tag}:rl", "onesb"] + gr, produces=())
    p.emit("tensor", lambda e, psr=psr, W=W, hil=hil, C=C: e.matmul(
        psr[:, 0:W], C["ones_b"][0:1, :], hil[0:1, 3 * W:4 * W], start=False,
        stop=True), needs=[], produces=[f"{tag}:rbc"])
    done = []
    for dc in range(DC):
        p.emit("vector", lambda e, psm=psm, W=W, x_tile=x_tile, tvec=tvec,
               xc=x_cols(dc): e.tensor_tensor(
                   out=tvec[:, 0:W], in0=x_tile[:, xc], in1=psm[:, 0:W],
                   op=ALU.subtract),
               needs=[f"{tag}:mbc"] + xq_ready, produces=[f"{tag}:tv:{dc}"])
        p.emit("vector", lambda e, psr=psr, W=W, tvec=tvec, uvec=uvec: e.tensor_tensor(
            out=uvec[:, 0:W], in0=tvec[:, 0:W], in1=psr[:, 0:W], op=ALU.mult),
            needs=[f"{tag}:tv:{dc}", f"{tag}:rbc"], produces=[f"{tag}:uv:{dc}"])
        p.emit("vector", lambda e, W=W, uvec=uvec, xln_tile=xln_tile,
               lc=xln_cols(dc): e.tensor_copy(xln_tile[:, lc], uvec[:, 0:W]),
               needs=[f"{tag}:uv:{dc}"] + ([xln_guard] if xln_guard else []),
               produces=[f"{tag}:xln:{dc}"])
        done.append(f"{tag}:xln:{dc}")
        if add_half:
            p.emit("vector", lambda e, W=W, uvec=uvec, x_tile=x_tile,
                   xc=x_cols(dc): e.scalar_tensor_tensor(
                       out=x_tile[:, xc], in0=uvec[:, 0:W], scalar=0.5,
                       in1=x_tile[:, xc], op0=ALU.mult, op1=ALU.add),
                   needs=[f"{tag}:uv:{dc}"], produces=[f"{tag}:xup:{dc}"])
            done.append(f"{tag}:xup:{dc}")
    pm_bank.consumed(im, f"{tag}:tv:{DC-1}")
    pr_bank.consumed(ir, f"{tag}:uv:{DC-1}")
    p.res[kb + ":x0"] = p.res[f"{tag}:ms:{DC-2}"]
    p.res[kb + ":x1"] = p.res[f"{tag}:ms:{DC-1}"]
    p.res[kb + ":row"] = p.res[f"{tag}:rl"]
    p.res[kb + ":hil"] = p.res[f"{tag}:rbc"]
    return done


def build_program(T=4096, E=16, emit_phases=("ffn1", "attn", "ffn2")):
    NQ = 4
    QT = T // NQ
    TC = QT // 2
    assert TC <= 512

    nc = bass.Bass("TRN2", target_bir_lowering=False)
    x_in = nc.dram_tensor("x", [T, D], F32, kind="ExternalInput").ap()
    w_qkv = nc.dram_tensor("w_qkv", [3 * D, D], F32, kind="ExternalInput").ap()
    w_out = nc.dram_tensor("w_out", [D, D], F32, kind="ExternalInput").ap()
    b_out = nc.dram_tensor("b_out", [D], F32, kind="ExternalInput").ap()
    wr = nc.dram_tensor("Wr", [D, E], F32, kind="ExternalInput").ap()
    br = nc.dram_tensor("br", [E], F32, kind="ExternalInput").ap()
    w1 = nc.dram_tensor("W1", [E, D, HFF], F32, kind="ExternalInput").ap()
    b1 = nc.dram_tensor("b1", [E, HFF], F32, kind="ExternalInput").ap()
    w2 = nc.dram_tensor("W2", [E, HFF, D], F32, kind="ExternalInput").ap()
    b2 = nc.dram_tensor("b2", [E, D], F32, kind="ExternalInput").ap()
    out = nc.dram_tensor("out", [T, D], F32, kind="ExternalOutput").ap()

    x_scr = nc.dram_tensor("x_scr", [DC, 128, T], F32, kind="Internal").ap()
    q_scr = nc.dram_tensor("q_scr", [DC, 128, T], BF16, kind="Internal").ap()
    v_scr = nc.dram_tensor("v_scr", [T // 128, 128, NH * 65], BF16,
                           kind="Internal").ap()
    cs_scr = nc.dram_tensor("cs_scr", [2, 128, T], F32, kind="Internal").ap()
    w_scr = nc.dram_tensor("w_scr", [E, 2, 128, DC * HFF], BF16,
                           kind="Internal").ap()

    with ExitStack() as ctx:
        p = P(nc, ctx)
        sb = lambda n, s, d=F32: ctx.enter_context(nc.sbuf_tensor(n, s, d))

        ident_f = sb("ident_f", [128, 128], F32)
        ident_b = sb("ident_b", [128, 128], BF16)
        ones_b = sb("ones_b", [128, 128], BF16)
        halfsel = sb("halfsel", [128, 2], BF16)
        m2swap = sb("m2swap", [128, 128], BF16)
        wr_sb = sb("wr_sb", [128, DC * E], BF16)
        br_sb = sb("br_sb", [1, 16], BF16)
        b2_sb = sb("b2_sb", [16, D], BF16)
        b1t = sb("b1t", [128, HC * E], F32)
        boutt = sb("boutt", [128, DC], F32)
        ltri_b = sb("ltri_b", [128, 128], BF16)
        iof0 = sb("iof0", [128, 128], BF16)
        iof1 = sb("iof1", [128, 128], BF16)
        C = dict(ident_f=ident_f, ident_b=ident_b, ones_b=ones_b,
                 halfsel=halfsel, m2swap=m2swap, wr_sb=wr_sb,
                 br_sb=br_sb, b2_sb=b2_sb, b1t=b1t, boutt=boutt,
                 ltri_b=ltri_b, iof0=iof0, iof1=iof1)

        setup(nc, p, ctx, C, T, E, wr, br, b1, b2, b_out, cs_scr)
        p.barrier()

        last_ffn1 = ("attn" not in emit_phases and "ffn2" not in emit_phases)
        if "ffn1" in emit_phases:
            ffn_phase(nc, p, ctx, C, dict(
                T=T, E=E, NQ=NQ, QT=QT, TC=TC, layer=0,
                x_nat_in=x_in, x_fm_in=None,
                x_fm_out=None if last_ffn1 else x_scr,
                x_nat_out=out if last_ffn1 else None, w1=w1, w2=w2,
                w_scr=w_scr, w_cast=True))
        else:
            ingest_only(nc, p, ctx, C, T, x_in, x_scr)
        p.barrier()

        if "attn" in emit_phases:
            attn_phase(nc, p, ctx, C, dict(
                T=T, x_scr=x_scr, q_scr=q_scr, v_scr=v_scr, cs_scr=cs_scr,
                w_qkv=w_qkv, w_out=w_out,
                x_nat_out=None if "ffn2" in emit_phases else out))
            p.barrier()

        if "ffn2" in emit_phases:
            ffn_phase(nc, p, ctx, C, dict(
                T=T, E=E, NQ=NQ, QT=QT, TC=TC, layer=2,
                x_nat_in=None, x_fm_in=x_scr,
                x_fm_out=None, x_nat_out=out, w1=w1, w2=w2,
                w_scr=w_scr, w_cast=("ffn1" not in emit_phases)))
            p.barrier()
        elif "attn" not in emit_phases and "ffn1" not in emit_phases:
            raise ValueError("no phases")

        if "attn" in emit_phases and "ffn2" not in emit_phases:
            spill_out(nc, p, ctx, C, T, x_scr, out)
            p.barrier()

        for s in p.dma_sems:
            p.wait_raw("sync", s, p.counts[s])

        block = ctx.enter_context(nc.Block())
        for eng in P.ENG:
            def runner(e, eng=eng):
                for f in p.streams[eng]:
                    f(e)
            getattr(block, eng)(runner)
    return nc


def setup(nc, p, ctx, C, T, E, wr, br, b1, b2, b_out, cs_scr):
    g = "gpsimd"
    p.addsem("d_wr", dma=True)
    p.addsem("d_br", dma=True)
    p.addsem("d_b2", dma=True)
    p.addsem("d_b1", dma=True)
    p.addsem("d_bo", dma=True)
    p.addsem("d_csp", dma=True)
    with ExitStack() as st:
        ssb = lambda n, s, d=F32: st.enter_context(nc.sbuf_tensor(n, s, d))
        iota_c = ssb("iota_c", [128, 128])
        iota_r = ssb("iota_r", [128, 128])
        t1 = ssb("t1c", [128, 1])
        t2 = ssb("t2c", [128, 1])
        t3 = ssb("t3c", [128, 1])
        b1_stage = ssb("b1_stage", [16, HFF])
        bo_stage = ssb("bo_stage", [DC, 128])
        tab_s = ssb("cs_tab_s", [32, T])
        tab_c = ssb("cs_tab_c", [32, T])
        tb2 = ssb("cs_tb2", [32, T])
        invf = ssb("cs_invf", [32, 1])
        frq = ssb("cs_frq", [32, T])
        kint = st.enter_context(nc.sbuf_tensor("cs_kint", [32, T], mybir.dt.int32))
        ps_su = st.enter_context(nc.psum_tensor("ps_su", [128, 512], F32))

        p.emit(g, lambda e, iota_c=iota_c: e.iota(
            iota_c[:], [[1, 128]], channel_multiplier=0,
            allow_small_or_imprecise_dtypes=True), produces=["su:ic"])
        p.emit(g, lambda e, iota_r=iota_r: e.iota(
            iota_r[:], [[0, 128]], channel_multiplier=1,
            allow_small_or_imprecise_dtypes=True), produces=["su:ir"])
        p.emit("vector", lambda e, C=C, iota_c=iota_c, iota_r=iota_r: e.tensor_tensor(
            out=C["ident_f"][:], in0=iota_c[:], in1=iota_r[:], op=ALU.is_equal),
            needs=["su:ic", "su:ir"], produces=["identf"])
        p.emit("vector", lambda e, C=C: e.tensor_copy(C["ident_b"][:],
                                                      C["ident_f"][:]),
               needs=["identf"], produces=["identb"])
        p.emit("vector", lambda e, C=C: e.memset(C["ones_b"][:], 1.0),
               produces=["onesb"])
        p.emit("vector", lambda e, C=C, iota_r=iota_r: e.tensor_scalar(
            out=C["halfsel"][:, 0:1], in0=iota_r[:, 0:1], scalar1=64.0,
            scalar2=None, op0=ALU.is_lt), needs=["su:ir"], produces=["hs0"])
        p.emit("vector", lambda e, C=C, iota_r=iota_r: e.tensor_scalar(
            out=C["halfsel"][:, 1:2], in0=iota_r[:, 0:1], scalar1=64.0,
            scalar2=None, op0=ALU.is_ge), needs=["su:ir", "hs0"],
            produces=["halfsel"])
        p.emit("vector", lambda e, C=C, iota_r=iota_r, iota_c=iota_c:
               e.tensor_tensor(out=C["ltri_b"][:], in0=iota_r[:], in1=iota_c[:],
                               op=ALU.is_lt),
               needs=["su:ic", "su:ir"], produces=["ltri"])
        p.emit("vector", lambda e, C=C, iota_c=iota_c: e.tensor_scalar(
            out=C["iof0"][:], in0=iota_c[:], scalar1=1.0, scalar2=None,
            op0=ALU.add), needs=["su:ic"], produces=["iof0"])
        p.emit("vector", lambda e, C=C, iota_c=iota_c: e.tensor_scalar(
            out=C["iof1"][:], in0=iota_c[:], scalar1=129.0, scalar2=None,
            op0=ALU.add), needs=["su:ic"], produces=["iof1"])
        p.emit("vector", lambda e, C=C, iota_r=iota_r: e.tensor_scalar(
            out=t1[:], in0=iota_r[:, 0:1], scalar1=64.0, scalar2=64.0,
            op0=ALU.is_ge, op1=ALU.mult) if False else e.tensor_scalar(
            out=t1[:], in0=iota_r[:, 0:1], scalar1=64.0, scalar2=64.0,
            op0=ALU.is_ge, op1=ALU.mult), needs=["su:ir"], produces=["su:r64"])
        p.emit("vector", lambda e, iota_r=iota_r, t1=t1, t2=t2: e.tensor_tensor(
            out=t2[:], in0=iota_r[:, 0:1], in1=t1[:], op=ALU.subtract),
            needs=["su:r64"], produces=["su:loc"])
        p.emit("vector", lambda e, t2=t2, t3=t3: e.tensor_scalar(
            out=t3[:], in0=t2[:], scalar1=32.0, scalar2=-64.0,
            op0=ALU.is_ge, op1=ALU.mult), needs=["su:loc"], produces=["su:u"])
        p.emit("vector", lambda e, t2=t2: e.tensor_scalar(
            out=t2[:], in0=t2[:], scalar1=32.0, scalar2=None, op0=ALU.add),
            needs=["su:u"], produces=["su:l32"])
        p.emit("vector", lambda e, t2=t2, t3=t3: e.tensor_tensor(
            out=t2[:], in0=t2[:], in1=t3[:], op=ALU.add),
            needs=["su:l32"], produces=["su:tl"])
        p.emit("vector", lambda e, t1=t1, t2=t2: e.tensor_tensor(
            out=t2[:], in0=t2[:], in1=t1[:], op=ALU.add),
            needs=["su:tl"], produces=["su:tgt"])
        p.emit("vector", lambda e, C=C, iota_c=iota_c, t2=t2: e.tensor_scalar(
            out=C["m2swap"][:], in0=iota_c[:], scalar1=t2[:], scalar2=None,
            op0=ALU.is_equal), needs=["su:tgt", "su:ic"], produces=["m2swap"])
        p.emit("vector", lambda e, C=C: e.memset(C["b2_sb"][:], 0.0),
               produces=["b2z"])
        p.emit("vector", lambda e, C=C: e.memset(C["br_sb"][:], 0.0),
               produces=["brz"])
        p.emit("vector", lambda e, b1_stage=b1_stage: e.memset(b1_stage[:], 0.0),
               produces=["b1z"])

        p.emit(g, lambda e, C=C, wr=wr: e.dma_start(
            out=C["wr_sb"][:, 0:DC * E].rearrange("p (dc e) -> p dc e", e=E),
            in_=wr.rearrange("(dc p) e -> p dc e", p=128)),
            sem="d_wr", dma=True, produces=["wr_sb"])
        p.emit(g, lambda e, C=C, br=br: e.dma_start(
            out=C["br_sb"][0:1, 0:E], in_=br.unsqueeze(0)),
            needs=["brz"], sem="d_br", dma=True, produces=["br_sb"])
        p.emit(g, lambda e, C=C, b2=b2: e.dma_start(
            out=C["b2_sb"][0:E, :], in_=b2),
            needs=["b2z"], sem="d_b2", dma=True, produces=["b2_sb"])
        p.emit("sync", lambda e, b1_stage=b1_stage, b1=b1: e.dma_start(
            out=b1_stage[0:E, :], in_=b1), needs=["b1z"],
            sem="d_b1", dma=True, produces=["b1_stage"])
        p.emit("sync", lambda e, bo_stage=bo_stage, b_out=b_out: e.dma_start(
            out=bo_stage[:], in_=b_out.rearrange("(dc p) -> dc p", p=128)),
            sem="d_bo", dma=True, produces=["bo_stage"])

        for hc in range(HC):
            p.emit("tensor", lambda e, hc=hc, ps_su=ps_su, b1_stage=b1_stage,
                   C=C: e.transpose(
                       ps_su[:, 0:16], b1_stage[0:16, hc * 128:(hc + 1) * 128],
                       C["ident_f"][0:16, 0:16]),
                   needs=["b1_stage", "identf"] + ([f"b1t:{hc-1}"] if hc else []),
                   produces=[f"ps_b1t:{hc}"])
            p.emit("vector", lambda e, hc=hc, ps_su=ps_su, C=C: e.tensor_copy(
                C["b1t"][:, hc * E:(hc + 1) * E], ps_su[:, 0:E]),
                needs=[f"ps_b1t:{hc}"], produces=[f"b1t:{hc}"])
        p.emit("tensor", lambda e, ps_su=ps_su, bo_stage=bo_stage, C=C: e.transpose(
            ps_su[:, 0:DC], bo_stage[:, :], C["ident_f"][0:DC, 0:DC]),
            needs=["bo_stage", "identf", f"b1t:{HC-1}"], produces=["ps_bo"])
        p.emit("vector", lambda e, ps_su=ps_su, C=C: e.tensor_copy(
            C["boutt"][:, 0:DC], ps_su[:, 0:DC]),
            needs=["ps_bo"], produces=["boutt"])

        p.emit(g, lambda e, invf=invf: e.iota(
            invf[:], [[0, 1]], channel_multiplier=1,
            allow_small_or_imprecise_dtypes=True), produces=["cs:i"])
        p.emit("scalar", lambda e, invf=invf: e.activation(
            out=invf[:], in_=invf[:], func=AF.Exp, bias=0.0,
            scale=-math.log(10000.0) / 32.0), needs=["cs:i"], produces=["cs:if"])
        p.emit(g, lambda e, frq=frq, T=T: e.iota(
            frq[:], [[1, T]], channel_multiplier=0,
            allow_small_or_imprecise_dtypes=True), produces=["cs:t"])
        p.emit("vector", lambda e, frq=frq, invf=invf: e.tensor_scalar(
            out=frq[:], in0=frq[:], scalar1=invf[0:32, :], scalar2=None,
            op0=ALU.mult), needs=["cs:t", "cs:if"], produces=["cs:f"])

        def rr(dst, bias, tg):
            p.emit("vector", lambda e, dst=dst, frq=frq,
                   bias=bias: e.tensor_scalar(
                       out=dst[:], in0=frq[:], scalar1=bias,
                       scalar2=1.0 / TWO_PI, op0=ALU.add, op1=ALU.mult),
                   needs=["cs:f"], produces=[f"cs:u{tg}"])
            p.emit("vector", lambda e, dst=dst, kint=kint: e.tensor_copy(
                kint[:], dst[:]),
                needs=[f"cs:u{tg}"], produces=[f"cs:k{tg}"])
            p.emit("vector", lambda e, dst=dst, kint=kint: e.tensor_copy(
                dst[:], kint[:]),
                needs=[f"cs:k{tg}"], produces=[f"cs:kf{tg}"])
            p.emit("vector", lambda e, dst=dst, frq=frq: e.scalar_tensor_tensor(
                out=dst[:], in0=dst[:], scalar=-TWO_PI,
                in1=frq[:], op0=ALU.mult, op1=ALU.add),
                needs=[f"cs:kf{tg}", "cs:f"], produces=[f"cs:d{tg}"])
            if bias != 0.0:
                p.emit("vector", lambda e, dst=dst, bias=bias: e.tensor_scalar(
                    out=dst[:], in0=dst[:], scalar1=bias, scalar2=None,
                    op0=ALU.add),
                    needs=[f"cs:d{tg}"], produces=[f"cs:db{tg}"])
            else:
                p.res[f"cs:db{tg}"] = p.res[f"cs:d{tg}"]
            p.emit("vector", lambda e, dst=dst, tb2=tb2: e.tensor_scalar(
                out=tb2[:], in0=dst[:], scalar1=math.pi,
                scalar2=-TWO_PI, op0=ALU.is_gt, op1=ALU.mult),
                needs=[f"cs:db{tg}"], produces=[f"cs:w{tg}"])
            p.emit("vector", lambda e, dst=dst, tb2=tb2: e.tensor_tensor(
                out=dst[:], in0=dst[:], in1=tb2[:], op=ALU.add),
                needs=[f"cs:w{tg}"], produces=[f"cs:r0{tg}"])
            p.emit("vector", lambda e, dst=dst, tb2=tb2: e.tensor_scalar(
                out=tb2[:], in0=dst[:], scalar1=-math.pi,
                scalar2=TWO_PI, op0=ALU.is_lt, op1=ALU.mult),
                needs=[f"cs:r0{tg}"], produces=[f"cs:w2{tg}"])
            p.emit("vector", lambda e, dst=dst, tb2=tb2: e.tensor_tensor(
                out=dst[:], in0=dst[:], in1=tb2[:], op=ALU.add),
                needs=[f"cs:w2{tg}"], produces=[f"cs:r{tg}"])
            p.emit("scalar", lambda e, dst=dst: e.activation(
                out=dst[:], in_=dst[:], func=AF.Sin, bias=0.0,
                scale=1.0), needs=[f"cs:r{tg}"], produces=[f"cs:s{tg}"])

        rr(tab_s, 0.0, "s")
        rr(tab_c, math.pi / 2.0, "c")
        for kk in range(4):
            p.emit("sync", lambda e, kk=kk, tab_c=tab_c, cs_scr=cs_scr: e.dma_start(
                out=cs_scr[0, kk * 32:(kk + 1) * 32, :], in_=tab_c[:]),
                needs=["cs:sc"], sem="d_csp", dma=True, produces=[f"cs:C{kk}"])
        p.emit("vector", lambda e, tab_s=tab_s, tb2=tb2: e.tensor_scalar(
            out=tb2[:], in0=tab_s[:], scalar1=-1.0, scalar2=None,
            op0=ALU.mult), needs=["cs:ss", "cs:rc"],
            produces=["cs:neg"])
        for kk in range(4):
            buf, srcr = ((tb2, "cs:neg") if kk % 2 == 0 else (tab_s, "cs:ss"))
            p.emit("sync", lambda e, kk=kk, buf=buf, cs_scr=cs_scr: e.dma_start(
                out=cs_scr[1, kk * 32:(kk + 1) * 32, :], in_=buf[:]),
                needs=[srcr], sem="d_csp", dma=True, produces=[f"cs:S{kk}"])


def ffn_phase(nc, p, ctx, C, a):
    T, E, NQ, QT, TC = a["T"], a["E"], a["NQ"], a["QT"], a["TC"]
    L = a["layer"]
    w1, w2 = a["w1"], a["w2"]
    NTC = 2
    ntb = QT // 128
    CAP = 256
    NCC = CAP // 128
    PADE = max(E, 8)
    assert E == 16 and QT == 1024 and TC == 512

    with ExitStack() as ph:
        psb = lambda n, s, d=F32: ph.enter_context(
            nc.sbuf_tensor(f"f{L}_{n}", s, d))
        pps = lambda n: ph.enter_context(
            nc.psum_tensor(f"f{L}_{n}", [128, 512], F32))
        x_q = psb("x_q", [128, DC * QT])
        xln = psb("xln", [128, DC * QT], BF16)
        xtm = psb("xtm", [128, ntb * D], BF16)
        stage = psb("stage", [128, D])
        xbf = [psb("xbf0", [128, TC], BF16), psb("xbf1", [128, TC], BF16)]
        xsq = [psb("xsq0", [128, TC], BF16), psb("xsq1", [128, TC], BF16)]
        row0 = psb("row0", [1, TC])
        row1 = psb("row1", [1, TC])
        hil = psb("hil", [1, 4 * TC], BF16)
        tvec = psb("tvec", [128, TC])
        uvec = psb("uvec", [128, TC])
        wbufs = [psb("wa", [128, DC * HFF], BF16),
                 psb("wb", [128, DC * HFF], BF16),
                 psb("wc", [128, DC * HFF], BF16)]
        hact = psb("hact", [128, HC * CAP], BF16)
        xg = psb("xg", [128, DC * CAP], BF16)
        otm = xg
        sS = psb("sS", [128, NCC * QT], BF16)
        stmp = [psb(f"stmp{i}", [128, 128], BF16) for i in range(3)]
        pc_all = psb("pc_all", [128, ntb * E])
        comb_all = psb("comb_all", [128, ntb * E])
        offs_f = psb("offs_f", [1, E])
        offs_b = psb("offs_b", [1, 2 * E], BF16)
        mask_b = psb("mask_b", [128, E], BF16)
        comb_tm = psb("comb_tm", [128, 16])
        chi = psb("chi", [128, 16], BF16)
        clo = psb("clo", [128, 16], BF16)
        comb_hi = psb("comb_hi", [16, QT], BF16)
        comb_lo = psb("comb_lo", [16, QT], BF16)
        expd = psb("expd", [128, PADE])
        m8 = psb("m8", [128, 8])
        rsum = psb("rsum", [128, 1])
        nmax = psb("nmax", [128, 1])
        ln_scr = (xbf, xsq, row0, row1, hil, tvec, uvec)

        def P_ap(tb):
            return stage[:, tb * 128:(tb + 1) * 128].bitcast(BF16)

        ps_pool = Bank([pps(f"ps_p{i}") for i in range(6)])
        ps_h = ps_o = ps_c = ps_pool
        ps_m = Bank([pps("ps_m")])
        ps_r = Bank([pps("ps_r")])

        wsem = [p.addsem(f"d_f{L}w{i}", dma=True) for i in range(3)]
        wsem_s = [p.addsem(f"d_f{L}s{i}", dma=True) for i in range(3)]
        wbsem = ([p.addsem(f"d_wb{i}", dma=True) for i in range(3)]
                 if a.get("w_cast") else None)
        w_scr = a.get("w_scr")
        xsem = p.addsem(f"d_f{L}x", dma=True)
        osem = p.addsem(f"d_f{L}o", dma=True)
        wfree = [[], [], []]
        WB_ALL = ["wb_all0", "wb_all1", "wb_all2"]
        xq_free = [None, None]
        stage_free = None
        xln_free = None
        xtm_guard = None
        pP_guard = None
        xg_guard = None
        hact_guard = None
        otm_guard = None
        sS_guard = None
        offs_guard = None
        mask_guard = None

        p.emit("vector", lambda e, comb_tm=comb_tm: e.memset(comb_tm[:], 0.0),
               produces=[f"f{L}:ctz"])
        p.emit("vector", lambda e, expd=expd: e.memset(expd[:], 0.0),
               produces=[f"f{L}:exz"])
        p.emit("vector", lambda e, comb_hi=comb_hi: e.memset(comb_hi[:], 0.0),
               produces=[f"f{L}:chz0"])
        p.emit("vector", lambda e, comb_lo=comb_lo: e.memset(comb_lo[:], 0.0),
               needs=[f"f{L}:chz0"], produces=[f"f{L}:chz"])
        comb_hl_free = f"f{L}:chz"
        expd_guard = f"f{L}:exz"
        nmax_guard = None
        chi_guard = None
        clo_guard = None

        for q in range(NQ):
            qt0 = q * QT
            xq_ready = {0: [], 1: []}
            if a["x_fm_in"] is not None:
                for tc in range(NTC):
                    t0l = tc * TC
                    p.emit("sync", lambda e, qt0=qt0, t0l=t0l, tc=tc, x_q=x_q,
                           a=a, QT=QT, TC=TC: e.dma_start(
                        out=x_q[:].rearrange("p (c t) -> p c t",
                                             t=QT)[:, :, t0l:t0l + TC],
                        in_=a["x_fm_in"][:, :, qt0 + t0l:qt0 + t0l + TC]
                        .transpose([1, 0, 2])),
                        needs=[xq_free[tc]], sem=xsem, dma=True,
                        produces=[f"f{L}:xq:{q}:{tc}"])
                    xq_ready[tc] = [f"f{L}:xq:{q}:{tc}"]
            else:
                for tb in range(ntb):
                    p.emit("sync", lambda e, qt0=qt0, tb=tb, stage=stage,
                           a=a: e.dma_start(
                               out=stage[:],
                               in_=a["x_nat_in"][qt0 + tb * 128:
                                                 qt0 + (tb + 1) * 128, :]),
                           needs=[xq_free[tb // (ntb // NTC)], stage_free],
                           sem=xsem, dma=True,
                           produces=[f"f{L}:xs:{q}:{tb}"])
                    for dc in range(DC):
                        bank = ps_m if dc % 2 == 0 else ps_r
                        bi, pst, gg = bank.acquire()
                        p.emit("tensor", lambda e, dc=dc, pst=pst, stage=stage,
                               C=C: e.transpose(
                                   pst[:, 0:128],
                                   stage[:, dc * 128:(dc + 1) * 128],
                                   C["ident_f"][:]),
                               needs=[f"f{L}:xs:{q}:{tb}", "identf"] + gg,
                               produces=[f"f{L}:it:{q}:{tb}:{dc}"])
                        p.emit("vector", lambda e, dc=dc, tb=tb, pst=pst,
                               x_q=x_q, QT=QT: e.tensor_copy(
                                   x_q[:, dc * QT + tb * 128:
                                       dc * QT + (tb + 1) * 128],
                                   pst[:, 0:128]),
                               needs=[f"f{L}:it:{q}:{tb}:{dc}"],
                               produces=[f"f{L}:ic:{q}:{tb}:{dc}"])
                        bank.consumed(bi, f"f{L}:ic:{q}:{tb}:{dc}")
                        xq_ready[tb // (ntb // NTC)].append(
                            f"f{L}:ic:{q}:{tb}:{dc}")
                    stage_free = f"f{L}:it:{q}:{tb}:{DC-1}"

            ln_done = list(xq_ready[0]) + list(xq_ready[1])
            ln_tc = {}
            for tc in range(NTC):
                t0 = tc * TC
                done = emit_ln(
                    p, C, f"f{L}:ln:{q}:{tc}", TC, x_q,
                    lambda dc, t0=t0, QT=QT: slice(dc * QT + t0,
                                                   dc * QT + t0 + TC),
                    xln,
                    lambda dc, t0=t0, QT=QT: slice(dc * QT + t0,
                                                   dc * QT + t0 + TC),
                    ln_scr, (ps_m, ps_r), xq_ready[tc], True,
                    xln_guard=xln_free)
                ln_done += done
                ln_tc[tc] = [r for r in done if ":xln:" in r]

            rt_done = []
            rt_tc = {}
            p.emit("vector", lambda e, offs_f=offs_f: e.memset(offs_f[:], 0.0),
                   needs=[offs_guard], produces=[f"f{L}:oz:{q}"])
            p.emit("vector", lambda e, offs_b=offs_b: e.memset(offs_b[:], 0.0),
                   needs=[offs_guard, f"f{L}:oz:{q}"],
                   produces=[f"f{L}:ozb:{q}"])
            for tb in range(ntb):
                bi, psc, gg = ps_c.acquire()
                for dc in range(DC):
                    p.emit("tensor", lambda e, dc=dc, tb=tb, psc=psc, xln=xln,
                           C=C, QT=QT, E=E: e.matmul(
                               psc[:, 0:E],
                               xln[:, dc * QT + tb * 128:
                                   dc * QT + (tb + 1) * 128],
                               C["wr_sb"][:, dc * E:(dc + 1) * E],
                               start=(dc == 0), stop=False),
                           needs=(ln_tc[tb // (ntb // NTC)] + ["wr_sb"] +
                                  gg) if dc == 0 else [],
                           produces=())
                p.emit("tensor", lambda e, psc=psc, C=C, E=E: e.matmul(
                    psc[:, 0:E], C["ones_b"][0:1, :], C["br_sb"][0:1, 0:E],
                    start=False, stop=True),
                    needs=["br_sb", "onesb"], produces=[f"f{L}:rt:{q}:{tb}"])
                p.emit("vector", lambda e, psc=psc, nmax=nmax, E=E: e.tensor_reduce(
                    out=nmax[:], in_=psc[:, 0:E], axis=AXX, op=ALU.max,
                    negate=True),
                    needs=[f"f{L}:rt:{q}:{tb}", nmax_guard],
                    produces=[f"f{L}:nm:{q}:{tb}"])
                p.emit("scalar", lambda e, psc=psc, expd=expd, nmax=nmax,
                       E=E: e.activation(
                           out=expd[:, 0:E], in_=psc[:, 0:E], func=AF.Exp,
                           bias=nmax[:], scale=1.0),
                       needs=[f"f{L}:nm:{q}:{tb}", expd_guard],
                       produces=[f"f{L}:ex:{q}:{tb}"])
                p.emit("vector", lambda e, expd=expd, m8=m8, PADE=PADE: e.max(
                    m8[:], expd[:, 0:PADE]),
                    needs=[f"f{L}:ex:{q}:{tb}"], produces=[f"f{L}:m8:{q}:{tb}"])
                p.emit("vector", lambda e, m8=m8, rsum=rsum: e.tensor_tensor(
                    out=rsum[:], in0=m8[:, 0:1], in1=m8[:, 1:2], op=ALU.add),
                    needs=[f"f{L}:m8:{q}:{tb}"], produces=[f"f{L}:s:{q}:{tb}"])
                p.emit("vector", lambda e, rsum=rsum: e.reciprocal(
                    rsum[:], rsum[:]),
                    needs=[f"f{L}:s:{q}:{tb}"], produces=[f"f{L}:rc:{q}:{tb}"])
                p.emit("vector", lambda e, rsum=rsum: e.tensor_scalar(
                    out=rsum[:], in0=rsum[:], scalar1=0.5, scalar2=None,
                    op0=ALU.mult),
                    needs=[f"f{L}:rc:{q}:{tb}"], produces=[f"f{L}:rh:{q}:{tb}"])
                p.emit("vector", lambda e, expd=expd, m8=m8, comb_tm=comb_tm,
                       E=E: e.tensor_scalar(
                           out=comb_tm[:, 0:E], in0=expd[:, 0:E],
                           scalar1=m8[:, 1:2], scalar2=None, op0=ALU.is_ge),
                       needs=[f"f{L}:m8:{q}:{tb}", f"f{L}:ctz"],
                       produces=[f"f{L}:mk:{q}:{tb}"])
                p.emit("vector", lambda e, expd=expd, rsum=rsum,
                       comb_tm=comb_tm, E=E: e.scalar_tensor_tensor(
                           out=comb_tm[:, 0:E], in0=expd[:, 0:E],
                           scalar=rsum[:], in1=comb_tm[:, 0:E],
                           op0=ALU.mult, op1=ALU.mult),
                       needs=[f"f{L}:mk:{q}:{tb}", f"f{L}:rh:{q}:{tb}"],
                       produces=[f"f{L}:cb:{q}:{tb}"])
                expd_guard = f"f{L}:cb:{q}:{tb}"
                nmax_guard = f"f{L}:ex:{q}:{tb}"
                p.emit("vector", lambda e, chi=chi, comb_tm=comb_tm: e.tensor_copy(
                    chi[:], comb_tm[:]),
                    needs=[f"f{L}:cb:{q}:{tb}", chi_guard],
                    produces=[f"f{L}:ch:{q}:{tb}"])
                p.emit("vector", lambda e, clo=clo, chi=chi,
                       comb_tm=comb_tm: e.tensor_tensor(
                           out=clo[:], in0=comb_tm[:], in1=chi[:],
                           op=ALU.subtract),
                       needs=[f"f{L}:ch:{q}:{tb}", clo_guard],
                       produces=[f"f{L}:cl:{q}:{tb}"])
                bi2, psc2, gg2 = ps_c.acquire()
                p.emit("tensor", lambda e, psc2=psc2, chi=chi, C=C: e.transpose(
                    psc2[0:16, 0:64].bitcast(BF16), chi[:], C["ident_b"][:]),
                    needs=[f"f{L}:ch:{q}:{tb}", "identb"] + gg2,
                    produces=[f"f{L}:th:{q}:{tb}"])
                bi3, psc3, gg3 = ps_c.acquire()
                p.emit("tensor", lambda e, psc3=psc3, clo=clo, C=C: e.transpose(
                    psc3[0:16, 0:64].bitcast(BF16), clo[:], C["ident_b"][:]),
                    needs=[f"f{L}:cl:{q}:{tb}"] + gg3,
                    produces=[f"f{L}:tl:{q}:{tb}"])
                p.emit("vector", lambda e, tb=tb, psc2=psc2,
                       comb_hi=comb_hi: e.tensor_copy(
                           comb_hi[0:16, tb * 128:(tb + 1) * 128],
                           psc2[0:16, 0:64].bitcast(BF16)),
                       needs=[f"f{L}:th:{q}:{tb}", comb_hl_free],
                       produces=[f"f{L}:chl:{q}:{tb}"])
                p.emit("vector", lambda e, tb=tb, psc3=psc3,
                       comb_lo=comb_lo: e.tensor_copy(
                           comb_lo[0:16, tb * 128:(tb + 1) * 128],
                           psc3[0:16, 0:64].bitcast(BF16)),
                       needs=[f"f{L}:tl:{q}:{tb}", comb_hl_free],
                       produces=[f"f{L}:cll:{q}:{tb}"])
                ps_c.consumed(bi, f"f{L}:ex:{q}:{tb}")
                ps_c.consumed(bi2, f"f{L}:chl:{q}:{tb}")
                ps_c.consumed(bi3, f"f{L}:cll:{q}:{tb}")
                rt_done += [f"f{L}:chl:{q}:{tb}", f"f{L}:cll:{q}:{tb}"]
                rt_tc[tb] = [f"f{L}:chl:{q}:{tb}", f"f{L}:cll:{q}:{tb}"]
                chi_guard = f"f{L}:th:{q}:{tb}"
                clo_guard = f"f{L}:tl:{q}:{tb}"

                p.emit("vector", lambda e, comb_tm=comb_tm, mask_b=mask_b,
                       E=E: e.tensor_scalar(
                           out=mask_b[:, 0:E], in0=comb_tm[:, 0:E],
                           scalar1=0.0, scalar2=None, op0=ALU.is_gt),
                       needs=[f"f{L}:cb:{q}:{tb}", mask_guard],
                       produces=[f"f{L}:mkb:{q}:{tb}"])
                p.emit("vector", lambda e, tb=tb, comb_all=comb_all,
                       comb_tm=comb_tm, E=E: e.tensor_copy(
                           comb_all[:, tb * E:(tb + 1) * E],
                           comb_tm[:, 0:E]),
                       needs=[f"f{L}:cb:{q}:{tb}"],
                       produces=[f"f{L}:ca:{q}:{tb}"])
                mi, pspin, mgg = ps_m.acquire()
                offs_state = (f"f{L}:ozb:{q}" if tb == 0
                              else f"f{L}:obl:{q}:{tb-1}")
                p.emit("tensor", lambda e, pspin=pspin, C=C, mask_b=mask_b,
                       E=E: e.matmul(
                           pspin[:, 0:E], C["ltri_b"][:], mask_b[:, 0:E],
                           start=True, stop=False),
                       needs=[f"f{L}:mkb:{q}:{tb}", "ltri"] + mgg,
                       produces=())
                p.emit("tensor", lambda e, pspin=pspin, C=C, offs_b=offs_b,
                       E=E: e.matmul(
                           pspin[:, 0:E], C["ones_b"][0:1, :],
                           offs_b[0:1, 0:E], start=False, stop=False),
                       needs=[offs_state, "onesb"], produces=())
                p.emit("tensor", lambda e, pspin=pspin, C=C, offs_b=offs_b,
                       E=E: e.matmul(
                           pspin[:, 0:E], C["ones_b"][0:1, :],
                           offs_b[0:1, E:2 * E], start=False, stop=True),
                       needs=[], produces=[f"f{L}:pin:{q}:{tb}"])
                p.emit("vector", lambda e, tb=tb, pspin=pspin, pc_all=pc_all,
                       mask_b=mask_b, E=E: e.scalar_tensor_tensor(
                           out=pc_all[:, tb * E:(tb + 1) * E],
                           in0=pspin[:, 0:E], scalar=1.0,
                           in1=mask_b[:, 0:E], op0=ALU.add, op1=ALU.mult),
                       needs=[f"f{L}:pin:{q}:{tb}", f"f{L}:mkb:{q}:{tb}"],
                       produces=[f"f{L}:pc:{q}:{tb}"])
                ri2, pscnt, rgg2 = ps_r.acquire()
                p.emit("tensor", lambda e, pscnt=pscnt, C=C, mask_b=mask_b,
                       E=E: e.matmul(
                           pscnt[0:1, 0:E], C["ones_b"][:, 0:1],
                           mask_b[:, 0:E], start=True, stop=True),
                       needs=[f"f{L}:mkb:{q}:{tb}", "onesb"] + rgg2,
                       produces=[f"f{L}:cnt:{q}:{tb}"])
                p.emit("vector", lambda e, offs_f=offs_f, pscnt=pscnt,
                       E=E: e.tensor_tensor(
                           out=offs_f[0:1, 0:E], in0=offs_f[0:1, 0:E],
                           in1=pscnt[0:1, 0:E], op=ALU.add),
                       needs=[f"f{L}:cnt:{q}:{tb}"],
                       produces=[f"f{L}:of:{q}:{tb}"])
                p.emit("vector", lambda e, offs_b=offs_b, offs_f=offs_f,
                       E=E: e.tensor_copy(offs_b[0:1, 0:E], offs_f[0:1, 0:E]),
                       needs=[f"f{L}:of:{q}:{tb}", f"f{L}:pin:{q}:{tb}"],
                       produces=[f"f{L}:obh:{q}:{tb}"])
                p.emit("vector", lambda e, offs_b=offs_b, offs_f=offs_f,
                       E=E: e.tensor_tensor(
                           out=offs_b[0:1, E:2 * E], in0=offs_f[0:1, 0:E],
                           in1=offs_b[0:1, 0:E], op=ALU.subtract),
                       needs=[f"f{L}:obh:{q}:{tb}"],
                       produces=[f"f{L}:obl:{q}:{tb}"])
                ps_m.consumed(mi, f"f{L}:pc:{q}:{tb}")
                ps_r.consumed(ri2, f"f{L}:of:{q}:{tb}")
                mask_guard = f"f{L}:cnt:{q}:{tb}"
            offs_guard = f"f{L}:pin:{q}:{ntb-1}"

            tm_all = []
            for tb in range(ntb):
                tc = tb // (ntb // NTC)
                for dc in range(DC):
                    bank = ps_m if dc % 2 == 0 else ps_r
                    bi, pst, gg = bank.acquire()
                    p.emit("tensor", lambda e, tb=tb, dc=dc, pst=pst, xln=xln,
                           C=C, QT=QT: e.transpose(
                               pst[:, 0:64].bitcast(BF16),
                               xln[:, dc * QT + tb * 128:
                                   dc * QT + (tb + 1) * 128],
                               C["ident_b"][:]),
                           needs=[f"f{L}:ln:{q}:{tc}:xln:{dc}", "identb"] + gg,
                           produces=[f"f{L}:tmt:{q}:{tb}:{dc}"])
                    p.emit("vector", lambda e, tb=tb, dc=dc, pst=pst,
                           xtm=xtm: e.tensor_copy(
                               xtm[:, tb * D + dc * 128:
                                   tb * D + (dc + 1) * 128],
                               pst[:, 0:64].bitcast(BF16)),
                           needs=[f"f{L}:tmt:{q}:{tb}:{dc}", xtm_guard],
                           produces=[f"f{L}:tm:{q}:{tb}:{dc}"])
                    bank.consumed(bi, f"f{L}:tm:{q}:{tb}:{dc}")
            xln_free = f"f{L}:tmt:{q}:{ntb-1}:{DC-1}"

            use_cast = a.get("w_cast") and q == 0

            def emit_P(ee, extra):
                for tb in range(ntb):
                    for cc in range(NCC):
                        p.emit("vector", lambda e, tb=tb, cc=cc, ee=ee,
                               pc_all=pc_all, C=C, E=E: e.tensor_scalar(
                                   out=P_ap(tb)[:, cc * 128:(cc + 1) * 128],
                                   in0=C[f"iof{cc}"][:],
                                   scalar1=pc_all[:, tb * E + ee:
                                                  tb * E + ee + 1],
                                   scalar2=None, op0=ALU.is_equal),
                               needs=[f"f{L}:pc:{q}:{tb}", f"iof{cc}"] + extra,
                               produces=[f"f{L}:P:{q}:{ee}:{tb}:{cc}"])

            emit_P(0, [pP_guard, stage_free])
            for ee in range(E):
                wi, wj = (2 * ee) % 3, (2 * ee + 1) % 3
                if use_cast or w_scr is None:
                    p.emit("gpsimd", lambda e, ee=ee, wi=wi, wbufs=wbufs,
                           w1=w1: e.dma_start(
                               out=wbufs[wi][:].rearrange("p (dc h) -> p dc h", h=HFF),
                               in_=w1[ee].rearrange("(dc p) h -> p dc h", p=128)),
                           needs=wfree[wi], sem=wsem[wi], dma=True,
                           produces=[f"f{L}:w1:{q}:{ee}"])
                    p.emit("gpsimd", lambda e, ee=ee, wj=wj, wbufs=wbufs,
                           w2=w2: e.dma_start(
                               out=wbufs[wj][:, 0:HC * D].rearrange("p (hc d) -> p hc d", d=D),
                               in_=w2[ee].rearrange("(hc p) d -> p hc d", p=128)),
                           needs=wfree[wj], sem=wsem[wj], dma=True,
                           produces=[f"f{L}:w2:{q}:{ee}"])
                    if use_cast:
                        p.emit("sync", lambda e, ee=ee, wi=wi, wbufs=wbufs,
                               w_scr=w_scr: e.dma_start(
                                   out=w_scr[ee, 0], in_=wbufs[wi][:]),
                               needs=[f"f{L}:w1:{q}:{ee}"], sem=wbsem[wi],
                               dma=True, produces=[f"wb1:{ee}"])
                        p.emit("sync", lambda e, ee=ee, wj=wj, wbufs=wbufs,
                               w_scr=w_scr: e.dma_start(
                                   out=w_scr[ee, 1][:, 0:HC * D],
                                   in_=wbufs[wj][:, 0:HC * D]),
                               needs=[f"f{L}:w2:{q}:{ee}"], sem=wbsem[wj],
                               dma=True, produces=[f"wb2:{ee}"])
                else:
                    p.emit("sync", lambda e, ee=ee, wi=wi, wbufs=wbufs,
                           w_scr=w_scr: e.dma_start(
                               out=wbufs[wi][:], in_=w_scr[ee, 0]),
                           needs=wfree[wi] + WB_ALL, sem=wsem_s[wi],
                           dma=True, produces=[f"f{L}:w1:{q}:{ee}"])
                    p.emit("gpsimd", lambda e, ee=ee, wj=wj, wbufs=wbufs,
                           w_scr=w_scr: e.dma_start(
                               out=wbufs[wj][:, 0:HC * D],
                               in_=w_scr[ee, 1][:, 0:HC * D]),
                           needs=wfree[wj] + WB_ALL, sem=wsem_s[wj],
                           dma=True, produces=[f"f{L}:w2:{q}:{ee}"])
                for dc in range(DC):
                    ci, psg, cgg = ps_c.acquire()
                    for tb in range(ntb):
                        p.emit("tensor", lambda e, tb=tb, dc=dc, psg=psg,
                               xtm=xtm, CAP=CAP: e.matmul(
                                   psg[:, 0:CAP],
                                   xtm[:, tb * D + dc * 128:
                                       tb * D + (dc + 1) * 128],
                                   P_ap(tb),
                                   start=(tb == 0), stop=(tb == ntb - 1)),
                               needs=[f"f{L}:tm:{q}:{tb}:{dc}",
                                      f"f{L}:P:{q}:{ee}:{tb}:0",
                                      f"f{L}:P:{q}:{ee}:{tb}:1"] +
                               (cgg if tb == 0 else []),
                               produces=[f"f{L}:g:{q}:{ee}:{dc}"]
                               if tb == ntb - 1 else ())
                    p.emit("scalar", lambda e, dc=dc, psg=psg, xg=xg,
                           CAP=CAP: e.activation(
                               out=xg[:, dc * CAP:(dc + 1) * CAP],
                               in_=psg[:, 0:CAP], func=AF.Copy, bias=0.0,
                               scale=1.0),
                           needs=[f"f{L}:g:{q}:{ee}:{dc}", otm_guard],
                           produces=[f"f{L}:xgc:{q}:{ee}:{dc}"])
                    ps_c.consumed(ci, f"f{L}:xgc:{q}:{ee}:{dc}")
                pP_guard = f"f{L}:g:{q}:{ee}:{DC-1}"
                st_hist = [None, None, None]

                def emit_S_chunk(sidx, ee=ee):
                    tb, cc = sidx // NCC, sidx % NCC
                    sl4 = sidx % 3
                    p.emit("vector", lambda e, tb=tb, cc=cc, ee=ee,
                           sl4=sl4, stmp=stmp, comb_all=comb_all,
                           E=E: e.tensor_scalar(
                               out=stmp[sl4][:],
                               in0=P_ap(tb)[:, cc * 128:(cc + 1) * 128],
                               scalar1=comb_all[:, tb * E + ee:
                                                tb * E + ee + 1],
                               scalar2=None, op0=ALU.mult),
                           needs=[f"f{L}:P:{q}:{ee}:{tb}:{cc}",
                                  f"f{L}:ca:{q}:{tb}", st_hist[sl4]],
                           produces=[f"f{L}:sm:{q}:{ee}:{tb}:{cc}"])
                    bank = ps_m if sidx % 2 == 0 else ps_r
                    bi4, pst4, gg4 = bank.acquire()
                    p.emit("tensor", lambda e, sl4=sl4, pst4=pst4,
                           stmp=stmp, C=C: e.transpose(
                               pst4[:, 0:64].bitcast(BF16), stmp[sl4][:],
                               C["ident_b"][:]),
                           needs=[f"f{L}:sm:{q}:{ee}:{tb}:{cc}",
                                  "identb"] + gg4,
                           produces=[f"f{L}:st:{q}:{ee}:{tb}:{cc}"])
                    st_hist[sl4] = f"f{L}:st:{q}:{ee}:{tb}:{cc}"
                    p.emit("scalar", lambda e, tb=tb, cc=cc, pst4=pst4,
                           sS=sS, QT=QT: e.activation(
                               out=sS[:, cc * QT + tb * 128:
                                      cc * QT + (tb + 1) * 128],
                               in_=pst4[:, 0:64].bitcast(BF16), func=AF.Copy,
                               bias=0.0, scale=1.0),
                           needs=[f"f{L}:st:{q}:{ee}:{tb}:{cc}", sS_guard],
                           produces=[f"f{L}:S:{q}:{ee}:{tb}:{cc}"])
                    bank.consumed(bi4, f"f{L}:S:{q}:{ee}:{tb}:{cc}")

                for hcc in range(HC):
                    pi, psh, pgg = ps_h.acquire()
                    for dc in range(DC):
                        p.emit("tensor", lambda e, hcc=hcc, dc=dc, wi=wi,
                               psh=psh, wbufs=wbufs, xg=xg, CAP=CAP: e.matmul(
                                   psh[:, 0:CAP],
                                   wbufs[wi][:, dc * HFF + hcc * 128:
                                             dc * HFF + (hcc + 1) * 128],
                                   xg[:, dc * CAP:(dc + 1) * CAP],
                                   start=(dc == 0), stop=(dc == DC - 1)),
                               needs=[f"f{L}:w1:{q}:{ee}",
                                      f"f{L}:xgc:{q}:{ee}:{dc}"] +
                               (pgg if dc == 0 else []),
                               produces=[f"f{L}:m1:{q}:{ee}:{hcc}"]
                               if dc == DC - 1 else ())
                    p.emit("scalar", lambda e, hcc=hcc, hact=hact, psh=psh,
                           ee=ee, C=C, CAP=CAP, E=E: e.activation(
                               out=hact[:, hcc * CAP:(hcc + 1) * CAP],
                               in_=psh[:, 0:CAP], func=AF.Gelu,
                               bias=C["b1t"][:, hcc * E + ee:
                                             hcc * E + ee + 1],
                               scale=1.0),
                           needs=[f"f{L}:m1:{q}:{ee}:{hcc}", f"b1t:{hcc}",
                                  hact_guard],
                           produces=[f"f{L}:gel:{q}:{ee}:{hcc}"])
                    ps_h.consumed(pi, f"f{L}:gel:{q}:{ee}:{hcc}")
                    emit_S_chunk(hcc)
                xg_guard = f"f{L}:m1:{q}:{ee}:{HC-1}"
                if ee + 1 < E:
                    emit_P(ee + 1, [pP_guard,
                                    f"f{L}:sm:{q}:{ee}:{ntb-1}:{NCC-1}"])
                for sc in range(NCC):
                    for dh in range(2):
                        oi, pso, ogg = ps_o.acquire()
                        for hcc in range(HC):
                            p.emit("tensor", lambda e, hcc=hcc, sc=sc, dh=dh,
                                   wj=wj, pso=pso, wbufs=wbufs, hact=hact,
                                   CAP=CAP: e.matmul(
                                       pso[:, 0:TC],
                                       hact[:, hcc * CAP + sc * 128:
                                            hcc * CAP + (sc + 1) * 128],
                                       wbufs[wj][:, hcc * D + dh * 512:
                                                 hcc * D + (dh + 1) * 512],
                                       start=(hcc == 0), stop=(hcc == HC - 1)),
                                   needs=[f"f{L}:gel:{q}:{ee}:{hcc}"] +
                                   (([f"f{L}:w2:{q}:{ee}"] + ogg)
                                    if hcc == 0 else []),
                                   produces=[f"f{L}:m2:{q}:{ee}:{sc}:{dh}"]
                                   if hcc == HC - 1 else ())
                        p.emit("scalar", lambda e, sc=sc, dh=dh, pso=pso,
                               otm=otm: e.activation(
                                   out=otm[:, sc * D + dh * 512:
                                           sc * D + (dh + 1) * 512],
                                   in_=pso[:, 0:TC], func=AF.Copy, bias=0.0,
                                   scale=1.0),
                               needs=[f"f{L}:m2:{q}:{ee}:{sc}:{dh}",
                                      xg_guard],
                               produces=[f"f{L}:oe:{q}:{ee}:{sc}:{dh}"])
                        ps_o.consumed(oi, f"f{L}:oe:{q}:{ee}:{sc}:{dh}")
                hact_guard = f"f{L}:m2:{q}:{ee}:{NCC-1}:1"
                for dc in range(DC):
                    for th in range(NTC):
                        ci2, psc2s, cgg2s = ps_c.acquire()
                        for cc in range(NCC):
                            p.emit("tensor", lambda e, dc=dc, th=th, cc=cc,
                                   psc2s=psc2s, otm=otm, sS=sS, QT=QT: e.matmul(
                                       psc2s[:, 0:TC],
                                       otm[:, cc * D + dc * 128:
                                           cc * D + (dc + 1) * 128],
                                       sS[:, cc * QT + th * TC:
                                          cc * QT + th * TC + TC],
                                       start=(cc == 0), stop=(cc == NCC - 1)),
                                   needs=[f"f{L}:oe:{q}:{ee}:{cc}:{dc//4}"] +
                                   [f"f{L}:S:{q}:{ee}:{tb2}:{cc}" for tb2 in
                                    range(th * 4, th * 4 + 4)] +
                                   (cgg2s if cc == 0 else []),
                                   produces=[f"f{L}:sc:{q}:{ee}:{dc}:{th}"]
                                   if cc == NCC - 1 else ())
                        p.emit("vector", lambda e, dc=dc, th=th, psc2s=psc2s,
                               x_q=x_q, QT=QT: e.tensor_tensor(
                                   out=x_q[:, dc * QT + th * TC:
                                           dc * QT + th * TC + TC],
                                   in0=x_q[:, dc * QT + th * TC:
                                           dc * QT + th * TC + TC],
                                   in1=psc2s[:, 0:TC], op=ALU.add),
                               needs=[f"f{L}:sc:{q}:{ee}:{dc}:{th}"],
                               produces=[f"f{L}:sca:{q}:{ee}:{dc}:{th}"])
                        ps_c.consumed(ci2, f"f{L}:sca:{q}:{ee}:{dc}:{th}")
                otm_guard = f"f{L}:sc:{q}:{ee}:{DC-1}:{NTC-1}"
                sS_guard = otm_guard
                wfree[wi] = [f"f{L}:m1:{q}:{ee}:{HC-1}"] + \
                    ([f"wb1:{ee}"] if use_cast else [])
                wfree[wj] = [f"f{L}:m2:{q}:{ee}:{NCC-1}:1"] + \
                    ([f"wb2:{ee}"] if use_cast else [])

            xtm_guard = f"f{L}:g:{q}:{E-1}:{DC-1}"
            if use_cast:
                for i in range(3):
                    p.res[f"wb_all{i}"] = (wbsem[i], p.counts[wbsem[i]])
            fin = []
            for tc in range(NTC):
                t0 = tc * TC
                for dc in range(DC):
                    ci, psc, cgg = ps_c.acquire()
                    p.emit("tensor", lambda e, dc=dc, t0=t0, psc=psc, C=C,
                           comb_hi=comb_hi, comb_lo=comb_lo, TC=TC: e.matmul(
                               psc[:, 0:TC],
                               C["b2_sb"][0:16, dc * 128:(dc + 1) * 128],
                               comb_hi[0:16, t0:t0 + TC],
                               start=True, stop=False),
                           needs=["b2_sb"] + rt_done + cgg, produces=())
                    p.emit("tensor", lambda e, dc=dc, t0=t0, psc=psc, C=C,
                           comb_hi=comb_hi, comb_lo=comb_lo, TC=TC: e.matmul(
                               psc[:, 0:TC],
                               C["b2_sb"][0:16, dc * 128:(dc + 1) * 128],
                               comb_lo[0:16, t0:t0 + TC],
                               start=False, stop=True),
                           needs=[], produces=[f"f{L}:b2m:{q}:{tc}:{dc}"])
                    p.emit("vector", lambda e, dc=dc, t0=t0, psc=psc, x_q=x_q,
                           QT=QT, TC=TC: e.tensor_tensor(
                               out=x_q[:, dc * QT + t0: dc * QT + t0 + TC],
                               in0=x_q[:, dc * QT + t0: dc * QT + t0 + TC],
                               in1=psc[:, 0:TC], op=ALU.add),
                           needs=[f"f{L}:b2m:{q}:{tc}:{dc}",
                                  f"f{L}:sca:{q}:{E-1}:{dc}:{tc}"],
                           produces=[f"f{L}:b2a:{q}:{tc}:{dc}"])
                    ps_c.consumed(ci, f"f{L}:b2a:{q}:{tc}:{dc}")
                    fin.append(f"f{L}:b2a:{q}:{tc}:{dc}")

            if a["x_fm_out"] is not None:
                for tc in range(NTC):
                    t0s = tc * TC
                    p.emit("sync", lambda e, qt0=qt0, t0s=t0s, x_q=x_q, a=a,
                           QT=QT, TC=TC: e.dma_start(
                        out=a["x_fm_out"][:, :, qt0 + t0s:qt0 + t0s + TC]
                        .transpose([1, 0, 2]),
                        in_=x_q[:].rearrange("p (c t) -> p c t",
                                             t=QT)[:, :, t0s:t0s + TC]),
                        needs=[f"f{L}:b2a:{q}:{tc}:{dc}" for dc in range(DC)],
                        sem=osem, dma=True, produces=[f"f{L}:xo:{q}:{tc}"])
                    xq_free[tc] = f"f{L}:xo:{q}:{tc}"
            else:
                prev_oc = None
                for tb in range(ntb):
                    ocs = []
                    for dc in range(DC):
                        bank = ps_m if dc % 2 == 0 else ps_r
                        bi, pst, gg = bank.acquire()
                        p.emit("tensor", lambda e, dc=dc, tb=tb, pst=pst,
                               x_q=x_q, C=C, QT=QT: e.transpose(
                                   pst[:, 0:128],
                                   x_q[:, dc * QT + tb * 128:
                                       dc * QT + (tb + 1) * 128],
                                   C["ident_f"][:]),
                               needs=[f"f{L}:b2a:{q}:{tb // (ntb // NTC)}:{d2}"
                                      for d2 in range(DC)] + gg,
                               produces=[f"f{L}:ot:{q}:{tb}:{dc}"])
                        p.emit("vector", lambda e, dc=dc, pst=pst,
                               stage=stage: e.tensor_copy(
                                   stage[:, dc * 128:(dc + 1) * 128],
                                   pst[:, 0:128]),
                               needs=[f"f{L}:ot:{q}:{tb}:{dc}", stage_free],
                               produces=[f"f{L}:oc:{q}:{tb}:{dc}"])
                        bank.consumed(bi, f"f{L}:oc:{q}:{tb}:{dc}")
                        ocs.append(f"f{L}:oc:{q}:{tb}:{dc}")
                        stage_free = None
                    p.emit("sync", lambda e, qt0=qt0, tb=tb, stage=stage,
                           a=a: e.dma_start(
                               out=a["x_nat_out"][qt0 + tb * 128:
                                                  qt0 + (tb + 1) * 128, :],
                               in_=stage[:]),
                           needs=ocs, sem=osem, dma=True,
                           produces=[f"f{L}:os:{q}:{tb}"])
                    stage_free = f"f{L}:os:{q}:{tb}"
                xq_free[0] = f"f{L}:os:{q}:{ntb // NTC - 1}"
                xq_free[1] = f"f{L}:os:{q}:{ntb-1}"
            comb_hl_free = f"f{L}:b2m:{q}:{NTC-1}:{DC-1}"


def ingest_only(nc, p, ctx, C, T, x_in, x_scr):
    with ExitStack() as ph:
        stage = ph.enter_context(nc.sbuf_tensor("ing_stage", [128, D], F32))
        fmt = ph.enter_context(nc.sbuf_tensor("ing_fm", [128, DC * 128], F32))
        bank = Bank([ph.enter_context(nc.psum_tensor("ing_ps0", [128, 512], F32)),
                     ph.enter_context(nc.psum_tensor("ing_ps1", [128, 512], F32))])
        seml = p.addsem("d_ingl", dma=True)
        sems = p.addsem("d_ings", dma=True)
        stage_free = None
        fm_free = None
        for tb in range(T // 128):
            p.emit("sync", lambda e, tb=tb, stage=stage, x_in=x_in: e.dma_start(
                out=stage[:], in_=x_in[tb * 128:(tb + 1) * 128, :]),
                needs=[stage_free], sem=seml, dma=True, produces=[f"ing:l:{tb}"])
            ocs = []
            for dc in range(DC):
                bi, pst, gg = bank.acquire()
                p.emit("tensor", lambda e, dc=dc, pst=pst, stage=stage,
                       C=C: e.transpose(
                           pst[:, 0:128], stage[:, dc * 128:(dc + 1) * 128],
                           C["ident_f"][:]),
                       needs=[f"ing:l:{tb}", "identf"] + gg,
                       produces=[f"ing:t:{tb}:{dc}"])
                p.emit("vector", lambda e, dc=dc, pst=pst, fmt=fmt: e.tensor_copy(
                    fmt[:, dc * 128:(dc + 1) * 128], pst[:, 0:128]),
                    needs=[f"ing:t:{tb}:{dc}", fm_free],
                    produces=[f"ing:c:{tb}:{dc}"])
                bank.consumed(bi, f"ing:c:{tb}:{dc}")
                ocs.append(f"ing:c:{tb}:{dc}")
                fm_free = None
            stage_free = f"ing:t:{tb}:{DC-1}"
            p.emit("sync", lambda e, tb=tb, fmt=fmt, x_scr=x_scr: e.dma_start(
                out=x_scr[:, :, tb * 128:(tb + 1) * 128].transpose([1, 0, 2]),
                in_=fmt[:].rearrange("p (c t) -> p c t", t=128)),
                needs=ocs, sem=sems, dma=True, produces=[f"ing:s:{tb}"])
            fm_free = f"ing:s:{tb}"


def spill_out(nc, p, ctx, C, T, x_scr, out):
    with ExitStack() as ph:
        fmt = ph.enter_context(nc.sbuf_tensor("sp_fm", [128, DC * 128], F32))
        stage = ph.enter_context(nc.sbuf_tensor("sp_st", [128, D], F32))
        bank = Bank([ph.enter_context(nc.psum_tensor("sp_ps0", [128, 512], F32)),
                     ph.enter_context(nc.psum_tensor("sp_ps1", [128, 512], F32))])
        seml = p.addsem("d_spl", dma=True)
        sems = p.addsem("d_sps", dma=True)
        fm_free = None
        stage_free = None
        for tb in range(T // 128):
            p.emit("sync", lambda e, tb=tb, fmt=fmt, x_scr=x_scr: e.dma_start(
                out=fmt[:].rearrange("p (c t) -> p c t", t=128),
                in_=x_scr[:, :, tb * 128:(tb + 1) * 128].transpose([1, 0, 2])),
                needs=[fm_free], sem=seml, dma=True, produces=[f"sp:l:{tb}"])
            ocs = []
            for dc in range(DC):
                bi, pst, gg = bank.acquire()
                p.emit("tensor", lambda e, dc=dc, pst=pst, fmt=fmt,
                       C=C: e.transpose(
                           pst[:, 0:128], fmt[:, dc * 128:(dc + 1) * 128],
                           C["ident_f"][:]),
                       needs=[f"sp:l:{tb}", "identf"] + gg,
                       produces=[f"sp:t:{tb}:{dc}"])
                p.emit("vector", lambda e, dc=dc, pst=pst,
                       stage=stage: e.tensor_copy(
                           stage[:, dc * 128:(dc + 1) * 128], pst[:, 0:128]),
                       needs=[f"sp:t:{tb}:{dc}", stage_free],
                       produces=[f"sp:c:{tb}:{dc}"])
                bank.consumed(bi, f"sp:c:{tb}:{dc}")
                ocs.append(f"sp:c:{tb}:{dc}")
                stage_free = None
            fm_free = f"sp:t:{tb}:{DC-1}"
            p.emit("sync", lambda e, tb=tb, stage=stage, out=out: e.dma_start(
                out=out[tb * 128:(tb + 1) * 128, :], in_=stage[:]),
                needs=ocs, sem=sems, dma=True, produces=[f"sp:s:{tb}"])
            stage_free = f"sp:s:{tb}"


def attn_phase(nc, p, ctx, C, a):
    with ExitStack() as actx:
        attn_phase_inner(nc, p, actx, C, a)


def attn_phase_inner(nc, p, ctx, C, a):
    T = a["T"]
    NKC = T // 128
    SBW = min(512, T // 8)
    NSB = T // SBW
    x_scr, q_scr, v_scr, cs_scr = a["x_scr"], a["q_scr"], a["v_scr"], a["cs_scr"]

    k_sb = ctx.enter_context(nc.sbuf_tensor("at_k", [128, 8 * T], BF16))
    bneg = ctx.enter_context(nc.sbuf_tensor("at_bneg", [128, 16], F32))

    with ExitStack() as ph:
        psb = lambda n, s, d=F32: ph.enter_context(nc.sbuf_tensor("atA_" + n, s, d))
        ppsA = lambda n, s: ph.enter_context(nc.psum_tensor("atA_" + n, s, F32))
        wqkvT = psb("wqkvT", [128, DC * 3 * D], BF16)
        wstage = [psb("wstage0", [128, D], BF16),
                  psb("wstage1", [128, D], BF16)]
        cs_c = [psb("cs_c0", [128, 512]), psb("cs_c1", [128, 512])]
        cs_s = [psb("cs_s0", [128, 512]), psb("cs_s1", [128, 512])]
        x1t = [psb("x1t0", [128, DC * 512]), psb("x1t1", [128, DC * 512])]
        xln2 = [psb("xln20", [128, DC * 512], BF16),
                psb("xln21", [128, DC * 512], BF16)]
        prer = [psb("prer0", [128, 512], BF16), psb("prer1", [128, 512], BF16)]
        rt1 = psb("rt1", [128, 512])
        rt2 = psb("rt2", [128, 512])
        qout = [psb("qout0", [128, 512], BF16), psb("qout1", [128, 512], BF16)]
        vstage = [psb("vst0", [128, NH * 65], BF16),
                  psb("vst1", [128, NH * 65], BF16)]
        qsq = psb("qsq", [128, 512], BF16)
        nrmq = psb("nrmq", [2, 8])
        nrmk = psb("nrmk", [2, 8])
        nscr = psb("nscr", [2, 1])
        brow = psb("brow", [1, 16])
        bmx = psb("bmx", [2, 8])
        bmxb = psb("bmxb", [2, 8], BF16)
        brb = psb("brb", [1, 16], BF16)
        xbf = [psb("xbf0", [128, 512], BF16), psb("xbf1", [128, 512], BF16)]
        xsq = [psb("xsq0", [128, 512], BF16), psb("xsq1", [128, 512], BF16)]
        row0 = psb("row0", [1, 512])
        row1 = psb("row1", [1, 512])
        hil = psb("hil", [1, 2048], BF16)
        tvec = psb("tvec", [128, 512])
        uvec = psb("uvec", [128, 512])
        ln_scr = (xbf, xsq, row0, row1, hil, tvec, uvec)

        pq = Bank([ppsA("pq0", [128, 512]), ppsA("pq1", [128, 512])])
        pw = Bank([ppsA("pw0", [128, 512]), ppsA("pw1", [128, 512])])
        pn = Bank([ppsA("pn", [128, 512])])
        pm = Bank([ppsA("pm", [128, 512])])

        wsem = p.addsem("d_aw", dma=True)
        xsem = p.addsem("d_ax", dma=True)
        qsem = [p.addsem("d_aq0", dma=True), p.addsem("d_aq1", dma=True)]
        vsem = [p.addsem("d_av0", dma=True), p.addsem("d_av1", dma=True)]
        csem = [p.addsem("d_ac0", dma=True), p.addsem("d_ac1", dma=True)]
        csem_s = [p.addsem("d_as0", dma=True), p.addsem("d_as1", dma=True)]

        csfree2 = [None, None]

        wstage_free = [None, None]
        for ec in range(3 * D // 128):
            wsl = ec % 2
            p.emit("gpsimd", lambda e, ec=ec, wsl=wsl, wstage=wstage,
                   a=a: e.dma_start(
                       out=wstage[wsl][:],
                       in_=a["w_qkv"][ec * 128:(ec + 1) * 128, :]),
                needs=[wstage_free[wsl]], sem=wsem, dma=True,
                produces=[f"at:ws:{ec}"])
            for dc in range(DC):
                bi, pst, gg = pq.acquire()
                p.emit("tensor", lambda e, dc=dc, wsl=wsl, pst=pst,
                       wstage=wstage, C=C: e.transpose(
                           pst[:, 0:64].bitcast(BF16),
                           wstage[wsl][:, dc * 128:(dc + 1) * 128],
                           C["ident_b"][:]),
                       needs=[f"at:ws:{ec}", "identb"] + gg,
                       produces=[f"at:wt:{ec}:{dc}"])
                p.emit("vector", lambda e, dc=dc, ec=ec, pst=pst,
                       wqkvT=wqkvT: e.tensor_copy(
                           wqkvT[:, dc * 3 * D + ec * 128:
                                 dc * 3 * D + (ec + 1) * 128],
                           pst[:, 0:64].bitcast(BF16)),
                       needs=[f"at:wt:{ec}:{dc}"],
                       produces=[f"at:wc:{ec}:{dc}"])
                pq.consumed(bi, f"at:wc:{ec}:{dc}")
            wstage_free[wsl] = f"at:wt:{ec}:{DC-1}"
        wq_done = [f"at:wc:{3*D//128-1}:{dc}" for dc in range(DC)] + \
                  [f"at:wc:{ec}:{DC-1}" for ec in range(3 * D // 128)]

        p.emit("vector", lambda e, nrmq=nrmq: e.memset(nrmq[:], -3.0e38),
               produces=["at:nq0"])
        p.emit("vector", lambda e, nrmk=nrmk: e.memset(nrmk[:], -3.0e38),
               produces=["at:nk0"])

        x1_free = [None, None]
        prfree = [None, None]
        prfree_t = [None, None]
        qofree = [None, None]
        vsfree = [None, None]
        qsq_guard = None
        NTCH = T // 512
        ln_next = {}

        def emit_ln_block(tcbn):
            t0n = tcbn * 512
            tsln = tcbn % 2
            p.emit("sync", lambda e, t0n=t0n, tsln=tsln, x1t=x1t,
                   x_scr=x_scr: e.dma_start(
                       out=x1t[tsln][:].rearrange("p (c t) -> p c t", t=512),
                       in_=x_scr[:, :, t0n:t0n + 512].transpose([1, 0, 2])),
                needs=[x1_free[tsln]], sem=xsem, dma=True,
                produces=[f"at:x1:{tcbn}"])
            p.emit("sync", lambda e, t0n=t0n, tsln=tsln, cs_c=cs_c,
                   cs_scr=cs_scr: e.dma_start(
                       out=cs_c[tsln][:], in_=cs_scr[0, :, t0n:t0n + 512]),
                   needs=[csfree2[tsln]], sem=csem[tsln], dma=True,
                   produces=[f"at:csc:{tcbn}"])
            p.emit("sync", lambda e, t0n=t0n, tsln=tsln, cs_s=cs_s,
                   cs_scr=cs_scr: e.dma_start(
                       out=cs_s[tsln][:], in_=cs_scr[1, :, t0n:t0n + 512]),
                   needs=[csfree2[tsln]], sem=csem_s[tsln], dma=True,
                   produces=[f"at:css:{tcbn}"])
            done_n = emit_ln(
                p, C, f"at:ln:{tcbn}", 512, x1t[tsln],
                lambda dc: slice(dc * 512, (dc + 1) * 512),
                xln2[tsln], lambda dc: slice(dc * 512, (dc + 1) * 512),
                ln_scr, (pm, pn), [f"at:x1:{tcbn}"], False,
                xln_guard=(f"at:v:{tcbn-2}:3:1" if tcbn >= 2 else None))
            x1_free[tsln] = done_n[-1]
            ln_next[tcbn] = done_n

        emit_ln_block(0)
        for tcb in range(NTCH):
            t0 = tcb * 512
            tsl = tcb % 2
            csl2 = tcb % 2
            ln_done = ln_next.pop(tcb)

            def emit_tail(ec, tcb=tcb, t0=t0, csl2=csl2):
                nonlocal qsq_guard
                psl = ec % 2
                wi2, psw, wgg = pw.acquire()
                p.emit("tensor", lambda e, psl=psl, psw=psw, prer=prer,
                       C=C: e.matmul(
                           psw[:, 0:512], C["m2swap"][:], prer[psl][:],
                           start=True, stop=True),
                       needs=[f"at:pr:{tcb}:{ec}", "m2swap"] + wgg,
                       produces=[f"at:sw:{tcb}:{ec}"])
                p.emit("vector", lambda e, psl=psl, csl2=csl2, rt1=rt1,
                       prer=prer, cs_c=cs_c: e.tensor_tensor(
                           out=rt1[:], in0=prer[psl][:],
                           in1=cs_c[csl2][:], op=ALU.mult),
                       needs=[f"at:pr:{tcb}:{ec}", f"at:csc:{tcb}"],
                       produces=[f"at:r1:{tcb}:{ec}"])
                p.emit("vector", lambda e, psw=psw, csl2=csl2, rt2=rt2,
                       cs_s=cs_s: e.tensor_tensor(
                           out=rt2[:], in0=psw[:, 0:512],
                           in1=cs_s[csl2][:], op=ALU.mult),
                       needs=[f"at:sw:{tcb}:{ec}", f"at:css:{tcb}"],
                       produces=[f"at:r2:{tcb}:{ec}"])
                pw.consumed(wi2, f"at:r2:{tcb}:{ec}")
                prfree[psl] = f"at:r1:{tcb}:{ec}"
                prfree_t[psl] = f"at:sw:{tcb}:{ec}"
                if ec < 8:
                    qsl = ec % 2
                    p.emit("vector", lambda e, qsl=qsl, rt1=rt1, rt2=rt2,
                           qout=qout: e.tensor_tensor(
                               out=qout[qsl][:], in0=rt1[:], in1=rt2[:],
                               op=ALU.add),
                           needs=[f"at:r1:{tcb}:{ec}", f"at:r2:{tcb}:{ec}",
                                  qofree[qsl]],
                           produces=[f"at:q:{tcb}:{ec}"])
                    p.emit("sync", lambda e, ec=ec, t0=t0, qsl=qsl, qout=qout,
                           q_scr=q_scr: e.dma_start(
                               out=q_scr[ec, :, t0:t0 + 512],
                               in_=qout[qsl][:]),
                           needs=[f"at:q:{tcb}:{ec}"], sem=qsem[qsl], dma=True,
                           produces=[f"at:qst:{tcb}:{ec}"])
                    qofree[qsl] = f"at:qst:{tcb}:{ec}"
                    rres = f"at:q:{tcb}:{ec}"
                    src_ap = (lambda qsl=qsl, qout=qout: qout[qsl][:])
                else:
                    kc = ec - 8
                    p.emit("vector", lambda e, kc=kc, t0=t0, rt1=rt1, rt2=rt2,
                           k_sb=k_sb, T=T: e.tensor_tensor(
                               out=k_sb[:, kc * T + t0: kc * T + t0 + 512],
                               in0=rt1[:], in1=rt2[:], op=ALU.add),
                           needs=[f"at:r1:{tcb}:{ec}", f"at:r2:{tcb}:{ec}"],
                           produces=[f"at:kk:{tcb}:{kc}"])
                    rres = f"at:kk:{tcb}:{kc}"

            def emit_norm(ec, tcb=tcb, t0=t0):
                nonlocal qsq_guard
                if ec < 8:
                    qsl = ec % 2
                    rres = f"at:q:{tcb}:{ec}"
                    src_ap = (lambda qsl=qsl, qout=qout: qout[qsl][:])
                else:
                    kc = ec - 8
                    rres = f"at:kk:{tcb}:{kc}"
                    src_ap = (lambda kc=kc, t0=t0, k_sb=k_sb, T=T:
                              k_sb[:, kc * T + t0: kc * T + t0 + 512])
                p.emit("vector", lambda e, src_ap=src_ap, qsq=qsq: e.tensor_tensor(
                    out=qsq[:], in0=src_ap(), in1=src_ap(), op=ALU.mult),
                    needs=[rres, qsq_guard], produces=[f"at:sq:{tcb}:{ec}"])
                ni, psn, ngg = pn.acquire()
                p.emit("tensor", lambda e, psn=psn, qsq=qsq, C=C: e.matmul(
                    psn[0:2, 0:512], C["halfsel"][:], qsq[:], start=True,
                    stop=True),
                    needs=[f"at:sq:{tcb}:{ec}", "halfsel"] + ngg,
                    produces=[f"at:nm:{tcb}:{ec}"])
                qsq_guard = f"at:nm:{tcb}:{ec}"
                p.emit("vector", lambda e, psn=psn, nscr=nscr: e.tensor_reduce(
                    out=nscr[:], in_=psn[0:2, 0:512], axis=AXX, op=ALU.max),
                    needs=[f"at:nm:{tcb}:{ec}"], produces=[f"at:nr:{tcb}:{ec}"])
                pn.consumed(ni, f"at:nr:{tcb}:{ec}")
                tgt, col = (nrmq, ec) if ec < 8 else (nrmk, ec - 8)
                p.emit("vector", lambda e, tgt=tgt, col=col,
                       nscr=nscr: e.tensor_tensor(
                           out=tgt[:, col:col + 1], in0=tgt[:, col:col + 1],
                           in1=nscr[:], op=ALU.max),
                       needs=[f"at:nr:{tcb}:{ec}", "at:nq0", "at:nk0"],
                       produces=[f"at:nx:{tcb}:{ec}"])

            for ec in range(16):
                bi, psq, gg = pq.acquire()
                for dc in range(DC):
                    p.emit("tensor", lambda e, ec=ec, dc=dc, tsl=tsl, psq=psq,
                           wqkvT=wqkvT, xln2=xln2: e.matmul(
                               psq[:, 0:512],
                               wqkvT[:, dc * 3 * D + ec * 128:
                                     dc * 3 * D + (ec + 1) * 128],
                               xln2[tsl][:, dc * 512:(dc + 1) * 512],
                               start=(dc == 0), stop=(dc == DC - 1)),
                           needs=(wq_done + ln_done + gg) if dc == 0 else [],
                           produces=[f"at:qk:{tcb}:{ec}"]
                           if dc == DC - 1 else ())
                psl = ec % 2
                p.emit("scalar", lambda e, psl=psl, psq=psq,
                       prer=prer: e.activation(
                           out=prer[psl][:], in_=psq[:, 0:512], func=AF.Copy,
                           bias=0.0, scale=1.0),
                       needs=[f"at:qk:{tcb}:{ec}", prfree[psl], prfree_t[psl]],
                       produces=[f"at:pr:{tcb}:{ec}"])
                pq.consumed(bi, f"at:pr:{tcb}:{ec}")
                if ec > 0:
                    emit_tail(ec - 1)
                if ec > 1:
                    emit_norm(ec - 2)
            emit_tail(15)
            emit_norm(14)
            emit_norm(15)
            if tcb + 1 < NTCH:
                emit_ln_block(tcb + 1)

            csfree2[csl2] = f"at:r2:{tcb}:{15}"
            for tb in range(4):
                vsl = tb % 2
                evdone = []
                for ev in range(2):
                    bi, psv, gg = pq.acquire()
                    for dc in range(DC):
                        p.emit("tensor", lambda e, tb=tb, ev=ev, dc=dc,
                               tsl=tsl, psv=psv, xln2=xln2,
                               wqkvT=wqkvT: e.matmul(
                                   psv[:, 0:512],
                                   xln2[tsl][:, dc * 512 + tb * 128:
                                             dc * 512 + (tb + 1) * 128],
                                   wqkvT[:, dc * 3 * D + 2 * D + ev * 512:
                                         dc * 3 * D + 2 * D + (ev + 1) * 512],
                                   start=(dc == 0), stop=(dc == DC - 1)),
                               needs=(wq_done + ln_done + gg)
                               if dc == 0 else [],
                               produces=[f"at:v:{tcb}:{tb}:{ev}"]
                               if dc == DC - 1 else ())
                    p.emit("vector", lambda e, vsl=vsl, ev=ev, psv=psv,
                           vstage=vstage: e.tensor_copy(
                               vstage[vsl][:, :].rearrange(
                                   "p (h c) -> p h c", c=65)
                               [:, ev * 8:(ev + 1) * 8, 0:64],
                               psv[:, 0:512]),
                           needs=[f"at:v:{tcb}:{tb}:{ev}", vsfree[vsl]],
                           produces=[f"at:ve:{tcb}:{tb}:{ev}"])
                    pq.consumed(bi, f"at:ve:{tcb}:{tb}:{ev}")
                    evdone.append(f"at:ve:{tcb}:{tb}:{ev}")
                p.emit("vector", lambda e, vsl=vsl, vstage=vstage: e.memset(
                    vstage[vsl][:, :].rearrange("p (h c) -> p h c", c=65)
                    [:, :, 64:65], 1.0),
                    needs=evdone, produces=[f"at:vo:{tcb}:{tb}"])
                p.emit("sync", lambda e, tcb=tcb, tb=tb, vsl=vsl,
                       vstage=vstage, v_scr=v_scr: e.dma_start(
                           out=v_scr[tcb * 4 + tb], in_=vstage[vsl][:]),
                       needs=[f"at:vo:{tcb}:{tb}"], sem=vsem[vsl], dma=True,
                       produces=[f"at:vst:{tcb}:{tb}"])
                vsfree[vsl] = f"at:vst:{tcb}:{tb}"

        nall = [f"at:nx:{NTCH-1}:{ec}" for ec in range(16)]
        p.emit("vector", lambda e, bmx=bmx, nrmq=nrmq, nrmk=nrmk: e.tensor_tensor(
            out=bmx[:], in0=nrmq[:], in1=nrmk[:], op=ALU.mult),
            needs=nall, produces=["at:bp"])
        p.emit("scalar", lambda e, bmx=bmx: e.activation(
            out=bmx[:], in_=bmx[:], func=AF.Ln, bias=0.0, scale=1.0),
            needs=["at:bp"], produces=["at:bl"])
        p.emit("scalar", lambda e, bmx=bmx: e.activation(
            out=bmx[:], in_=bmx[:], func=AF.Exp, bias=0.0, scale=0.5),
            needs=["at:bl"], produces=["at:be"])
        p.emit("scalar", lambda e, bmx=bmx: e.activation(
            out=bmx[:], in_=bmx[:], func=AF.Copy, bias=0.0, scale=-ASCALE),
            needs=["at:be"], produces=["at:bn"])
        p.emit("vector", lambda e, bmxb=bmxb, bmx=bmx: e.tensor_copy(
            bmxb[:], bmx[:]), needs=["at:bn"], produces=["at:bnb"])
        bi, psB, gg = pm.acquire()
        for h in range(16):
            p.emit("tensor", lambda e, h=h, psB=psB, C=C, bmxb=bmxb: e.matmul(
                psB[0:1, h:h + 1], C["ident_b"][0:2, (h % 2):(h % 2) + 1],
                bmxb[0:2, (h // 2):(h // 2) + 1], start=True, stop=True),
                needs=(["at:bnb", "identb"] + gg) if h == 0 else [],
                produces=[f"at:b1:{h}"] if h == 15 else ())
        p.emit("vector", lambda e, psB=psB, brow=brow: e.tensor_copy(
            brow[0:1, 0:16], psB[0:1, 0:16]),
            needs=["at:b1:15"], produces=["at:brow"])
        p.emit("vector", lambda e, brb=brb, brow=brow: e.tensor_copy(
            brb[:], brow[0:1, 0:16]), needs=["at:brow"], produces=["at:brb"])
        bi2, psB2, gg2 = pm.acquire()
        p.emit("tensor", lambda e, psB2=psB2, C=C, brb=brb: e.matmul(
            psB2[:, 0:16], C["ones_b"][0:1, :], brb[0:1, 0:16],
            start=True, stop=True),
            needs=["at:brb", "onesb"] + gg2, produces=["at:bbc"])
        p.emit("vector", lambda e, psB2=psB2, bneg=bneg: e.tensor_copy(
            bneg[:], psB2[:, 0:16]), needs=["at:bbc"], produces=["at:bneg"])
        pm.consumed(bi, "at:brow")
        pm.consumed(bi2, "at:bneg")

    p.barrier()

    with ExitStack() as ph:
        psb = lambda n, s, d=F32: ph.enter_context(nc.sbuf_tensor("atB_" + n, s, d))
        v_aug = psb("vaug", [128, NKC * NH * 65], BF16)
        woutT = psb("woutT", [64, NH * DC * 128], BF16)
        qblk = [psb("qblk0", [128, 8 * SBW], BF16)]
        wstage = qblk[0] if 8 * SBW >= D else psb("wstage", [128, D], BF16)
        attnT = [psb(f"attnT{i}", [128, 2 * SBW], BF16) for i in range(3)]
        ofm = psb("ofm", [64, NH * SBW], BF16)
        rbc_sb = [psb("rbc0", [128, SBW])]
        dpad = rbc_sb[0]
        dhi = psb("dhi", [65, SBW], BF16)
        dlo = psb("dlo", [65, SBW], BF16)
        x1p = [psb("x1p0", [128, SBW])]
        x2p = [psb("x2p0", [128, SBW])]

        psS = [ph.enter_context(nc.psum_tensor("atB_scA", [128, 1024], F32)),
               ph.enter_context(nc.psum_tensor("atB_scB", [128, 1024], F32)),
               ph.enter_context(nc.psum_tensor("atB_scC", [128, 1024], F32))]
        psO = [ph.enter_context(nc.psum_tensor("atB_oA", [65, 512], F32)),
               ph.enter_context(nc.psum_tensor("atB_oB", [65, 512], F32))]

        vsem2 = p.addsem("d_bv", dma=True)
        qsem2 = [p.addsem("d_bq0", dma=True), p.addsem("d_bq1", dma=True)]
        wsem2 = p.addsem("d_bw", dma=True)
        xsem2 = [p.addsem("d_bx0", dma=True), p.addsem("d_bx1", dma=True)]
        osem2 = [p.addsem("d_bo0", dma=True), p.addsem("d_bo1", dma=True)]

        sc_bank = Bank(psS)
        p.emit("sync", lambda e, v_aug=v_aug, v_scr=v_scr: e.dma_start(
            out=v_aug[:].rearrange("p (b c) -> p b c", c=NH * 65),
            in_=v_scr.transpose([1, 0, 2])),
            sem=vsem2, dma=True, produces=["bt:vaug"])

        wst_free = None
        for dc in range(DC):
            p.emit("gpsimd", lambda e, dc=dc, wstage=wstage, a=a: e.dma_start(
                out=wstage[:, 0:D], in_=a["w_out"][dc * 128:(dc + 1) * 128, :]),
                needs=[wst_free], sem=wsem2, dma=True,
                produces=[f"bt:ws:{dc}"])
            for h in range(NH):
                bi, psw, gg = sc_bank.acquire()
                p.emit("tensor", lambda e, h=h, psw=psw, wstage=wstage,
                       C=C: e.transpose(
                           psw[0:64, 0:64].bitcast(BF16),
                           wstage[:, h * 64:(h + 1) * 64], C["ident_b"][:]),
                       needs=[f"bt:ws:{dc}", "identb"] + gg,
                       produces=[f"bt:wt:{h}:{dc}"])
                p.emit("vector", lambda e, h=h, dc=dc, psw=psw,
                       woutT=woutT: e.tensor_copy(
                           woutT[:, (h * DC + dc) * 128:
                                 (h * DC + dc + 1) * 128],
                           psw[0:64, 0:64].bitcast(BF16)),
                       needs=[f"bt:wt:{h}:{dc}"], produces=[f"bt:wc:{h}:{dc}"])
                sc_bank.consumed(bi, f"bt:wc:{h}:{dc}")
            wst_free = f"bt:wt:{NH-1}:{dc}"
        wout_done = [f"bt:wc:{h}:{DC-1}" for h in range(NH)] + \
                    [f"bt:wc:{NH-1}:{dc}" for dc in range(DC)]

        batches = []
        kk = 0
        while kk < NKC:
            n = min(2, NKC - kk)
            batches.append((kk, n))
            kk += n

        o_bank = Bank(psO)
        at_bank = Bank(attnT)
        rbc_bank = Bank(rbc_sb)
        qb_free = [f"bt:wt:{NH-1}:{DC-1}", None]
        x1_free2 = [None, None]
        x2_free = [None, None]
        ofree = {}
        hold = {"dfree": None}
        pending_norm = None

        for sbi in range(NSB):
            s0 = sbi * SBW
            qsl = 0
            p.emit("sync", lambda e, s0=s0, qsl=qsl, qblk=qblk, q_scr=q_scr,
                   SBW=SBW: e.dma_start(
                       out=qblk[qsl][:].rearrange("p (c t) -> p c t", t=SBW),
                       in_=q_scr[:, :, s0:s0 + SBW].transpose([1, 0, 2])),
                   needs=[qb_free[qsl]], sem=qsem2[qsl], dma=True,
                   produces=[f"bt:qb:{sbi}"])
            head_done = []
            for hp in range(8):
                heads = (2 * hp, 2 * hp + 1)
                osl = {}

                def emit_vms(pend, side, sbi=sbi):
                    k0, nb, atb, ai = pend
                    oi, pO, ogg = osl[side]
                    h = heads[side]
                    for j in range(nb):
                        tch = k0 + j
                        lastm = (k0 + nb == NKC and j == nb - 1)
                        p.emit("tensor", lambda e, tch=tch, h=h, j=j, pO=pO,
                               atb=atb, v_aug=v_aug,
                               fi=(k0 == 0 and j == 0), lastm=lastm,
                               SBW=SBW: e.matmul(
                                   pO[:, 0:SBW],
                                   v_aug[:, (tch * NH + h) * 65:
                                         (tch * NH + h) * 65 + 65],
                                   atb[:, j * SBW:(j + 1) * SBW],
                                   start=fi, stop=lastm),
                               needs=([f"bt:ex:{sbi}:{h}:{k0}", "bt:vaug"] +
                                      (ogg if k0 == 0 else []))
                               if j == 0 else [],
                               produces=[f"bt:vm:{sbi}:{h}" if lastm else
                                         f"bt:vmb:{sbi}:{h}:{k0}"]
                               if j == nb - 1 else ())
                    at_bank.consumed(
                        ai, f"bt:vm:{sbi}:{h}" if (k0 + nb == NKC)
                        else f"bt:vmb:{sbi}:{h}:{k0}")

                pending = {0: None, 1: None}
                for k_idx, (k0, nb) in enumerate(batches):
                    psinfo = {0: sc_bank.acquire(), 1: sc_bank.acquire()}
                    for j in range(nb):
                        for side in (0, 1):
                            h = heads[side]
                            kc = h // 2
                            roff = (h % 2) * 64
                            si, pS, sgg = psinfo[side]
                            p.emit("tensor", lambda e, kc=kc, roff=roff,
                                   tch=k0 + j, j=j, pS=pS, qsl=qsl, qblk=qblk,
                                   k_sb=k_sb, T=T, SBW=SBW: e.matmul(
                                       pS[:, j * SBW:(j + 1) * SBW],
                                       k_sb[roff:roff + 64,
                                            kc * T + tch * 128:
                                            kc * T + (tch + 1) * 128],
                                       qblk[qsl][roff:roff + 64,
                                                 kc * SBW:(kc + 1) * SBW],
                                       start=True, stop=True,
                                       tile_position=(roff, 0)),
                                   needs=([f"bt:qb:{sbi}"] + sgg)
                                   if j == 0 else [],
                                   produces=[f"bt:sc:{sbi}:{h}:{k0}:{j}"])
                    for side in (0, 1):
                        h = heads[side]
                        si, pS, _sg = psinfo[side]
                        ai, atb, agg = at_bank.acquire()
                        p.emit("scalar", lambda e, nb=nb, pS=pS, atb=atb, h=h,
                               bneg=bneg, SBW=SBW: e.activation(
                                   out=atb[:, 0:nb * SBW],
                                   in_=pS[:, 0:nb * SBW],
                                   func=AF.Exp, bias=bneg[:, h:h + 1],
                                   scale=ASCALE),
                               needs=[f"bt:sc:{sbi}:{h}:{k0}:{j}"
                                      for j in range(nb)] + ["at:bneg"] + agg,
                               produces=[f"bt:ex:{sbi}:{h}:{k0}"])
                        sc_bank.consumed(si, f"bt:ex:{sbi}:{h}:{k0}")
                        if pending[side] is not None:
                            emit_vms(pending[side], side)
                        pending[side] = (k0, nb, atb, ai)
                    if k_idx == 0:
                        if pending_norm is not None:
                            pending_norm()
                            pending_norm = None
                        for side in (0, 1):
                            osl[side] = o_bank.acquire()
                for side in (0, 1):
                    emit_vms(pending[side], side)

                def make_norm(heads=heads, osl=osl, sbi=sbi):
                    def norm():
                        for side in (0, 1):
                            h = heads[side]
                            oi, pO, ogg = osl[side]
                            p.emit("vector", lambda e, pO=pO, dpad=dpad,
                                   SBW=SBW: e.reciprocal(
                                       dpad[64:65, 0:SBW], pO[64:65, 0:SBW]),
                                   needs=[f"bt:vm:{sbi}:{h}", hold["dfree"]],
                                   produces=[f"bt:rcp:{sbi}:{h}"])
                            p.emit("vector", lambda e, dpad=dpad, dhi=dhi,
                                   SBW=SBW: e.tensor_copy(
                                       dhi[64:65, 0:SBW], dpad[64:65, 0:SBW]),
                                   needs=[f"bt:rcp:{sbi}:{h}"],
                                   produces=[f"bt:rhi:{sbi}:{h}"])
                            p.emit("vector", lambda e, dpad=dpad, dhi=dhi,
                                   dlo=dlo, SBW=SBW: e.tensor_tensor(
                                       out=dlo[64:65, 0:SBW],
                                       in0=dpad[64:65, 0:SBW],
                                       in1=dhi[64:65, 0:SBW],
                                       op=ALU.subtract),
                                   needs=[f"bt:rhi:{sbi}:{h}"],
                                   produces=[f"bt:rlo:{sbi}:{h}"])
                            ci, pC, cgg = sc_bank.acquire()
                            p.emit("tensor", lambda e, pC=pC, dhi=dhi, C=C,
                                   SBW=SBW: e.matmul(
                                       pC[:, 0:SBW], C["ones_b"][64:65, :],
                                       dhi[64:65, 0:SBW], start=True,
                                       stop=False, tile_position=(64, 0)),
                                   needs=[f"bt:rhi:{sbi}:{h}", "onesb"] + cgg,
                                   produces=())
                            p.emit("tensor", lambda e, pC=pC, dlo=dlo, C=C,
                                   SBW=SBW: e.matmul(
                                       pC[:, 0:SBW], C["ones_b"][64:65, :],
                                       dlo[64:65, 0:SBW], start=False,
                                       stop=True, tile_position=(64, 0)),
                                   needs=[f"bt:rlo:{sbi}:{h}"],
                                   produces=[f"bt:rbm:{sbi}:{h}"])
                            hold["dfree"] = f"bt:rbm:{sbi}:{h}"
                            ri, rb, rgg = rbc_bank.acquire()
                            p.emit("vector", lambda e, rb=rb, pC=pC,
                                   SBW=SBW: e.tensor_copy(
                                       rb[:, 0:SBW], pC[:, 0:SBW]),
                                   needs=[f"bt:rbm:{sbi}:{h}"] + rgg,
                                   produces=[f"bt:rbc:{sbi}:{h}"])
                            sc_bank.consumed(ci, f"bt:rbc:{sbi}:{h}")
                            p.emit("vector", lambda e, h=h, rb=rb, pO=pO,
                                   ofm=ofm, SBW=SBW: e.tensor_tensor(
                                       out=ofm[:, h * SBW:(h + 1) * SBW],
                                       in0=pO[0:64, 0:SBW],
                                       in1=rb[0:64, 0:SBW], op=ALU.mult),
                                   needs=[f"bt:rbc:{sbi}:{h}", ofree.get(h)],
                                   produces=[f"bt:o:{sbi}:{h}"])
                            rbc_bank.consumed(ri, f"bt:o:{sbi}:{h}")
                            o_bank.consumed(oi, f"bt:o:{sbi}:{h}")
                            head_done.append(f"bt:o:{sbi}:{h}")
                    return norm

                pending_norm = make_norm()
            if pending_norm is not None:
                pending_norm()
                pending_norm = None
            qb_free[qsl] = f"bt:sc:{sbi}:{NH-1}:{batches[-1][0]}:{batches[-1][1]-1}"

            for dc in range(DC):
                pi, pP, pgg = sc_bank.acquire()
                for h in range(NH):
                    p.emit("tensor", lambda e, h=h, dc=dc, pP=pP, woutT=woutT,
                           ofm=ofm, SBW=SBW: e.matmul(
                               pP[:, 0:SBW],
                               woutT[:, (h * DC + dc) * 128:
                                     (h * DC + dc + 1) * 128],
                               ofm[:, h * SBW:(h + 1) * SBW],
                               start=(h == 0), stop=(h == NH - 1)),
                           needs=(head_done + wout_done + pgg)
                           if h == 0 else [],
                           produces=[f"bt:pj:{sbi}:{dc}"]
                           if h == NH - 1 else ())
                x1s = 0
                p.emit("sync", lambda e, dc=dc, s0=s0, x1s=x1s, x1p=x1p,
                       x_scr=x_scr, SBW=SBW: e.dma_start(
                           out=x1p[x1s][:, 0:SBW],
                           in_=x_scr[dc, :, s0:s0 + SBW]),
                       needs=[x1_free2[x1s]], sem=xsem2[x1s], dma=True,
                       produces=[f"bt:x1:{sbi}:{dc}"])
                x2s = 0
                p.emit("vector", lambda e, dc=dc, pP=pP, x1s=x1s, x2s=x2s,
                       x1p=x1p, x2p=x2p, C=C, SBW=SBW: e.scalar_tensor_tensor(
                           out=x2p[x2s][:, 0:SBW], in0=pP[:, 0:SBW],
                           scalar=C["boutt"][:, dc:dc + 1],
                           in1=x1p[x1s][:, 0:SBW], op0=ALU.add, op1=ALU.add),
                       needs=[f"bt:pj:{sbi}:{dc}", "boutt",
                              f"bt:x1:{sbi}:{dc}", x2_free[x2s]],
                       produces=[f"bt:x2:{sbi}:{dc}"])
                sc_bank.consumed(pi, f"bt:x2:{sbi}:{dc}")
                x1_free2[x1s] = f"bt:x2:{sbi}:{dc}"
                p.emit("sync", lambda e, dc=dc, s0=s0, x2s=x2s, x2p=x2p,
                       x_scr=x_scr, SBW=SBW: e.dma_start(
                           out=x_scr[dc, :, s0:s0 + SBW],
                           in_=x2p[x2s][:, 0:SBW]),
                       needs=[f"bt:x2:{sbi}:{dc}"], sem=osem2[x2s], dma=True,
                       produces=[f"bt:xst:{sbi}:{dc}"])
                x2_free[x2s] = f"bt:xst:{sbi}:{dc}"
            for h in range(NH):
                ofree[h] = f"bt:pj:{sbi}:{DC-1}"


def kernel(L0_attn_x, L0_attn_w_qkv, L0_attn_w_out, L0_attn_b_out,
           L0_ffn_Wr, L0_ffn_br, L0_ffn_W1, L0_ffn_b1, L0_ffn_W2, L0_ffn_b2):
    x = np.ascontiguousarray(np.asarray(L0_attn_x, dtype=np.float32))
    B, S, _ = x.shape
    E = int(np.asarray(L0_ffn_Wr).shape[1])
    nc = build_program(T=S, E=E)
    base = dict(
        w_qkv=np.ascontiguousarray(np.asarray(L0_attn_w_qkv, dtype=np.float32)),
        w_out=np.ascontiguousarray(np.asarray(L0_attn_w_out, dtype=np.float32)),
        b_out=np.ascontiguousarray(np.asarray(L0_attn_b_out, dtype=np.float32)),
        Wr=np.ascontiguousarray(np.asarray(L0_ffn_Wr, dtype=np.float32)),
        br=np.ascontiguousarray(np.asarray(L0_ffn_br, dtype=np.float32)),
        W1=np.ascontiguousarray(np.asarray(L0_ffn_W1, dtype=np.float32)),
        b1=np.ascontiguousarray(np.asarray(L0_ffn_b1, dtype=np.float32)),
        W2=np.ascontiguousarray(np.asarray(L0_ffn_W2, dtype=np.float32)),
        b2=np.ascontiguousarray(np.asarray(L0_ffn_b2, dtype=np.float32)),
    )
    in_maps = [dict(base, x=np.ascontiguousarray(x[c])) for c in range(B)]
    res = run_bass_kernel_spmd(nc, in_maps, core_ids=list(range(B)))
    return np.stack([res.results[c]["out"] for c in range(B)], axis=0)
